# version 22
# baseline (speedup 1.0000x reference)
import numpy as np
from contextlib import ExitStack

import concourse.bass as bass
import concourse.mybir as mybir
from concourse.bass_utils import run_bass_kernel_spmd

F32 = mybir.dt.float32
BF16 = mybir.dt.bfloat16
AX = mybir.AxisListType.X
AF = mybir.ActivationFunctionType
ALU = mybir.AluOpType

D = 1024
NCH = 8
H = 16
MF = 266
MCS = [(0, 128), (128, 256), (256, 266)]
EPS = 1e-4
LNEPS = 1e-5
DN = float(64) ** -0.25
E = 8

ENGS = ("sync", "tensor", "vector", "scalar", "gpsimd")


class Sched:
    def __init__(self):
        self.q = {e: [] for e in ENGS}
        self.n = {}
        self.lastwait = {}

    def c(self, sem):
        return self.n.get(sem, 0)

    def bump(self, sem, n):
        self.n[sem] = self.c(sem) + n

    DMASEMS = ("xd", "wd0", "wd1", "wg0", "wg1", "od", "gd", "cc")

    def do(self, eng, fn, waits=(), inc=None, incn=1, drain=False):
        waits = list(waits)
        if inc in self.DMASEMS:
            waits.append((inc, self.c(inc)))
        w = []
        for s, v in waits:
            if v <= 0:
                continue
            if s in self.DMASEMS:
                v = self.c(s)
            assert v <= self.c(s), f"forward wait {s} {v} > {self.c(s)}"
            key = (eng, s)
            if self.lastwait.get(key, -1) < v:
                w.append((s, v))
                self.lastwait[key] = v
        self.q[eng].append((fn, tuple(w), inc, incn, drain))
        if inc is not None:
            self.n[inc] = self.c(inc) + incn
        return self.c(inc) if inc is not None else None


def build(T, groups, part="full", stop_after=None):
    nT = T // 128
    nB = T // 512
    nc = bass.Bass("TRN2", target_bir_lowering=False,
                   detect_race_conditions=(part != "fused"))

    x_in = nc.dram_tensor("x_in", [T, D], F32, kind="ExternalInput").ap()
    w_q = nc.dram_tensor("w_q", [D, D], F32, kind="ExternalInput").ap()
    w_k = nc.dram_tensor("w_k", [D, D], F32, kind="ExternalInput").ap()
    w_v = nc.dram_tensor("w_v", [D, D], F32, kind="ExternalInput").ap()
    w_o = nc.dram_tensor("w_o", [D, D], F32, kind="ExternalInput").ap()
    b_o = nc.dram_tensor("b_o", [D], F32, kind="ExternalInput").ap()
    prj = nc.dram_tensor("prj", [MF, 64], F32, kind="ExternalInput").ap()
    w_g = nc.dram_tensor("w_g", [D, E], F32, kind="ExternalInput").ap()
    b_g = nc.dram_tensor("b_g", [E], F32, kind="ExternalInput").ap()
    w_1 = nc.dram_tensor("w_1", [E, D, D], F32, kind="ExternalInput").ap()
    b_1 = nc.dram_tensor("b_1", [E, D], F32, kind="ExternalInput").ap()
    w_2 = nc.dram_tensor("w_2", [E, D, D], F32, kind="ExternalInput").ap()
    b_2 = nc.dram_tensor("b_2", [E, D], F32, kind="ExternalInput").ap()
    out = None
    if part != "a":
        out = nc.dram_tensor("out", [T, D], F32, kind="ExternalOutput").ap()
    c_out = s_out = c_in = s_in = None
    if part == "a":
        c_out = nc.dram_tensor("c_out", [65, H * 267], F32, kind="ExternalOutput").ap()
        s_out = nc.dram_tensor("s_out", [16, 1], F32, kind="ExternalOutput").ap()
    G = len(groups[0])
    PAIRS = max(1, len(groups[0]) // 2)
    NSL = 2 * PAIRS
    if part == "b":
        c_in = nc.dram_tensor("c_in", [G, 65, H * 267], F32, kind="ExternalInput").ap()
        s_in = nc.dram_tensor("s_in", [1, G * 16], F32, kind="ExternalInput").ap()
    x_far = slot_in = d_cslab = d_stab = d_flag = None
    if part == "fused":
        if PAIRS == 2:
            x_far = nc.dram_tensor("x_far", [T, D], F32, kind="ExternalInput").ap()
        slot_in = nc.dram_tensor("slot_in", [1, 16], mybir.dt.int32, kind="ExternalInput").ap()
        d_cslab = nc.dram_tensor("d_cslab", [NSL, 65, H * 267], F32, kind="Internal", addr_space="Shared").ap()
        d_stab = nc.dram_tensor("d_stab", [NSL, 16], F32, kind="Internal", addr_space="Shared").ap()
        d_flag = nc.dram_tensor("d_flag", [2, 16], mybir.dt.int32, kind="Internal", addr_space="Shared").ap()

    dr_diag = nc.dram_tensor("dr_diag", [H, T], F32, kind="Internal").ap()
    dr_rmax = nc.dram_tensor("dr_rmax", [H, T], F32, kind="Internal").ap()
    dr_ci = nc.dram_tensor("dr_ci", [65, H * 267], F32, kind="Internal").ap()
    dr_co = nc.dram_tensor("dr_co", [65, H * 267], F32, kind="Internal").ap()
    dr_si = nc.dram_tensor("dr_si", [1, H], F32, kind="Internal").ap()
    dr_so = nc.dram_tensor("dr_so", [1, H], F32, kind="Internal").ap()

    st = ExitStack()
    A = st.enter_context

    x_tm = A(nc.sbuf_tensor("x_tm", [128, nT * 1024], BF16))
    B_fm = A(nc.sbuf_tensor("B_fm", [128, NCH * T], BF16))
    C_big = A(nc.sbuf_tensor("C_big", [128, max(NCH * T + nT * 16 * 65, 32768)], BF16))
    D_du = A(nc.sbuf_tensor("D_du", [128, max(nT * 1024, 16384)], BF16))
    scr = A(nc.sbuf_tensor("scr", [128, 7168], F32))
    stag_w = A(nc.sbuf_tensor("stag_w", [128, 2048], F32))

    ident_b = A(nc.sbuf_tensor("ident_b", [128, 128], BF16))
    ident_f = A(nc.sbuf_tensor("ident_f", [128, 128], F32))
    ones_b = A(nc.sbuf_tensor("ones_b", [128, 128], BF16))
    onesf = A(nc.sbuf_tensor("onesf", [1, 128], F32))
    epsf = A(nc.sbuf_tensor("epsf", [1, 128], F32))
    projdup = A(nc.sbuf_tensor("projdup", [128, MF], BF16))
    projex = A(nc.sbuf_tensor("projex", [65, MF], BF16))
    projcb = A(nc.sbuf_tensor("projcb", [128, 64], BF16))
    diagTk = A(nc.sbuf_tensor("diagTk", [128, H * 16], F32))
    estab = A(nc.sbuf_tensor("estab", [128, H], F32))
    kp_sc = A(nc.sbuf_tensor("kp_sc", [128, 4 * 267], BF16))
    epsv = A(nc.sbuf_tensor("epsv", [128, 65], F32))
    vsn = A(nc.sbuf_tensor("vsn", [1, 65], F32))
    stabloc = A(nc.sbuf_tensor("stabloc", [16, 1], F32))
    stabrow = A(nc.sbuf_tensor("stabrow", [1, H], F32))
    stab4 = A(nc.sbuf_tensor("stab4", [1, 64], F32))
    slotv = A(nc.sbuf_tensor("slotv", [1, 16], mybir.dt.int32))
    magict = A(nc.sbuf_tensor("magict", [1, 16], mybir.dt.int32))
    flgt = A(nc.sbuf_tensor("flgt", [1, 32], mybir.dt.int32))
    maxk = A(nc.sbuf_tensor("maxk", [128, H], F32))
    rmaxsb = A(nc.sbuf_tensor("rmaxsb", [128, 32], F32))
    lnin = A(nc.sbuf_tensor("lnin", [128, 1], F32))
    lnout16 = A(nc.sbuf_tensor("lnout16", [128, 16], F32))
    bn6 = A(nc.sbuf_tensor("bn6", [128, 12], F32))
    agg2 = A(nc.sbuf_tensor("agg2", [128, 2], F32))
    rstd = A(nc.sbuf_tensor("rstd", [128, 1], F32))
    nbv = A(nc.sbuf_tensor("nbv", [128, 1], F32))
    lneps = A(nc.sbuf_tensor("lneps", [128, 1], F32))
    epsc = A(nc.sbuf_tensor("epsc", [128, 1], F32))
    b1T = A(nc.sbuf_tensor("b1T", [128, 64], F32))
    wgb = A(nc.sbuf_tensor("wgb", [128, 64], BF16))
    bgb = A(nc.sbuf_tensor("bgb", [1, 8], BF16))
    sel8 = A(nc.sbuf_tensor("sel8", [8, 8 * 128], BF16))
    oh_f = A(nc.sbuf_tensor("oh_f", [128, 8], F32))
    ohmax = A(nc.sbuf_tensor("ohmax", [128, 1], F32))

    pb = [A(nc.psum_tensor(f"pb{i}", [128, 512], F32)) for i in range(8)]
    pbb = [p[:, :].bitcast(BF16) for p in pb]

    sems = {}
    for s in ("xd", "wd0", "wd1", "wg0", "wg1", "od", "gd", "cc", "pe", "ac", "dv", "gp", "pd"):
        sems[s] = A(nc.semaphore(s))

    nc.const_aps.register(F32, EPS, epsc[:, 0:1])

    S = Sched()
    MAGIC_BITS = 0x5CA1AB1E
    rslot_sync = rslot_gp = None
    if part == "fused":
        rslot_sync = nc.sync.alloc_register("rslot_s")
        rslot_gp = nc.gpsimd.alloc_register("rslot_g")
        S.do("sync", lambda e: e.dma_start(out=slotv[:], in_=slot_in), waits=[], inc="xd", incn=16)
        S.do("sync", (lambda: (lambda e: e.reg_load(rslot_sync, slotv[0:1, 0:1])))(),
             waits=[("xd", S.c("xd"))], inc=None)
        S.do("gpsimd", (lambda: (lambda e: e.reg_load(rslot_gp, slotv[0:1, 0:1])))(),
             waits=[("xd", S.c("xd"))], inc="gp")

    SQF = [scr[:, 0:512], scr[:, 512:1024]]
    SQB = [scr[:, 0:256].bitcast(BF16),
           scr[:, 512:768].bitcast(BF16)]
    DROW = [scr[0:1, 1024:1024 + T], scr[64:65, 1024:1024 + T]]
    B16RT = scr[0:16, 1024:3072]
    RM64S = [scr[64:65, 1024:1024 + T], scr[64:65, 3072:3072 + T]]
    NSB = [scr[0:64, 1024:1536], scr[0:64, 1536:2048], scr[0:64, 2048:2560], scr[0:64, 2560:3072]]
    PROJSTG = scr[0:128, 1024:1088]
    cstg = scr[0:65, 3072:3606]
    cstg4 = scr[0:65, 1024:1024 + 4 * 267]
    cstg4s = [scr[0:65, 1024:1024 + 4 * 267], scr[0:65, 3606:3606 + 4 * 267]]
    ZF = [scr[:, 0:1024], scr[:, 1024:2048]]
    OUTS = [scr[:, 2048:3072], scr[:, 3072:4096]]
    BOBC = scr[:, 4096:5120]
    DINVB = [scr[64:65, 6144 + i * 256:6144 + (i + 1) * 256].bitcast(BF16) for i in (0, 1)]
    NUMSC = [scr[0:64, 4608 + i * 256:4608 + (i + 1) * 256].bitcast(BF16) for i in (0, 1)]
    BOSB = scr[0:1, 5120:6144]
    LNR = [scr[64:65, 5120 + i * 512:5120 + (i + 1) * 512] for i in (0, 1)]
    B2B = scr[0:8, 5120:5632].bitcast(BF16)
    BSTG = scr[0:8, 6144:7168]
    hg_b = scr[:, 6656:6912].bitcast(BF16)
    ohT = scr[0:8, 2048:2048 + T // 2].bitcast(BF16)
    WGSTG = scr[0:128, 6144:6208]
    BGSTG = scr[0:1, 6208:6216]


    k_fm = C_big[:, 0:NCH * T]
    k_fm3 = C_big[:, 0:NCH * T].rearrange("p (c t) -> p c t", c=NCH)
    VOFF = NCH * T
    q_ex = C_big[0:65, 0:16 * T]
    HM = [C_big[:, 24576:28672], C_big[:, 28672:32768]]
    def ring_c(s):
        return C_big[:, s * 8192:(s + 1) * 8192]

    attn_ring = [D_du[:, 0:8192], D_du[:, 8192:16384]]
    QP = D_du[:, 0:3072]
    CAUG = D_du[:, 8192:11392]
    y_acc = D_du
    B3 = B_fm.rearrange("p (c t) -> p c t", c=NCH)

    def init_g(g):
        g.memset(ident_b[:], 0.0)
        g.memset(ident_f[:], 0.0)
        g.memset(ones_b[:], 1.0)
        g.memset(onesf[:], 1.0)
        g.memset(epsf[:], EPS)
        g.memset(lneps[:], LNEPS)
        g.memset(epsc[:], EPS)
        g.memset(maxk[:], -3e38)
        g.memset(sel8[:], 0.0)
        g.memset(projex[64:65, :], 1.0)
        g.memset(kp_sc[:].rearrange("p (s c) -> p s c", c=267)[:, :, 266:267], 1.0)
        g.memset(C_big[:, VOFF:VOFF + nT * 16 * 65].rearrange("p (a c) -> p a c", c=65)[:, :, 64:65], 1.0)
        g.drain()
        g.affine_select(out=ident_b[:], in_=ident_b[:], compare_op=ALU.not_equal,
                        fill=1.0, base=0, pattern=[[-1, 128]], channel_multiplier=1)
        g.affine_select(out=ident_f[:], in_=ident_f[:], compare_op=ALU.not_equal,
                        fill=1.0, base=0, pattern=[[-1, 128]], channel_multiplier=1)
        ins = g.affine_select(out=sel8[:].rearrange("p (e n) -> p e n", e=8),
                              in_=sel8[:].rearrange("p (e n) -> p e n", e=8),
                              compare_op=ALU.not_equal, fill=1.0, base=0,
                              pattern=[[-1, 8], [0, 128]], channel_multiplier=1)
        g.drain()
        return ins
    S.do("gpsimd", init_g, inc="gp")
    GP_INIT = S.c("gp")


    def _probe_finish():
        S.do("sync", lambda e: e.dma_start(out=s_out, in_=stabloc[:]),
             waits=[("gp", GP_INIT)], inc="od", incn=16)
        S.do("sync", lambda e: e.wait_ge(sems["od"], S.c("od")), waits=[])
        with nc.Block() as blk:
            for engname in ENGS:
                steps = S.q[engname]
                def body(eng, steps=steps):
                    for fn, waits, inc, incn, drain in steps:
                        for sname, v in waits:
                            eng.wait_ge(sems[sname], v)
                        if drain:
                            eng.drain()
                        ins = fn(eng)
                        if inc is not None:
                            ins.then_inc(sems[inc], incn)
                getattr(blk, engname)(body)
        st.close()
        return nc

    if stop_after == "init":
        return _probe_finish()

    for pi, (r0, r1) in enumerate([(0, 128), (128, 256), (256, 266)]):
        rows = r1 - r0
        S.do("sync", (lambda r0=r0, rows=rows: (lambda e: e.dma_start(
            out=PROJSTG[0:rows, :], in_=prj[r0:r0 + rows, :])))(),
            waits=[("gp", S.c("gp"))], inc="xd", incn=16)
        S.do("gpsimd", (lambda rows=rows: (lambda e: e.tensor_copy(projcb[0:rows, :], PROJSTG[0:rows, :])))(),
             waits=[("xd", S.c("xd")), ("pe", S.c("pe"))], inc="gp", drain=True)
        S.do("tensor", (lambda rows=rows: (lambda e: e.transpose(
            pbb[6][0:64, 0:rows], projcb[0:rows, :], ident_b[0:rows, 0:rows])))(),
            waits=[("gp", S.c("gp")), ("dv", S.c("dv"))], inc="pe")
        S.do("vector", (lambda r0=r0, rows=rows: (lambda e: e.tensor_copy(
            projdup[0:64, r0:r0 + rows], pbb[6][0:64, 0:rows])))(),
            waits=[("pe", S.c("pe"))], inc="dv")
    S.do("vector", lambda e: e.tensor_copy(projex[0:64, :], projdup[0:64, :]),
         waits=[("dv", S.c("dv"))], inc="dv", drain=True)
    S.do("gpsimd", lambda e: e.dma_start(out=projdup[64:128, :], in_=projdup[0:64, :]),
         waits=[("dv", S.c("dv"))], inc="gd", incn=16)
    GD_PROJ = S.c("gd")
    DV_PROJ = S.c("dv")

    def emit_xload(x_src, extra_pe=0, base_ac=None):
        xcast = {}
        xtp_dv = {}
        for tau in range(nT):
            if tau % 2 == 0:
                w = [("ac", xcast[tau - 3])] if tau >= 4 else [("ac", S.c("ac") if base_ac is None else base_ac)]
                S.do("sync", (lambda tau=tau, x_src=x_src: (lambda e: e.dma_start(
                    out=scr[:, 2048 + (tau % 4) * 1024:2048 + (tau % 4) * 1024 + 2048]
                    .rearrange("p (a d) -> p a d", a=2),
                    in_=x_src.rearrange("(q p) d -> p q d", p=128)[:, tau:tau + 2, :])))(),
                    waits=w, inc="xd", incn=16)
            wc = [("xd", S.c("xd"))]
            if tau == 0 and extra_pe:
                wc.append(("pe", extra_pe))
            xcast[tau] = S.do("scalar", (lambda tau=tau: (lambda e: e.copy(
                x_tm[:, tau * 1024:(tau + 1) * 1024],
                scr[:, 2048 + (tau % 4) * 1024:2048 + (tau % 4) * 1024 + 1024])))(),
                waits=wc, inc="ac")
            bank = pbb[6 + tau % 2]
            wtp = [("ac", xcast[tau]), ("gp", GP_INIT), ("dv", DV_PROJ)]
            if tau < 2 and extra_pe:
                wtp.append(("pe", extra_pe))
                wtp.append(("dv", S.c("dv")))
            if tau >= 2:
                wtp.append(("dv", xtp_dv[tau - 2]))
            for c in range(NCH):
                S.do("tensor", (lambda tau=tau, c=c, bank=bank: (lambda e: e.transpose(
                    bank[:, c * 128:(c + 1) * 128],
                    x_tm[:, tau * 1024 + c * 128:tau * 1024 + (c + 1) * 128], ident_b[:])))(),
                    waits=wtp if c == 0 else [], inc="pe" if c == NCH - 1 else None)
            xtp_dv[tau] = S.do("vector", (lambda tau=tau, bank=bank: (lambda e: e.tensor_copy(
                B3[:, :, tau * 128:(tau + 1) * 128],
                bank[:, 0:1024].rearrange("p (c n) -> p c n", c=NCH))))(),
                waits=[("pe", S.c("pe"))], inc="dv")
        return S.c("dv")

    XFM_DV = 0
    if part == "b":
        XFM_DV = emit_xload(x_in)
    if stop_after == "xfm":
        XFM_DV = emit_xload(x_in)
        return _probe_finish()

    castc = [("gp", 0), ("gp", 0)]
    def load_weight(hbm_ap, dest, free_pe, queue="sync", cast="gpsimd", pieces=None):
        src = hbm_ap.rearrange("(c p) d -> p c d", p=128)
        csem = "gp" if cast == "gpsimd" else "ac"
        done = None
        for idx, piece in enumerate(range(8) if pieces is None else pieces):
            slot = idx % 2
            wsem = "wd0" if slot == 0 else "wd1"
            w = [castc[slot]]
            if idx < 2 and free_pe:
                w.append(("pe", free_pe))
            S.do(queue, (lambda piece=piece, slot=slot, src=src: (lambda e: e.dma_start(
                out=stag_w[:, slot * 1024:(slot + 1) * 1024],
                in_=src[:, piece, :])))(),
                waits=w, inc=wsem, incn=16)
            wcast = [(wsem, S.c(wsem))]
            if idx < 2 and free_pe:
                wcast.append(("pe", free_pe))
            if cast == "gpsimd":
                cnt = S.do("gpsimd", (lambda piece=piece, slot=slot, dest=dest: (lambda e: e.tensor_copy(
                    dest[:, piece * 1024:(piece + 1) * 1024], stag_w[:, slot * 1024:(slot + 1) * 1024])))(),
                    waits=wcast, inc="gp")
            else:
                cnt = S.do("scalar", (lambda piece=piece, slot=slot, dest=dest: (lambda e: e.copy(
                    dest[:, piece * 1024:(piece + 1) * 1024], stag_w[:, slot * 1024:(slot + 1) * 1024])))(),
                    waits=wcast, inc="ac", drain=True)
            done = (csem, cnt)
            castc[slot] = done
        return done

    K_PROJ_PE = 0
    DIAGT_AC = 0
    GD_B16 = 0
    gp_wk = gp_wv = ("gp", 0)
    if part in ("a", "full"):
        kpasses = ["own"]
    elif part == "fused":
        kpasses = (["far"] if PAIRS == 2 else []) + ["own"]
    else:
        kpasses = []
    for kpass in kpasses:
      ctx_dv = {}
      cdma = {}
      if kpass == kpasses[0]:
          AC0 = S.c("ac")
          gp_wk = load_weight(w_k, attn_ring[0], 0, cast="scalar")
          XFM_DV = emit_xload(x_far if kpass == "far" else x_in, base_ac=AC0)
          gp_wv = load_weight(w_v, attn_ring[1], 0)
      else:
          XFM_DV = emit_xload(x_in, extra_pe=S.c("pe"))
      if kpass != kpasses[0]:
          S.do("gpsimd", lambda e: e.memset(maxk[:], -3e38),
               waits=[("dv", S.c("dv"))], inc="gp", drain=True)
      kev = {}
      kdiag_ac = {}
      drow_dma = {}
      for c in range(NCH):
          for tb in range(nB):
              u = c * nB + tb
              bank = u % 4
              wmm = [gp_wk, ("dv", XFM_DV)]
              if u < 4:
                  wmm.append(("ac", S.c("ac")))
                  wmm.append(("dv", S.c("dv")))
              if u >= 4:
                  wmm.append(("ac", kev[u - 4]))
              for kc in range(NCH):
                  S.do("tensor", (lambda c=c, tb=tb, kc=kc, bank=bank: (lambda e: e.matmul(
                      pb[bank][:, :], attn_ring[0][:, kc * 1024 + c * 128:kc * 1024 + (c + 1) * 128],
                      B_fm[:, kc * T + tb * 512:kc * T + (tb + 1) * 512],
                      start=(kc == 0), stop=(kc == NCH - 1))))(),
                      waits=wmm if kc == 0 else [], inc="pe" if kc == NCH - 1 else None)
              pemm = S.c("pe")
              S.do("vector", (lambda c=c, tb=tb, bank=bank: (lambda e: e.tensor_scalar_mul(
                  k_fm3[:, c, tb * 512:(tb + 1) * 512], pb[bank][:, :], DN)))(),
                  waits=[("pe", pemm)], inc="dv", drain=True)
              KCP_DV = S.c("dv")
              S.do("scalar", (lambda bank=bank, u=u: (lambda e: e.activation(
                  SQB[u % 2], pb[bank][:, :], AF.Square, bias=0.0, scale=DN)))(),
                  waits=[("dv", KCP_DV)], inc="ac")
              kev[u] = S.c("ac")
              for hh in range(2):
                  h = 2 * c + hh
                  dbank = 4 + (h * nB + tb) % 2
                  wdg = [("ac", kev[u])]
                  pk = h * nB + tb - 4
                  if pk in kdiag_ac:
                      wdg.append(("ac", kdiag_ac[pk]))
                  tp = (64, 64) if hh == 1 else None
                  S.do("tensor", (lambda hh=hh, u=u, dbank=dbank, tp=tp: (lambda e: e.matmul(
                      pb[dbank][64:65, :] if tp else pb[dbank][0:1, :],
                      ones_b[hh * 64:hh * 64 + 64, 0:1], SQB[u % 2][hh * 64:hh * 64 + 64, :],
                      start=True, stop=True, tile_position=tp)))(),
                      waits=wdg, inc="pe")
                  wdr = [("pe", S.c("pe"))]
                  if tb == 0 and c >= 1:
                      wdr.append(("gd", drow_dma[2 * (c - 1) + hh]))
                  kdiag_ac[h * nB + tb] = S.do("scalar",
                      (lambda hh=hh, tb=tb, dbank=dbank: (lambda e: e.activation(
                          DROW[hh][:, tb * 512:(tb + 1) * 512],
                          pb[dbank][64:65, :] if hh == 1 else pb[dbank][0:1, :],
                          AF.Copy, bias=0.0, scale=-0.5)))(),
                      waits=wdr, inc="ac")
          for hh in range(2):
              h = 2 * c + hh
              drow_dma[h] = S.do("gpsimd", (lambda h=h, hh=hh: (lambda e: e.dma_start(
                  out=dr_diag[h:h + 1, :], in_=DROW[hh])))(),
                  waits=[("ac", kdiag_ac[h * nB + nB - 1])], inc="gd", incn=16)
      K_PROJ_PE = S.c("pe")
      if stop_after == "kproj":
          return _probe_finish()

      vev = {}
      for tau in range(nT):
          for half in range(2):
              u = tau * 2 + half
              bank = u % 4
              wmm = [gp_wv, ("dv", XFM_DV)]
              if u < 4:
                  wmm.append(("ac", S.c("ac")))
                  wmm.append(("dv", S.c("dv")))
              if u >= 4:
                  wmm.append(("dv", vev[u - 4]))
              for kc in range(NCH):
                  S.do("tensor", (lambda tau=tau, half=half, kc=kc, bank=bank: (lambda e: e.matmul(
                      pb[bank][:, :], B_fm[:, kc * T + tau * 128:kc * T + (tau + 1) * 128],
                      attn_ring[1][:, kc * 1024 + half * 512:kc * 1024 + (half + 1) * 512],
                      start=(kc == 0), stop=(kc == NCH - 1))))(),
                      waits=wmm if kc == 0 else [], inc="pe" if kc == NCH - 1 else None)
              vev[u] = S.do("vector", (lambda tau=tau, half=half, bank=bank: (lambda e: e.tensor_copy(
                  C_big[:, VOFF + tau * 1040 + half * 520:VOFF + tau * 1040 + half * 520 + 520]
                  .rearrange("p (h d) -> p h d", h=8)[:, :, 0:64],
                  pb[bank][:, :].rearrange("p (h d) -> p h d", h=8))))(),
                  waits=[("pe", S.c("pe"))], inc="dv")
      V_DV = S.c("dv")
      if stop_after == "vproj":
          return _probe_finish()

      S.do("gpsimd", lambda e: e.dma_start(
          out=B16RT[0:nT, 0:H * 128].rearrange("q (h p) -> q h p", h=H),
          in_=dr_diag.rearrange("h (q p) -> q h p", p=128)),
          waits=[("gd", max(drow_dma.values()))], inc="gd", incn=16)
      GD_B16 = S.c("gd")
      dgt_ac = {}
      for h in range(H):
          tb6 = 6 + h % 2
          wtp = [("gd", GD_B16)]
          if h <= 1:
              wtp.append(("dv", S.c("dv")))
          if h >= 2:
              wtp.append(("ac", dgt_ac[h - 2]))
          S.do("tensor", (lambda h=h, tb6=tb6: (lambda e: e.transpose(
              pb[tb6][0:128, 0:nT], B16RT[0:nT, h * 128:(h + 1) * 128],
              ident_f[0:nT, 0:nT])))(),
              waits=wtp, inc="pe")
          dgt_ac[h] = S.do("scalar", (lambda h=h, tb6=tb6: (lambda e: e.copy(
              diagTk[:, h * 16:h * 16 + nT], pb[tb6][0:128, 0:nT])))(),
              waits=[("pe", S.c("pe"))], inc="ac")
      DIAGT_AC = S.c("ac")
      if stop_after == "diag":
          return _probe_finish()


      ctx_dv = {}
      cdma = {}
      kexp = {}
      ctx_pe = {}
      rmaxdv = {}
      LAG = 4
      NU = H * nT

      def kp_head(u):
          return u // nT, u % nT

      for u in range(NU + LAG):
          uc = u - LAG
          if uc >= 0:
              h, tau = kp_head(uc)
              cbank = 4 + h % 2
              wctx = [("ac", kexp[uc])]
              if tau == 0 and h >= 2:
                  wctx.append(("dv", ctx_dv[h - 2]))
              ctx_pe[uc] = S.do("tensor", (lambda h=h, tau=tau, cbank=cbank: (lambda e: e.matmul(
                  pb[cbank][0:65, 0:267],
                  C_big[:, VOFF + tau * 1040 + h * 65:VOFF + tau * 1040 + h * 65 + 65],
                  kp_sc[:, (tau % 4) * 267:(tau % 4) * 267 + 267],
                  start=(tau == 0), stop=(tau == nT - 1))))(),
                  waits=wctx, inc="pe")
              if tau == nT - 1:
                  S.do("scalar", (lambda h=h: (lambda e: e.activation(
                      lnout16[:, 0:nT], rmaxsb[:, (h % 2) * 16:(h % 2) * 16 + nT], AF.Ln, bias=0.0, scale=1.0)))(),
                      waits=[("dv", rmaxdv[uc])], inc="ac", drain=True)
                  S.do("vector", (lambda h=h: (lambda e: e.tensor_sub(
                      lnout16[:, 0:nT], lnout16[:, 0:nT], diagTk[:, h * 16:h * 16 + nT])))(),
                      waits=[("ac", S.c("ac"))], inc=None, drain=True)
                  S.do("vector", (lambda h=h: (lambda e: e.reduce_max(
                      lnin[:], lnout16[:, 0:nT], axis=AX)))(),
                      waits=[], inc=None, drain=True)
                  S.do("vector", (lambda h=h: (lambda e: e.tensor_max(
                      maxk[:, h:h + 1], maxk[:, h:h + 1], lnin[:])))(),
                      waits=[], inc="dv", drain=True)
                  wev = [("pe", S.c("pe"))]
                  if h >= 2:
                      wev.append(("gd" if part == "full" else "od", cdma[h - 2]))
                  ctx_dv[h] = S.do("vector", (lambda h=h, cbank=cbank: (lambda e: e.tensor_copy(
                      cstg[:, (h % 2) * 267:(h % 2) * 267 + 267], pb[cbank][0:65, 0:267])))(),
                      waits=wev, inc="dv")
                  if part == "fused":
                      ofs = 0 if kpass == "own" else 1
                      def slabfn(e, h=h, ofs=ofs, sl=(h % 2)):
                          sap = cstg[:, sl * 267:sl * 267 + 267]
                          with e.If_eq(rslot_sync, 0):
                              e.dma_start(out=d_cslab[0 * PAIRS + ofs][:, h * 267:(h + 1) * 267],
                                          in_=sap).then_inc(sems["od"], 16)
                          with e.Else():
                              e.dma_start(out=d_cslab[1 * PAIRS + ofs][:, h * 267:(h + 1) * 267],
                                          in_=sap).then_inc(sems["od"], 16)
                          e.end_ifs()
                          return None
                      S.do("sync", slabfn, waits=[("dv", ctx_dv[h]), ("od", S.c("od"))], inc=None)
                      S.bump("od", 16)
                      cdma[h] = S.c("od")
                  else:
                      cdst = dr_ci if part == "full" else c_out
                      csem = "gd" if part == "full" else "od"
                      cdma[h] = S.do("gpsimd" if part == "full" else "sync",
                          (lambda h=h, cdst=cdst: (lambda e: e.dma_start(
                          out=cdst[:, h * 267:(h + 1) * 267], in_=cstg[:, (h % 2) * 267:(h % 2) * 267 + 267])))(),
                          waits=[("dv", ctx_dv[h])], inc=csem, incn=16)
          if u < NU:
              h, tau = kp_head(u)
              hb = (h % 2) * 64
              hc = h // 2
              dbank = u % 4
              wmm = []
              if u < 4:
                  wmm = [("ac", DIAGT_AC), ("dv", V_DV), ("gd", GD_PROJ)]
              if u >= 4:
                  wmm.append(("ac", kexp[u - 4]))
              S.do("tensor", (lambda hb=hb, hc=hc, tau=tau, dbank=dbank: (lambda e: e.matmul(
                  pb[dbank][0:128, 0:266], k_fm3[hb:hb + 64, hc, tau * 128:(tau + 1) * 128],
                  projdup[hb:hb + 64, :], start=True, stop=True)))(),
                  waits=wmm, inc="pe")
              pedash = S.c("pe")
              wexp = [("pe", pedash)]
              if u >= 4:
                  wexp.append(("pe", ctx_pe.get(u - LAG, 0)))
                  wexp.append(("dv", rmaxdv[u - 4]))
              kexp[u] = S.do("scalar", (lambda h=h, tau=tau, dbank=dbank: (lambda e: e.activation(
                  kp_sc[:, (tau % 4) * 267:(tau % 4) * 267 + 266], pb[dbank][0:128, 0:266], AF.Exp,
                  bias=diagTk[:, h * 16 + tau:h * 16 + tau + 1], scale=1.0)))(),
                  waits=wexp, inc="ac")
              rmaxdv[u] = S.do("vector", (lambda h=h, tau=tau: (lambda e: e.reduce_max(
                  rmaxsb[:, (h % 2) * 16 + tau % 16:(h % 2) * 16 + tau % 16 + 1],
                  kp_sc[:, (tau % 4) * 267:(tau % 4) * 267 + 266], axis=AX)))(),
                  waits=[("ac", kexp[u])], inc="dv", drain=True)
      S.do("tensor", lambda e: e.transpose(pb[6][0:16, 0:128], maxk[:], ident_f[:]),
           waits=[("dv", S.c("dv")), ("ac", DIAGT_AC)], inc="pe")
      S.do("vector", lambda e: e.reduce_max(stabloc[:], pb[6][0:16, 0:128], axis=AX),
           waits=[("pe", S.c("pe"))], inc="dv", drain=True)
      if part == "fused":
          ofs = 0 if kpass == "own" else 1
          def stabfn(e, ofs=ofs):
              with e.If_eq(rslot_sync, 0):
                  e.dma_start(out=d_stab[0 * PAIRS + ofs][None, :].rearrange("o h -> (o h) o" if False else "o h -> h o"),
                              in_=stabloc[:]).then_inc(sems["od"], 16)
              with e.Else():
                  e.dma_start(out=d_stab[1 * PAIRS + ofs][None, :].rearrange("o h -> h o"),
                              in_=stabloc[:]).then_inc(sems["od"], 16)
              e.end_ifs()
              return None
          S.do("sync", stabfn, waits=[("dv", S.c("dv")), ("od", S.c("od"))], inc=None)
          S.bump("od", 16)

    if part == "fused":
        S.do("gpsimd", lambda e: e.memset(magict[:], MAGIC_BITS),
             waits=[], inc="gp", drain=True)
        def flagfn(e):
            with e.If_eq(rslot_gp, 0):
                e.dma_start(out=d_flag[0], in_=magict[:]).then_inc(sems["gd"], 16)
            with e.Else():
                e.dma_start(out=d_flag[1], in_=magict[:]).then_inc(sems["gd"], 16)
            e.end_ifs()
            return None
        S.do("gpsimd", flagfn,
             waits=[("od", S.c("od")), ("gp", S.c("gp")), ("gd", S.c("gd"))], inc=None)
        S.bump("gd", 16)
        GD_FLAG = S.c("gd")
        def pollfn(e):
            with e.register("cnt") as rcnt, e.register("pdc") as rpd, e.register("fv") as rfv:
                e.reg_mov(rcnt, 0)
                e.reg_mov(rpd, 0)
                def cond():
                    return e.compute_val(rcnt) < 2
                with e.While(cond):
                    e.reg_add(rpd, rpd, 16)
                    e.dma_start(out=flgt[:], in_=d_flag.rearrange("a b -> (a b)")[None, :]).then_inc(sems["pd"], 16)
                    e.wait_ge(sems["pd"], rpd)
                    e.reg_mov(rcnt, 0)
                    for c in range(2):
                        e.reg_load(rfv, flgt[0:1, c * 16:c * 16 + 1])
                        with e.If_eq(rfv, MAGIC_BITS):
                            e.reg_add(rcnt, rcnt, 1)
                        e.end_ifs()
            return e.nop()
        S.do("gpsimd", pollfn, waits=[("gd", GD_FLAG)], inc="gp")
        BARRIER_GP = S.c("gp")
    if part != "a":
        gp_wq = load_weight(w_q, attn_ring[0], K_PROJ_PE, cast="scalar")

    if part == "a":
        S.do("sync", lambda e: e.dma_start(out=s_out, in_=stabloc[:]),
             waits=[("dv", S.c("dv"))], inc="od", incn=16)
        S.do("sync", lambda e: e.wait_ge(sems["od"], S.c("od")), waits=[("gd", S.c("gd"))])
        with nc.Block() as blk:
            for engname in ENGS:
                steps = S.q[engname]
                def body(eng, steps=steps):
                    for fn, waits, inc, incn, drain in steps:
                        for sname, v in waits:
                            eng.wait_ge(sems[sname], v)
                        if drain:
                            eng.drain()
                        ins = fn(eng)
                        if inc is not None:
                            assert ins is not None
                            ins.then_inc(sems[inc], incn)
                getattr(blk, engname)(body)
        st.close()
        return nc
    if part == "full":
        S.do("gpsimd", lambda e: e.dma_start(out=dr_si.rearrange("o h -> h o"), in_=stabloc[:]),
             waits=[("dv", S.c("dv"))], inc="gd", incn=16)
        S.do("gpsimd", lambda e: e.collective_compute("AllReduce", ALU.max, groups, [dr_si], [dr_so]),
             waits=[("gd", S.c("gd"))], inc="cc", incn=16)
        S.do("gpsimd", lambda e: e.collective_compute("AllReduce", ALU.add, groups, [dr_ci], [dr_co]),
             waits=[("gd", max(cdma.values())), ("cc", S.c("cc"))], inc="cc", incn=16)
        CC_DONE = S.c("cc")
        S.do("gpsimd", lambda e: e.dma_start(out=stabrow[:], in_=dr_so),
             waits=[("cc", CC_DONE)], inc="gd", incn=16)
    qev = {}
    qdiag_ac = {}
    for h in range(H):
        for tb in range(nB):
            u = h * nB + tb
            bank = u % 4
            wmm = [gp_wq, ("dv", XFM_DV)]
            if u < 4:
                wmm.append(("ac", S.c("ac")))
                wmm.append(("dv", S.c("dv")))
            if u >= 4:
                wmm.append(("ac", qev[u - 4]))
            for kc in range(NCH):
                S.do("tensor", (lambda h=h, tb=tb, kc=kc, bank=bank: (lambda e: e.matmul(
                    pb[bank][0:64, :], attn_ring[0][:, kc * 1024 + h * 64:kc * 1024 + h * 64 + 64],
                    B_fm[:, kc * T + tb * 512:kc * T + (tb + 1) * 512],
                    start=(kc == 0), stop=(kc == NCH - 1))))(),
                    waits=wmm if kc == 0 else [], inc="pe" if kc == NCH - 1 else None)
            peq = S.c("pe")
            S.do("vector", (lambda h=h, tb=tb, bank=bank: (lambda e: e.tensor_scalar_mul(
                q_ex[0:64, h * T + tb * 512:h * T + (tb + 1) * 512], pb[bank][0:64, :], DN)))(),
                waits=[("pe", peq)], inc="dv", drain=True)
            QCP_DV = S.c("dv")
            S.do("scalar", (lambda bank=bank, u=u: (lambda e: e.activation(
                SQB[u % 2][0:64, :], pb[bank][0:64, :], AF.Square, bias=0.0, scale=DN)))(),
                waits=[("dv", QCP_DV)], inc="ac")
            qev[u] = S.c("ac")
            dbank = 4 + u % 2
            wdg = [("ac", qev[u])]
            if u < 2:
                wdg.append(("ac", S.c("ac")))
                wdg.append(("dv", S.c("dv")))
            if u - 4 in qdiag_ac:
                wdg.append(("ac", qdiag_ac[u - 4]))
            S.do("tensor", (lambda u=u, dbank=dbank: (lambda e: e.matmul(
                pb[dbank][64:65, :], ones_b[0:64, 0:1], SQB[u % 2][0:64, :],
                start=True, stop=True, tile_position=(0, 64))))(),
                waits=wdg, inc="pe")
            qdiag_ac[u] = S.do("scalar", (lambda h=h, tb=tb, dbank=dbank: (lambda e: e.activation(
                q_ex[64:65, h * T + tb * 512:h * T + (tb + 1) * 512], pb[dbank][64:65, :],
                AF.Copy, bias=0.0, scale=-0.5)))(),
                waits=[("pe", S.c("pe"))], inc="ac")
    QPROJ_AC = S.c("ac")
    QPROJ_PE = S.c("pe")
    if part in ("b", "fused"):
        CC_DONE = 0
        GX = G if part == "b" else NSL
        if part == "b":
            S.do("sync", lambda e: e.dma_start(out=stab4[0:1, 0:GX * 16], in_=s_in), waits=[], inc="xd", incn=16)
        else:
            S.do("sync", lambda e: e.dma_start(out=stab4[0:1, 0:GX * 16],
                                               in_=d_stab.rearrange("a b -> (a b)")[None, :]),
                 waits=[("gp", BARRIER_GP)], inc="xd", incn=16)
        n = GX
        first = True
        while n > 1:
            half = n // 2
            S.do("vector", (lambda half=half: (lambda e: e.tensor_max(
                stab4[0:1, 0:half * 16].rearrange("o (a h) -> o a h", h=16),
                stab4[0:1, 0:half * 16].rearrange("o (a h) -> o a h", h=16),
                stab4[0:1, half * 16:2 * half * 16].rearrange("o (a h) -> o a h", h=16))))(),
                waits=[("xd", S.c("xd"))] if first else [], inc=None, drain=True)
            first = False
            n = half
        S.do("vector", lambda e: e.tensor_copy(stabrow[:], stab4[0:1, 0:16]),
             waits=[], inc="dv", drain=True)
    gp_woA = load_weight(w_o, attn_ring[0], QPROJ_PE, cast="scalar", pieces=(3, 4, 5, 6, 7))
    S.do("tensor", lambda e: e.matmul(pb[4][0:128, 0:16], onesf[:], stabrow[:], start=True, stop=True),
         waits=[("gd", S.c("gd")), ("ac", S.c("ac")), ("dv", S.c("dv")), ("gp", GP_INIT)], inc="pe")
    S.do("scalar", lambda e: e.activation(estab[:], pb[4][0:128, 0:16], AF.Exp, bias=0.0, scale=-1.0),
         waits=[("pe", S.c("pe"))], inc="ac")
    ESTAB_AC = S.c("ac")
    caug_dv = {}
    cadd_dv = {}
    for h in range(H):
        slot = h % 2
        c4 = cstg4s[h % 2]
        wdma = [("cc", CC_DONE)] if part == "full" else []
        if part == "full":
            if h >= 2:
                wdma.append(("dv", caug_dv[h - 2]))
            S.do("gpsimd", (lambda h=h, slot=slot: (lambda e: e.dma_start(
                out=cstg[:, slot * 267:slot * 267 + 267], in_=dr_co[:, h * 267:(h + 1) * 267])))(),
                waits=wdma, inc="gd", incn=16)
        else:
            GX = G if part == "b" else NSL
            CSRC = c_in if part == "b" else d_cslab
            if h >= 2:
                wdma.append(("dv", cadd_dv[h - 2]))
            else:
                wdma.append(("ac", ESTAB_AC))
            if part == "fused" and h == 0:
                wdma.append(("gp", BARRIER_GP))
            S.do("gpsimd", (lambda h=h, GX=GX, CSRC=CSRC, c4=c4: (lambda e: e.dma_start(
                out=c4[0:65, 0:GX * 267].rearrange("p (g n) -> p g n", g=GX),
                in_=CSRC[:, :, h * 267:(h + 1) * 267].rearrange("g p n -> p g n"))))(),
                waits=wdma, inc="gd", incn=16)
            n = GX
            first = True
            while n > 2:
                half = n // 2
                S.do("vector", (lambda half=half, c4=c4: (lambda e: e.tensor_add(
                    c4[0:65, 0:half * 267].rearrange("p (g n) -> p g n", n=267),
                    c4[0:65, 0:half * 267].rearrange("p (g n) -> p g n", n=267),
                    c4[0:65, half * 267:2 * half * 267].rearrange("p (g n) -> p g n", n=267))))(),
                    waits=[("gd", S.c("gd"))] if first else [], inc=None, drain=True)
                first = False
                n = half
            S.do("vector", (lambda slot=slot, first=first, c4=c4: (lambda e: e.tensor_add(
                cstg[:, slot * 267:slot * 267 + 267], c4[0:65, 0:267], c4[0:65, 267:534])))(),
                waits=[("gd", S.c("gd"))] if first else [], inc="dv", drain=True)
            cadd_dv[h] = S.c("dv")
        csrc_w = ("gd", S.c("gd")) if part == "full" else ("dv", S.c("dv"))
        tbank = 6 + h % 2
        wtp = [csrc_w, ("ac", ESTAB_AC)]
        if h < 2:
            wtp.append(("dv", S.c("dv")))
        if h >= 2:
            wtp.append(("dv", caug_dv[h - 2]))
        for i, (m0, m1) in enumerate(MCS):
            S.do("tensor", (lambda slot=slot, i=i, m0=m0, m1=m1, tbank=tbank: (lambda e: e.transpose(
                pb[tbank][0:m1 - m0, i * 65:(i + 1) * 65], cstg[:, slot * 267 + m0:slot * 267 + m1],
                ident_f[0:65, 0:65])))(),
                waits=wtp if i == 0 else [], inc=None)
        S.do("tensor", (lambda slot=slot, tbank=tbank: (lambda e: e.transpose(
            pb[tbank][0:1, 195:260], cstg[:, slot * 267 + 266:slot * 267 + 267], ident_f[0:65, 0:65])))(),
            waits=[], inc="pe")
        petp = S.c("pe")
        S.do("vector", (lambda tbank=tbank: (lambda e: e.tensor_copy(vsn[:], pb[tbank][0:1, 195:260])))(),
             waits=[("pe", petp)], inc="dv", drain=True)
        ebank = 2 + h % 2
        S.do("tensor", (lambda ebank=ebank: (lambda e: e.matmul(
            pb[ebank][0:128, 0:65], epsf[:], vsn[:], start=True, stop=True)))(),
            waits=[("dv", S.c("dv")), ("ac", QPROJ_AC)], inc="pe")
        S.do("vector", (lambda ebank=ebank: (lambda e: e.tensor_copy(epsv[:], pb[ebank][0:128, 0:65])))(),
             waits=[("pe", S.c("pe"))], inc="dv", drain=True)
        for i, (m0, m1) in enumerate(MCS):
            rows = m1 - m0
            caug_dv[h] = S.do("vector", (lambda h=h, i=i, rows=rows, tbank=tbank: (lambda e: e.scalar_tensor_tensor(
                CAUG[0:rows, h * 195 + i * 65:h * 195 + (i + 1) * 65],
                pb[tbank][0:rows, i * 65:(i + 1) * 65], estab[0:rows, h:h + 1],
                epsv[0:rows, :], ALU.mult, ALU.add)))(),
                waits=[], inc="dv", drain=True)
    CAUG_DV = S.c("dv")

    rmx_dma = {}
    p1dv = {}
    for h in range(H):
        for tau in range(nT):
            u = h * nT + tau
            dbank = u % 4
            wmm = [("ac", QPROJ_AC)]
            if u == 0:
                wmm.append(("gd", GD_B16))
                wmm.append(("dv", S.c("dv")))
            if u >= 4:
                wmm.append(("dv", p1dv[u - 4]))
            S.do("tensor", (lambda h=h, tau=tau, dbank=dbank: (lambda e: e.matmul(
                pb[dbank][0:128, 0:266], q_ex[:, h * T + tau * 128:h * T + (tau + 1) * 128],
                projex[:], start=True, stop=True)))(),
                waits=wmm, inc="pe")
            p1dv[u] = S.do("vector", (lambda tau=tau, dbank=dbank: (lambda e: e.reduce_max(
                rmaxsb[:, tau:tau + 1], pb[dbank][0:128, 0:266], axis=AX)))(),
                waits=[("pe", S.c("pe"))], inc="dv")
        wtp = [("dv", S.c("dv"))]
        if h >= 1:
            wtp.append(("gd", rmx_dma[h - 1]))
        S.do("tensor", (lambda: (lambda e: e.transpose(
            pb[6][0:nT, 128:256], rmaxsb[:, 0:nT], ident_f[:])))(),
            waits=wtp, inc="pe")
        S.do("vector", lambda e: e.tensor_copy(B16RT[0:nT, 0:128], pb[6][0:nT, 128:256]),
             waits=[("pe", S.c("pe"))], inc="dv", drain=True)
        rmx_dma[h] = S.do("gpsimd", (lambda h=h: (lambda e: e.dma_start(
            out=dr_rmax[h:h + 1, :].rearrange("o (q p) -> (o q) p", p=128), in_=B16RT[0:nT, 0:128])))(),
            waits=[("dv", S.c("dv"))], inc="gd", incn=16)
    P1_PE = S.c("pe")

    statq_dv = {}
    qexp = {}
    qexp_dv = {}
    st5ev = {}
    st5ac = {}
    numsc_dv = {}
    nsbdv = {}

    statq_dma = {}

    def emit_statq_dma(h):
        wrb = [("gd", rmx_dma[h])]
        if h >= 2:
            wrb.append(("dv", statq_dv[h - 2]))
        statq_dma[h] = S.do("gpsimd",
            (lambda h=h: (lambda e: e.dma_start(out=RM64S[h % 2], in_=dr_rmax[h:h + 1, :])))(),
            waits=wrb, inc="gd", incn=16)

    def emit_statq_stt(h):
        statq_dv[h] = S.do("vector", (lambda h=h: (lambda e: e.scalar_tensor_tensor(
            q_ex[64:65, h * T:(h + 1) * T], q_ex[64:65, h * T:(h + 1) * T], 2.0, RM64S[h % 2],
            ALU.mult, ALU.subtract)))(),
            waits=[("gd", statq_dma[h]), ("pe", P1_PE)], inc="dv", drain=True)

    for hh0 in (0, 1):
        emit_statq_dma(hh0)
        emit_statq_stt(hh0)
    U = H * nB

    def stage_A(u, mcs):
        h, tb = u // nB, u % nB
        if nB < 3 and 0 in mcs and tb == 0 and 1 <= h < H - 1:
            emit_statq_dma(h + 1)
            emit_statq_stt(h + 1)
        qslot = u % 2
        for i, (m0, m1) in ((j, MCS[j]) for j in mcs):
            rows = m1 - m0
            uu = u * 3 + i
            bank = uu % 2
            wmm = [("dv", statq_dv[h])]
            if uu >= 2:
                wmm.append(("ac", qexp[uu - 2]))
                wmm.append(("dv", qexp_dv[uu - 2]))
            S.do("tensor", (lambda h=h, tb=tb, m0=m0, m1=m1, bank=bank: (lambda e: e.matmul(
                pb[bank][0:m1 - m0, :], projex[:, m0:m1],
                q_ex[:, h * T + tb * 512:h * T + (tb + 1) * 512], start=True, stop=True)))(),
                waits=wmm, inc="pe")
            pem = S.c("pe")
            qexp[uu] = S.do("scalar", (lambda rows=rows, bank=bank: (lambda e: e.activation(
                SQF[bank][0:rows, :], pb[bank][0:rows, :], AF.Exp, bias=0.0, scale=1.0)))(),
                waits=[("pe", pem)], inc="ac", drain=True)
            qexp_dv[uu] = S.do("vector", (lambda i=i, rows=rows, bank=bank, qslot=qslot: (lambda e: e.tensor_scalar_add(
                QP.rearrange("p (s x) -> p s x", s=2)[0:rows, qslot, i * 512:(i + 1) * 512],
                SQF[bank][0:rows, :], EPS)))(),
                waits=[("ac", S.c("ac"))], inc="dv", drain=True)

    st5pe = {}

    def stage_B(u):
        h, tb = u // nB, u % nB
        qslot = u % 2
        sbank = 4 + u % 4
        for i, (m0, m1) in enumerate(MCS):
            rows = m1 - m0
            wmm = [("dv", qexp_dv[u * 3 + i]), ("dv", CAUG_DV)]
            if i == 0:
                if u - 4 in st5ev:
                    wmm.append(("dv", st5ev[u - 4]))
                    wmm.append(("ac", st5ac[u - 4]))
                if u < 4:
                    wmm.append(("dv", S.c("dv")))
            S.do("tensor", (lambda h=h, i=i, rows=rows, qslot=qslot, sbank=sbank: (lambda e: e.matmul(
                pb[sbank][0:65, :], CAUG[0:rows, h * 195 + i * 65:h * 195 + (i + 1) * 65],
                QP.rearrange("p (s x) -> p s x", s=2)[0:rows, qslot, i * 512:(i + 1) * 512],
                start=(i == 0), stop=(i == 2))))(),
                waits=wmm, inc="pe" if i == 2 else None)
        st5pe[u] = S.c("pe")
        st5ac[u] = S.do("scalar", (lambda u2=u % 2, sbank=sbank: (lambda e: e.activation(
            LNR[u2], pb[sbank][64:65, :], AF.Ln, bias=0.0, scale=1.0)))(),
            waits=[("pe", st5pe[u])], inc="ac", drain=True)
        S.do("scalar", (lambda u2=u % 2: (lambda e: e.activation(
            DINVB[u2], LNR[u2], AF.Exp, bias=0.0, scale=-1.0)))(),
            waits=[], inc="ac", drain=True)

    def stage_C(u):
        h, tb = u // nB, u % nB
        if nB >= 3:
            if tb == 0 and h + 2 < H:
                emit_statq_dma(h + 2)
            if tb == 2 and h + 2 < H:
                emit_statq_stt(h + 2)
        sbank = 4 + u % 4
        bbank = 2 + u % 2
        S.do("tensor", (lambda u2=u % 2, bbank=bbank: (lambda e: e.matmul(
            pb[bbank][0:64, :], ones_b[64:65, 0:64], DINVB[u2], start=True, stop=True)))(),
            waits=[("ac", st5ac[u] + 1), ("dv", nsbdv.get(u - 2, 0))], inc="pe")
        S.do("vector", (lambda u4=u % 4, bbank=bbank: (lambda e: e.tensor_copy(NSB[u4], pb[bbank][0:64, :])))(),
             waits=[("pe", S.c("pe"))], inc="dv", drain=True)
        ACnsb = S.c("dv")
        nsbdv[u] = ACnsb
        if h % 2 == 0:
            st5ev[u] = S.do("vector", (lambda h=h, tb=tb, sbank=sbank, u4=u % 4: (lambda e: e.tensor_mul(
                B3[0:64, h // 2, tb * 512:(tb + 1) * 512], pb[sbank][0:64, :], NSB[u4])))(),
                waits=[("dv", ACnsb)], inc="dv")
        else:
            st5ev[u] = S.do("vector", (lambda tb=tb, sbank=sbank, u4=u % 4: (lambda e: e.tensor_mul(
                NUMSC[tb % 2], pb[sbank][0:64, :], NSB[u4])))(),
                waits=[("dv", ACnsb)] + ([("gd", numsc_dv[u - 2])] if (u - 2) in numsc_dv else []),
                inc="dv")
            numsc_dv[u] = S.do("gpsimd", (lambda h=h, tb=tb: (lambda e: e.dma_start(
                out=B3[64:128, h // 2, tb * 512:(tb + 1) * 512], in_=NUMSC[tb % 2])))(),
                waits=[("dv", st5ev[u])], inc="gd", incn=16)

    RING_PE = {}
    for v in range(U + 2):
        if v < U:
            stage_A(v, (0, 1))
        if 1 <= v <= U:
            stage_B(v - 1)
            if v - 1 in (15, 31, 47):
                RING_PE[v - 1] = S.c("pe")
        if v < U:
            stage_A(v, (2,))
        if v >= 2:
            stage_C(v - 2)
    ATTN_DV = S.c("dv")
    ATTN_GD = S.c("gd")
    ST5_PE = S.c("pe")
    AC_ATTN = S.c("ac")

    wgp = {}
    S.do("sync", lambda e: e.dma_start(out=BOSB, in_=b_o[None, :]),
         waits=[("pe", P1_PE)], inc="xd", incn=16)
    XD_BO = S.c("xd")
    for half in range(2):
        S.do("tensor", (lambda half=half: (lambda e: e.matmul(
            pb[2 + half][0:128, :], onesf[:], BOSB[:, half * 512:(half + 1) * 512], start=True, stop=True)))(),
            waits=[("xd", XD_BO), ("dv", ATTN_DV), ("ac", AC_ATTN)], inc="pe")
        S.do("scalar", (lambda half=half: (lambda e: e.copy(
            BOBC[:, half * 512:(half + 1) * 512], pb[2 + half][0:128, :])))(),
            waits=[("pe", S.c("pe")), ("gd", ATTN_GD)], inc="ac")
    BOBC_AC = S.c("ac")

    gp_wo = load_weight(w_o, attn_ring[0], ST5_PE, cast="scalar", pieces=(0, 1, 2))
    wgp[("w1", 0)] = load_weight(w_1[0], ring_c(0), ST5_PE)
    wgp[("w2", 0)] = load_weight(w_2[0], ring_c(1), ST5_PE)

    woev = {}
    xfm2_dv = {}
    ln_dv = {}

    def emit_xfm2(tau):
        bank = pbb[6 + tau % 2]
        wtp = [("dv", ln_dv[tau])]
        if tau >= 2:
            wtp.append(("dv", xfm2_dv[tau - 2]))
        for c in range(NCH):
            S.do("tensor", (lambda tau=tau, c=c, bank=bank: (lambda e: e.transpose(
                bank[:, c * 128:(c + 1) * 128],
                x_tm[:, tau * 1024 + c * 128:tau * 1024 + (c + 1) * 128], ident_b[:])))(),
                waits=wtp if c == 0 else [], inc="pe" if c == NCH - 1 else None)
        xfm2_dv[tau] = S.do("vector", (lambda tau=tau, bank=bank: (lambda e: e.tensor_copy(
            B3[:, :, tau * 128:(tau + 1) * 128],
            bank[:, 0:1024].rearrange("p (c n) -> p c n", c=NCH))))(),
            waits=[("pe", S.c("pe"))], inc="dv")

    for tau in range(nT):
        zslot = tau % 2
        wz = [("ac", BOBC_AC)]
        if tau >= 2:
            wz.append(("dv", ln_dv[tau - 2]))
        S.do("vector", (lambda tau=tau, zslot=zslot: (lambda e: e.tensor_add(
            ZF[zslot], x_tm[:, tau * 1024:(tau + 1) * 1024], BOBC)))(),
            waits=wz, inc="dv", drain=True)
        ZXBO = S.c("dv")
        for half in range(2):
            u = tau * 2 + half
            bank = u % 2
            wmm = [gp_wo, ("dv", ATTN_DV), ("gd", ATTN_GD)]
            if u >= 2:
                wmm.append(("dv", woev[u - 2]))
            for kc in range(NCH):
                S.do("tensor", (lambda tau=tau, half=half, kc=kc, bank=bank: (lambda e: e.matmul(
                    pb[bank][:, :], B_fm[:, kc * T + tau * 128:kc * T + (tau + 1) * 128],
                    attn_ring[0][:, kc * 1024 + half * 512:kc * 1024 + (half + 1) * 512],
                    start=(kc == 0), stop=(kc == NCH - 1))))(),
                    waits=wmm if kc == 0 else [], inc="pe" if kc == NCH - 1 else None)
            woev[u] = S.do("vector", (lambda half=half, bank=bank, zslot=zslot: (lambda e: e.tensor_add(
                ZF[zslot][:, half * 512:(half + 1) * 512], ZF[zslot][:, half * 512:(half + 1) * 512],
                pb[bank][:, :])))(),
                waits=[("pe", S.c("pe")), ("dv", ZXBO)], inc="dv", drain=True)
        wln = [("dv", S.c("dv"))]
        if tau >= 1:
            wln.append(("ac", ln_dv[(tau - 1, "sq")]))
        S.do("vector", (lambda zslot=zslot: (lambda e: e.bn_stats(bn6[:, 0:6], ZF[zslot][:, 0:512])))(),
             waits=wln, inc=None, drain=True)
        S.do("vector", (lambda zslot=zslot: (lambda e: e.bn_stats(bn6[:, 6:12], ZF[zslot][:, 512:1024])))(),
             waits=[], inc=None, drain=True)
        S.do("vector", lambda e: e.bn_aggr(agg2[:], bn6[:]), waits=[], inc="dv", drain=True)
        ln_dv[(tau, "sq")] = S.do("scalar", lambda e: e.activation(
            rstd[:], agg2[:, 1:2], AF.Sqrt, bias=lneps[:], scale=1.0),
            waits=[("dv", S.c("dv"))], inc="ac")
        S.do("vector", lambda e: e.reciprocal(rstd[:], rstd[:]),
             waits=[("ac", ln_dv[(tau, "sq")])], inc=None, drain=True)
        S.do("vector", lambda e: e.tensor_scalar(nbv[:], agg2[:, 0:1], rstd[:, 0:1], -1.0, ALU.mult, ALU.mult),
             waits=[], inc=None, drain=True)
        ln_dv[tau] = S.do("vector", (lambda tau=tau, zslot=zslot: (lambda e: e.tensor_scalar(
            x_tm[:, tau * 1024:(tau + 1) * 1024], ZF[zslot], rstd[:, 0:1], nbv[:, 0:1], ALU.mult, ALU.add)))(),
            waits=[], inc="dv", drain=True)
        if tau >= 2:
            emit_xfm2(tau - 2)
    emit_xfm2(nT - 2)
    emit_xfm2(nT - 1)
    WO_PE = S.c("pe")
    LN1_DV = S.c("dv")

    S.do("sync", lambda e: e.dma_start(out=WGSTG.rearrange("p (c e) -> p c e", c=NCH),
                                       in_=w_g.rearrange("(c p) e -> p c e", p=128)),
         waits=[("dv", ATTN_DV)], inc="xd", incn=16)
    S.do("gpsimd", lambda e: e.tensor_copy(wgb[:], WGSTG), waits=[("xd", S.c("xd"))], inc="gp", drain=True)
    S.do("sync", lambda e: e.dma_start(out=BGSTG, in_=b_g[None, :]), waits=[], inc="xd", incn=16)
    S.do("gpsimd", lambda e: e.tensor_copy(bgb[:], BGSTG), waits=[("xd", S.c("xd"))], inc="gp", drain=True)
    GP_GATE = S.c("gp")
    S.do("sync", lambda e: e.dma_start(out=BSTG, in_=b_1), waits=[("gp", S.c("gp"))], inc="xd", incn=16)
    XD_B1 = S.c("xd")
    for c in range(NCH):
        S.do("tensor", (lambda c=c: (lambda e: e.transpose(
            pb[2][0:128, c * 8:(c + 1) * 8], BSTG[:, c * 128:(c + 1) * 128], ident_f[0:8, 0:8])))(),
            waits=[("xd", XD_B1), ("ac", BOBC_AC)] if c == 0 else [], inc="pe" if c == NCH - 1 else None)
    S.do("scalar", lambda e: e.copy(b1T[:], pb[2][0:128, 0:64]), waits=[("pe", S.c("pe"))], inc="ac")
    B1T_AC = S.c("ac")
    S.do("sync", lambda e: e.dma_start(out=BSTG, in_=b_2), waits=[("ac", B1T_AC)], inc="xd", incn=16)
    S.do("gpsimd", lambda e: e.tensor_copy(B2B, BSTG), waits=[("xd", S.c("xd")), ("pe", S.c("pe"))],
         inc="gp", drain=True)
    GP_MISC = S.c("gp")

    ohT_dv = {}
    for tau in range(nT):
        gbank = 2 + tau % 2
        wmm = [("dv", xfm2_dv[tau]), ("gp", GP_GATE), ("ac", B1T_AC)]
        if tau >= 2:
            wmm.append(("dv", ohT_dv[tau - 2]))
        for kc in range(NCH):
            S.do("tensor", (lambda tau=tau, kc=kc, gbank=gbank: (lambda e: e.matmul(
                pb[gbank][0:128, 0:8], B_fm[:, kc * T + tau * 128:kc * T + (tau + 1) * 128],
                wgb[:, kc * 8:(kc + 1) * 8], start=(kc == 0), stop=False)))(),
                waits=wmm if kc == 0 else [], inc=None)
        S.do("tensor", (lambda gbank=gbank: (lambda e: e.matmul(
            pb[gbank][0:128, 0:8], ones_b[0:1, 0:128], bgb[:], start=False, stop=True)))(),
            waits=[], inc="pe")
        peg = S.c("pe")
        S.do("vector", (lambda gbank=gbank: (lambda e: e.reduce_max(ohmax[:], pb[gbank][0:128, 0:8], axis=AX)))(),
             waits=[("pe", peg)], inc=None, drain=True)
        S.do("vector", (lambda gbank=gbank: (lambda e: e.tensor_scalar(
            oh_f[:], pb[gbank][0:128, 0:8], ohmax[:, 0:1], None, ALU.is_equal)))(),
            waits=[], inc="dv", drain=True)
        S.do("tensor", lambda e: e.transpose(pb[6][0:8, 256:384], oh_f[:], ident_f[:]),
             waits=[("dv", S.c("dv"))], inc="pe")
        ohT_dv[tau] = S.do("vector", (lambda tau=tau: (lambda e: e.tensor_copy(
            ohT[:, tau * 128:(tau + 1) * 128], pb[6][0:8, 256:384])))(),
            waits=[("pe", S.c("pe"))], inc="dv")
    OHT_DV = S.c("dv")

    ring_last = {0: RING_PE.get(15, ST5_PE), 1: RING_PE.get(31, ST5_PE), 2: RING_PE.get(47, ST5_PE)}
    hm_free = {0: 0, 1: 0}
    selfree = {0: OHT_DV, 1: OHT_DV}
    ybfree = {0: 0, 1: 0}
    pre_free = {0: 0, 1: 0}
    gelu_prev = {}
    lastmask = {}

    for ee in range(E):
        s1 = (2 * ee) % 3
        s2 = (2 * ee + 1) % 3

        def w1_block(tb, ee=ee, s1=s1):
            selb = (ee * nB + tb) % 2
            sb = 2 + selb
            selpe = S.do("tensor", (lambda ee=ee, tb=tb, sb=sb: (lambda e: e.matmul(
                pb[sb][0:128, :], sel8[:, ee * 128:(ee + 1) * 128],
                ohT[:, tb * 512:(tb + 1) * 512], start=True, stop=True)))(),
                waits=[("dv", selfree[selb]), wgp[("w1", ee)], ("gp", GP_MISC),
                       ("pe", hm_free[tb % 2])],
                inc="pe")
            for ic in range(NCH):
                u = ic % 2
                bank = u
                wmm = [("ac", pre_free[u])] if pre_free[u] else []
                for kc in range(NCH):
                    S.do("tensor", (lambda tb=tb, ic=ic, kc=kc, bank=bank, s1=s1: (lambda e: e.matmul(
                        pb[bank][:, :], ring_c(s1)[:, kc * 1024 + ic * 128:kc * 1024 + (ic + 1) * 128],
                        B_fm[:, kc * T + tb * 512:kc * T + (tb + 1) * 512],
                        start=(kc == 0), stop=(kc == NCH - 1))))(),
                        waits=wmm if kc == 0 else [], inc="pe" if kc == NCH - 1 else None)
                pew1 = S.c("pe")
                wg = [("pe", pew1), ("ac", B1T_AC)]
                if (ee, tb, ic - 1) in gelu_prev:
                    wg.append(("dv", gelu_prev[(ee, tb, ic - 1)]))
                pre_free[u] = S.do("scalar", (lambda ee=ee, ic=ic, bank=bank: (lambda e: e.activation(
                    hg_b[:], pb[bank][:, :], AF.Gelu, bias=b1T[:, ic * 8 + ee:ic * 8 + ee + 1], scale=1.0)))(),
                    waits=wg, inc="ac")
                gelu_prev[(ee, tb, ic)] = S.do("vector",
                    (lambda tb=tb, ic=ic, sb=sb: (lambda e: e.tensor_mul(
                        HM[tb % 2][:, ic * 512:(ic + 1) * 512], hg_b[:], pb[sb][0:128, :])))(),
                    waits=[("ac", pre_free[u]), ("pe", selpe)], inc="dv")
            lastmask[(ee, tb)] = gelu_prev[(ee, tb, NCH - 1)]
            selfree[selb] = lastmask[(ee, tb)]
            if tb == nB - 1:
                ring_last[s1] = S.c("pe")

        def w2_block(tb, ee=ee, s2=s2):
            for tsub in range(4):
                tau = tb * 4 + tsub
                for half in range(2):
                    u = (tsub * 2 + half) % 2
                    ybank = 4 + u
                    wmm = [wgp[("w2", ee)], ("dv", lastmask[(ee, tb)]), ("dv", OHT_DV)]
                    if ybfree[u]:
                        wmm.append(("dv", ybfree[u]))
                    first = True
                    if ee == 0:
                        S.do("tensor", (lambda tau=tau, half=half, ybank=ybank: (lambda e: e.matmul(
                            pb[ybank][:, :], ohT[:, tau * 128:(tau + 1) * 128],
                            B2B[:, half * 512:(half + 1) * 512], start=True, stop=False)))(),
                            waits=wmm, inc=None)
                        first = False
                    for ic in range(NCH):
                        S.do("tensor", (lambda tb=tb, tsub=tsub, half=half, ic=ic, ybank=ybank, s2=s2:
                             (lambda e: e.matmul(
                                 pb[ybank][:, :],
                                 HM[tb % 2][:, ic * 512 + tsub * 128:ic * 512 + (tsub + 1) * 128],
                                 ring_c(s2)[:, ic * 1024 + half * 512:ic * 1024 + (half + 1) * 512],
                                 start=(first and ic == 0), stop=(ic == NCH - 1))))(),
                            waits=wmm if (first and ic == 0) else [], inc="pe" if ic == NCH - 1 else None)
                    if ee == 0:
                        ybfree[u] = S.do("vector", (lambda tau=tau, half=half, ybank=ybank: (lambda e: e.tensor_copy(
                            y_acc[:, tau * 1024 + half * 512:tau * 1024 + (half + 1) * 512],
                            pb[ybank][:, :])))(),
                            waits=[("pe", S.c("pe")), ("pe", WO_PE)], inc="dv")
                    else:
                        ybfree[u] = S.do("vector", (lambda tau=tau, half=half, ybank=ybank: (lambda e: e.tensor_add(
                            y_acc[:, tau * 1024 + half * 512:tau * 1024 + (half + 1) * 512],
                            y_acc[:, tau * 1024 + half * 512:tau * 1024 + (half + 1) * 512],
                            pb[ybank][:, :])))(),
                            waits=[("pe", S.c("pe"))], inc="dv")
            hm_free[tb % 2] = S.c("pe")
            if tb == nB - 1:
                ring_last[s2] = S.c("pe")

        if nB == 1:
            w1_block(0)
            if ee < E - 1:
                wgp[("w1", ee + 1)] = load_weight(w_1[ee + 1], ring_c((2 * ee + 2) % 3), ring_last[(2 * ee + 2) % 3])
                wgp[("w2", ee + 1)] = load_weight(w_2[ee + 1], ring_c((2 * ee + 3) % 3), ring_last[(2 * ee + 3) % 3])
            w2_block(0)
        else:
            w1_block(0)
            if ee < E - 1:
                wgp[("w1", ee + 1)] = load_weight(w_1[ee + 1], ring_c((2 * ee + 2) % 3), ring_last[(2 * ee + 2) % 3])
            w1_block(1)
            w2_block(0)
            for tb in range(2, nB):
                w1_block(tb)
                if tb == nB - 1 and ee < E - 1:
                    wgp[("w2", ee + 1)] = load_weight(w_2[ee + 1], ring_c((2 * ee + 3) % 3), ring_last[(2 * ee + 3) % 3])
                w2_block(tb - 1)
            w2_block(nB - 1)
    FFN_DV = S.c("dv")

    odc = 0
    for tau in range(nT):
        zslot = tau % 2
        wz = [("dv", FFN_DV)]
        if tau >= 2:
            wz.append(("od", 16 * (tau - 1)))
        S.do("vector", (lambda tau=tau, zslot=zslot: (lambda e: e.tensor_add(
            OUTS[zslot], x_tm[:, tau * 1024:(tau + 1) * 1024],
            y_acc[:, tau * 1024:(tau + 1) * 1024])))(),
            waits=wz, inc="dv", drain=True)
        wln = []
        if tau >= 1:
            wln.append(("ac", ln_dv[("ln2sq", tau - 1)]))
        S.do("vector", (lambda zslot=zslot: (lambda e: e.bn_stats(bn6[:, 0:6], OUTS[zslot][:, 0:512])))(),
             waits=wln, inc=None, drain=True)
        S.do("vector", (lambda zslot=zslot: (lambda e: e.bn_stats(bn6[:, 6:12], OUTS[zslot][:, 512:1024])))(),
             waits=[], inc=None, drain=True)
        S.do("vector", lambda e: e.bn_aggr(agg2[:], bn6[:]), waits=[], inc="dv", drain=True)
        ln_dv[("ln2sq", tau)] = S.do("scalar", lambda e: e.activation(
            rstd[:], agg2[:, 1:2], AF.Sqrt, bias=lneps[:], scale=1.0),
            waits=[("dv", S.c("dv"))], inc="ac")
        S.do("vector", lambda e: e.reciprocal(rstd[:], rstd[:]),
             waits=[("ac", ln_dv[("ln2sq", tau)])], inc=None, drain=True)
        S.do("vector", lambda e: e.tensor_scalar(nbv[:], agg2[:, 0:1], rstd[:, 0:1], -1.0, ALU.mult, ALU.mult),
             waits=[], inc=None, drain=True)
        S.do("vector", (lambda zslot=zslot: (lambda e: e.tensor_scalar(
            OUTS[zslot], OUTS[zslot], rstd[:, 0:1], nbv[:, 0:1], ALU.mult, ALU.add)))(),
            waits=[], inc="dv", drain=True)
        S.do("sync", (lambda tau=tau, zslot=zslot: (lambda e: e.dma_start(
            out=out[tau * 128:(tau + 1) * 128, :], in_=OUTS[zslot])))(),
            waits=[("dv", S.c("dv"))], inc="od", incn=16)
        odc = S.c("od")
    S.do("sync", lambda e: e.wait_ge(sems["od"], odc), waits=[])

    with nc.Block() as blk:
        for engname in ENGS:
            steps = S.q[engname]
            def body(eng, steps=steps):
                for fn, waits, inc, incn, drain in steps:
                    for s, v in waits:
                        eng.wait_ge(sems[s], v)
                    if drain:
                        eng.drain()
                    ins = fn(eng)
                    if inc is not None:
                        assert ins is not None
                        ins.then_inc(sems[inc], incn)
            getattr(blk, engname)(body)

    st.close()
    return nc


MODE = "fused"


def kernel(L0_attn_x, L0_attn_Wq, L0_attn_Wk, L0_attn_Wv, L0_attn_Wo, L0_attn_bo,
           L0_attn_proj, L0_ffn_Wg, L0_ffn_bg, L0_ffn_W1, L0_ffn_b1, L0_ffn_W2, L0_ffn_b2):
    T = 2048
    groups = [[0, 1, 2, 3], [4, 5, 6, 7]]
    f = lambda a: np.ascontiguousarray(np.asarray(a, dtype=np.float32))
    x = f(L0_attn_x)
    shared = {
        "w_q": f(L0_attn_Wq), "w_k": f(L0_attn_Wk), "w_v": f(L0_attn_Wv),
        "w_o": f(L0_attn_Wo), "b_o": f(L0_attn_bo), "prj": f(L0_attn_proj),
        "w_g": f(L0_ffn_Wg), "b_g": f(L0_ffn_bg), "w_1": f(L0_ffn_W1),
        "b_1": f(L0_ffn_b1), "w_2": f(L0_ffn_W2), "b_2": f(L0_ffn_b2),
    }
    in_maps = []
    for core in range(8):
        b, t0 = core // 4, (core % 4) * T
        m = dict(shared)
        m["x_in"] = np.ascontiguousarray(x[b, t0:t0 + T, :])
        in_maps.append(m)
    if MODE == "fused":
        nc = build(T, groups, part="fused")
        for core in range(8):
            g0, i = (core // 4) * 4, core % 4
            far = g0 + (i + 2) % 4
            in_maps[core]["x_far"] = np.ascontiguousarray(x[far // 4, (far % 4) * T:(far % 4) * T + T, :])
            in_maps[core]["slot_in"] = np.full((1, 16), core % 2, np.int32)
        res = run_bass_kernel_spmd(nc, in_maps, core_ids=list(range(8)))
        outs = [res.results[c]["out"] for c in range(8)]
    elif MODE == "full":
        nc = build(T, groups, part="full")
        res = run_bass_kernel_spmd(nc, in_maps, core_ids=list(range(8)))
        outs = [res.results[c]["out"] for c in range(8)]
    else:
        nca = build(T, groups, part="a")
        ra = run_bass_kernel_spmd(nca, in_maps, core_ids=list(range(8)))
        c_parts = [ra.results[c]["c_out"] for c in range(8)]
        s_parts = [ra.results[c]["s_out"] for c in range(8)]
        ncb = build(T, groups, part="b")
        in_maps_b = []
        for core in range(8):
            g0 = (core // 4) * 4
            m = dict(in_maps[core])
            m["c_in"] = np.ascontiguousarray(np.stack(c_parts[g0:g0 + 4], axis=0))
            m["s_in"] = np.ascontiguousarray(
                np.concatenate([s_parts[g0 + i].reshape(-1) for i in range(4)]).reshape(1, 64))
            in_maps_b.append(m)
        rb = run_bass_kernel_spmd(ncb, in_maps_b, core_ids=list(range(8)))
        outs = [rb.results[c]["out"] for c in range(8)]
    return np.stack([np.concatenate(outs[0:4], axis=0),
                     np.concatenate(outs[4:8], axis=0)], axis=0)


# version 23
# speedup vs baseline: 1.0001x; 1.0001x over previous
import numpy as np
from contextlib import ExitStack

import concourse.bass as bass
import concourse.mybir as mybir
from concourse.bass_utils import run_bass_kernel_spmd

F32 = mybir.dt.float32
BF16 = mybir.dt.bfloat16
AX = mybir.AxisListType.X
AF = mybir.ActivationFunctionType
ALU = mybir.AluOpType

D = 1024
NCH = 8
H = 16
MF = 266
MCS = [(0, 128), (128, 256), (256, 266)]
EPS = 1e-4
LNEPS = 1e-5
DN = float(64) ** -0.25
E = 8

ENGS = ("sync", "tensor", "vector", "scalar", "gpsimd")


class Sched:
    def __init__(self):
        self.q = {e: [] for e in ENGS}
        self.n = {}
        self.lastwait = {}

    def c(self, sem):
        return self.n.get(sem, 0)

    def bump(self, sem, n):
        self.n[sem] = self.c(sem) + n

    DMASEMS = ("xd", "wd0", "wd1", "wg0", "wg1", "od", "gd", "cc")

    def do(self, eng, fn, waits=(), inc=None, incn=1, drain=False):
        waits = list(waits)
        if inc in self.DMASEMS:
            waits.append((inc, self.c(inc)))
        w = []
        for s, v in waits:
            if v <= 0:
                continue
            if s in self.DMASEMS:
                v = self.c(s)
            assert v <= self.c(s), f"forward wait {s} {v} > {self.c(s)}"
            key = (eng, s)
            if self.lastwait.get(key, -1) < v:
                w.append((s, v))
                self.lastwait[key] = v
        self.q[eng].append((fn, tuple(w), inc, incn, drain))
        if inc is not None:
            self.n[inc] = self.c(inc) + incn
        return self.c(inc) if inc is not None else None


def build(T, groups, part="full", stop_after=None):
    nT = T // 128
    nB = T // 512
    nc = bass.Bass("TRN2", target_bir_lowering=False,
                   detect_race_conditions=(part != "fused"))

    x_in = nc.dram_tensor("x_in", [T, D], F32, kind="ExternalInput").ap()
    w_q = nc.dram_tensor("w_q", [D, D], F32, kind="ExternalInput").ap()
    w_k = nc.dram_tensor("w_k", [D, D], F32, kind="ExternalInput").ap()
    w_v = nc.dram_tensor("w_v", [D, D], F32, kind="ExternalInput").ap()
    w_o = nc.dram_tensor("w_o", [D, D], F32, kind="ExternalInput").ap()
    b_o = nc.dram_tensor("b_o", [D], F32, kind="ExternalInput").ap()
    prj = nc.dram_tensor("prj", [MF, 64], F32, kind="ExternalInput").ap()
    w_g = nc.dram_tensor("w_g", [D, E], F32, kind="ExternalInput").ap()
    b_g = nc.dram_tensor("b_g", [E], F32, kind="ExternalInput").ap()
    w_1 = nc.dram_tensor("w_1", [E, D, D], F32, kind="ExternalInput").ap()
    b_1 = nc.dram_tensor("b_1", [E, D], F32, kind="ExternalInput").ap()
    w_2 = nc.dram_tensor("w_2", [E, D, D], F32, kind="ExternalInput").ap()
    b_2 = nc.dram_tensor("b_2", [E, D], F32, kind="ExternalInput").ap()
    out = None
    if part != "a":
        out = nc.dram_tensor("out", [T, D], F32, kind="ExternalOutput").ap()
    c_out = s_out = c_in = s_in = None
    if part == "a":
        c_out = nc.dram_tensor("c_out", [65, H * 267], F32, kind="ExternalOutput").ap()
        s_out = nc.dram_tensor("s_out", [16, 1], F32, kind="ExternalOutput").ap()
    G = len(groups[0])
    PAIRS = max(1, len(groups[0]) // 2)
    NSL = 2 * PAIRS
    if part == "b":
        c_in = nc.dram_tensor("c_in", [G, 65, H * 267], F32, kind="ExternalInput").ap()
        s_in = nc.dram_tensor("s_in", [1, G * 16], F32, kind="ExternalInput").ap()
    x_far = slot_in = d_cslab = d_stab = d_flag = None
    if part == "fused":
        if PAIRS == 2:
            x_far = nc.dram_tensor("x_far", [T, D], F32, kind="ExternalInput").ap()
        slot_in = nc.dram_tensor("slot_in", [1, 16], mybir.dt.int32, kind="ExternalInput").ap()
        d_cslab = nc.dram_tensor("d_cslab", [NSL, 65, H * 267], F32, kind="Internal", addr_space="Shared").ap()
        d_stab = nc.dram_tensor("d_stab", [NSL, 16], F32, kind="Internal", addr_space="Shared").ap()
        d_flag = nc.dram_tensor("d_flag", [2, 16], mybir.dt.int32, kind="Internal", addr_space="Shared").ap()

    dr_diag = nc.dram_tensor("dr_diag", [H, T], F32, kind="Internal").ap()
    dr_rmax = nc.dram_tensor("dr_rmax", [H, T], F32, kind="Internal").ap()
    dr_ci = nc.dram_tensor("dr_ci", [65, H * 267], F32, kind="Internal").ap()
    dr_co = nc.dram_tensor("dr_co", [65, H * 267], F32, kind="Internal").ap()
    dr_si = nc.dram_tensor("dr_si", [1, H], F32, kind="Internal").ap()
    dr_so = nc.dram_tensor("dr_so", [1, H], F32, kind="Internal").ap()

    st = ExitStack()
    A = st.enter_context

    x_tm = A(nc.sbuf_tensor("x_tm", [128, nT * 1024], BF16))
    B_fm = A(nc.sbuf_tensor("B_fm", [128, NCH * T], BF16))
    C_big = A(nc.sbuf_tensor("C_big", [128, max(NCH * T + nT * 16 * 65, 32768)], BF16))
    D_du = A(nc.sbuf_tensor("D_du", [128, max(nT * 1024, 16384)], BF16))
    scr = A(nc.sbuf_tensor("scr", [128, 7168], F32))
    stag_w = A(nc.sbuf_tensor("stag_w", [128, 2048], F32))

    ident_b = A(nc.sbuf_tensor("ident_b", [128, 128], BF16))
    ident_f = A(nc.sbuf_tensor("ident_f", [128, 128], F32))
    ones_b = A(nc.sbuf_tensor("ones_b", [128, 128], BF16))
    onesf = A(nc.sbuf_tensor("onesf", [1, 128], F32))
    epsf = A(nc.sbuf_tensor("epsf", [1, 128], F32))
    projdup = A(nc.sbuf_tensor("projdup", [128, MF], BF16))
    projex = A(nc.sbuf_tensor("projex", [65, MF], BF16))
    projcb = A(nc.sbuf_tensor("projcb", [128, 64], BF16))
    diagTk = A(nc.sbuf_tensor("diagTk", [128, H * 16], F32))
    estab = A(nc.sbuf_tensor("estab", [128, H], F32))
    kp_sc = A(nc.sbuf_tensor("kp_sc", [128, 4 * 267], BF16))
    epsv = A(nc.sbuf_tensor("epsv", [128, 65], F32))
    vsn = A(nc.sbuf_tensor("vsn", [1, 65], F32))
    stabloc = A(nc.sbuf_tensor("stabloc", [16, 1], F32))
    stabrow = A(nc.sbuf_tensor("stabrow", [1, H], F32))
    stab4 = A(nc.sbuf_tensor("stab4", [1, 64], F32))
    slotv = A(nc.sbuf_tensor("slotv", [1, 16], mybir.dt.int32))
    magict = A(nc.sbuf_tensor("magict", [1, 16], mybir.dt.int32))
    flgt = A(nc.sbuf_tensor("flgt", [1, 32], mybir.dt.int32))
    maxk = A(nc.sbuf_tensor("maxk", [128, H], F32))
    rmaxsb = A(nc.sbuf_tensor("rmaxsb", [128, 32], F32))
    lnin = A(nc.sbuf_tensor("lnin", [128, 1], F32))
    lnout16 = A(nc.sbuf_tensor("lnout16", [128, 16], F32))
    bn6 = A(nc.sbuf_tensor("bn6", [128, 12], F32))
    agg2 = A(nc.sbuf_tensor("agg2", [128, 2], F32))
    rstd = A(nc.sbuf_tensor("rstd", [128, 1], F32))
    nbv = A(nc.sbuf_tensor("nbv", [128, 1], F32))
    lneps = A(nc.sbuf_tensor("lneps", [128, 1], F32))
    epsc = A(nc.sbuf_tensor("epsc", [128, 1], F32))
    b1T = A(nc.sbuf_tensor("b1T", [128, 64], F32))
    wgb = A(nc.sbuf_tensor("wgb", [128, 64], BF16))
    bgb = A(nc.sbuf_tensor("bgb", [1, 8], BF16))
    sel8 = A(nc.sbuf_tensor("sel8", [8, 8 * 128], BF16))
    oh_f = A(nc.sbuf_tensor("oh_f", [128, 8], F32))
    ohmax = A(nc.sbuf_tensor("ohmax", [128, 1], F32))

    pb = [A(nc.psum_tensor(f"pb{i}", [128, 512], F32)) for i in range(8)]
    pbb = [p[:, :].bitcast(BF16) for p in pb]

    sems = {}
    for s in ("xd", "wd0", "wd1", "wg0", "wg1", "od", "gd", "cc", "pe", "ac", "dv", "gp", "pd"):
        sems[s] = A(nc.semaphore(s))

    nc.const_aps.register(F32, EPS, epsc[:, 0:1])

    S = Sched()
    MAGIC_BITS = 0x5CA1AB1E
    rslot_sync = rslot_gp = None
    if part == "fused":
        rslot_sync = nc.sync.alloc_register("rslot_s")
        rslot_gp = nc.gpsimd.alloc_register("rslot_g")
        S.do("sync", lambda e: e.dma_start(out=slotv[:], in_=slot_in), waits=[], inc="xd", incn=16)
        S.do("sync", (lambda: (lambda e: e.reg_load(rslot_sync, slotv[0:1, 0:1])))(),
             waits=[("xd", S.c("xd"))], inc=None)
        S.do("gpsimd", (lambda: (lambda e: e.reg_load(rslot_gp, slotv[0:1, 0:1])))(),
             waits=[("xd", S.c("xd"))], inc="gp")

    SQF = [scr[:, 0:512], scr[:, 512:1024]]
    SQB = [scr[:, 0:256].bitcast(BF16),
           scr[:, 512:768].bitcast(BF16)]
    DROW = [scr[0:1, 1024:1024 + T], scr[64:65, 1024:1024 + T]]
    B16RT = scr[0:16, 1024:3072]
    RM64S = [scr[64:65, 1024:1024 + T], scr[64:65, 3072:3072 + T]]
    NSB = [scr[0:64, 1024:1536], scr[0:64, 1536:2048], scr[0:64, 2048:2560], scr[0:64, 2560:3072]]
    PROJSTG = scr[0:128, 1024:1088]
    cstg = scr[0:65, 3072:3606]
    cstg4 = scr[0:65, 1024:1024 + 4 * 267]
    cstg4s = [scr[0:65, 1024:1024 + 4 * 267], scr[0:65, 3606:3606 + 4 * 267]]
    ZF = [scr[:, 0:1024], scr[:, 1024:2048]]
    OUTS = [scr[:, 2048:3072], scr[:, 3072:4096]]
    BOBC = scr[:, 4096:5120]
    DINVB = [scr[64:65, 6144 + i * 256:6144 + (i + 1) * 256].bitcast(BF16) for i in (0, 1)]
    NUMSC = [scr[0:64, 4608 + i * 256:4608 + (i + 1) * 256].bitcast(BF16) for i in (0, 1)]
    BOSB = scr[0:1, 5120:6144]
    LNR = [scr[64:65, 5120 + i * 512:5120 + (i + 1) * 512] for i in (0, 1)]
    B2B = scr[0:8, 5120:5632].bitcast(BF16)
    BSTG = scr[0:8, 6144:7168]
    hg_b = scr[:, 6656:6912].bitcast(BF16)
    ohT = scr[0:8, 2048:2048 + T // 2].bitcast(BF16)
    WGSTG = scr[0:128, 6144:6208]
    BGSTG = scr[0:1, 6208:6216]


    k_fm = C_big[:, 0:NCH * T]
    k_fm3 = C_big[:, 0:NCH * T].rearrange("p (c t) -> p c t", c=NCH)
    VOFF = NCH * T
    q_ex = C_big[0:65, 0:16 * T]
    HM = [C_big[:, 24576:28672], C_big[:, 28672:32768]]
    def ring_c(s):
        return C_big[:, s * 8192:(s + 1) * 8192]

    attn_ring = [D_du[:, 0:8192], D_du[:, 8192:16384]]
    QP = D_du[:, 0:3072]
    CAUG = D_du[:, 8192:11392]
    y_acc = D_du
    B3 = B_fm.rearrange("p (c t) -> p c t", c=NCH)

    def init_g(g):
        g.memset(ident_b[:], 0.0)
        g.memset(ident_f[:], 0.0)
        g.memset(ones_b[:], 1.0)
        g.memset(onesf[:], 1.0)
        g.memset(epsf[:], EPS)
        g.memset(lneps[:], LNEPS)
        g.memset(epsc[:], EPS)
        g.memset(maxk[:], -3e38)
        g.memset(sel8[:], 0.0)
        g.memset(projex[64:65, :], 1.0)
        g.memset(kp_sc[:].rearrange("p (s c) -> p s c", c=267)[:, :, 266:267], 1.0)
        g.memset(C_big[:, VOFF:VOFF + nT * 16 * 65].rearrange("p (a c) -> p a c", c=65)[:, :, 64:65], 1.0)
        g.drain()
        g.affine_select(out=ident_b[:], in_=ident_b[:], compare_op=ALU.not_equal,
                        fill=1.0, base=0, pattern=[[-1, 128]], channel_multiplier=1)
        g.affine_select(out=ident_f[:], in_=ident_f[:], compare_op=ALU.not_equal,
                        fill=1.0, base=0, pattern=[[-1, 128]], channel_multiplier=1)
        ins = g.affine_select(out=sel8[:].rearrange("p (e n) -> p e n", e=8),
                              in_=sel8[:].rearrange("p (e n) -> p e n", e=8),
                              compare_op=ALU.not_equal, fill=1.0, base=0,
                              pattern=[[-1, 8], [0, 128]], channel_multiplier=1)
        g.drain()
        return ins
    S.do("gpsimd", init_g, inc="gp")
    GP_INIT = S.c("gp")


    def _probe_finish():
        S.do("sync", lambda e: e.dma_start(out=s_out, in_=stabloc[:]),
             waits=[("gp", GP_INIT)], inc="od", incn=16)
        S.do("sync", lambda e: e.wait_ge(sems["od"], S.c("od")), waits=[])
        with nc.Block() as blk:
            for engname in ENGS:
                steps = S.q[engname]
                def body(eng, steps=steps):
                    for fn, waits, inc, incn, drain in steps:
                        for sname, v in waits:
                            eng.wait_ge(sems[sname], v)
                        if drain:
                            eng.drain()
                        ins = fn(eng)
                        if inc is not None:
                            ins.then_inc(sems[inc], incn)
                getattr(blk, engname)(body)
        st.close()
        return nc

    if stop_after == "init":
        return _probe_finish()

    for pi, (r0, r1) in enumerate([(0, 128), (128, 256), (256, 266)]):
        rows = r1 - r0
        S.do("sync", (lambda r0=r0, rows=rows: (lambda e: e.dma_start(
            out=PROJSTG[0:rows, :], in_=prj[r0:r0 + rows, :])))(),
            waits=[("gp", S.c("gp"))], inc="xd", incn=16)
        S.do("gpsimd", (lambda rows=rows: (lambda e: e.tensor_copy(projcb[0:rows, :], PROJSTG[0:rows, :])))(),
             waits=[("xd", S.c("xd")), ("pe", S.c("pe"))], inc="gp", drain=True)
        S.do("tensor", (lambda rows=rows: (lambda e: e.transpose(
            pbb[6][0:64, 0:rows], projcb[0:rows, :], ident_b[0:rows, 0:rows])))(),
            waits=[("gp", S.c("gp")), ("dv", S.c("dv"))], inc="pe")
        S.do("vector", (lambda r0=r0, rows=rows: (lambda e: e.tensor_copy(
            projdup[0:64, r0:r0 + rows], pbb[6][0:64, 0:rows])))(),
            waits=[("pe", S.c("pe"))], inc="dv")
    S.do("vector", lambda e: e.tensor_copy(projex[0:64, :], projdup[0:64, :]),
         waits=[("dv", S.c("dv"))], inc="dv", drain=True)
    S.do("gpsimd", lambda e: e.dma_start(out=projdup[64:128, :], in_=projdup[0:64, :]),
         waits=[("dv", S.c("dv"))], inc="gd", incn=16)
    GD_PROJ = S.c("gd")
    DV_PROJ = S.c("dv")

    def emit_xload(x_src, extra_pe=0, base_ac=None, split=False):
        xcast = {}
        xtp_dv = {}
        xdsem = {}
        for tau in range(nT):
            if tau % 2 == 0:
                w = [("ac", xcast[tau - 3])] if tau >= 4 else [("ac", S.c("ac") if base_ac is None else base_ac)]
                q, sem = (("gpsimd", "wg0") if (split and (tau // 2) % 2 == 1) else ("sync", "xd"))
                S.do(q, (lambda tau=tau, x_src=x_src: (lambda e: e.dma_start(
                    out=scr[:, 2048 + (tau % 4) * 1024:2048 + (tau % 4) * 1024 + 2048]
                    .rearrange("p (a d) -> p a d", a=2),
                    in_=x_src.rearrange("(q p) d -> p q d", p=128)[:, tau:tau + 2, :])))(),
                    waits=w, inc=sem, incn=16)
                xdsem[tau] = xdsem[tau + 1] = (sem, S.c(sem))
            wc = [xdsem[tau]]
            if tau == 0 and extra_pe:
                wc.append(("pe", extra_pe))
            xcast[tau] = S.do("scalar", (lambda tau=tau: (lambda e: e.copy(
                x_tm[:, tau * 1024:(tau + 1) * 1024],
                scr[:, 2048 + (tau % 4) * 1024:2048 + (tau % 4) * 1024 + 1024])))(),
                waits=wc, inc="ac")
            bank = pbb[6 + tau % 2]
            wtp = [("ac", xcast[tau]), ("gp", GP_INIT), ("dv", DV_PROJ)]
            if tau < 2 and extra_pe:
                wtp.append(("pe", extra_pe))
                wtp.append(("dv", S.c("dv")))
            if tau >= 2:
                wtp.append(("dv", xtp_dv[tau - 2]))
            for c in range(NCH):
                S.do("tensor", (lambda tau=tau, c=c, bank=bank: (lambda e: e.transpose(
                    bank[:, c * 128:(c + 1) * 128],
                    x_tm[:, tau * 1024 + c * 128:tau * 1024 + (c + 1) * 128], ident_b[:])))(),
                    waits=wtp if c == 0 else [], inc="pe" if c == NCH - 1 else None)
            xtp_dv[tau] = S.do("vector", (lambda tau=tau, bank=bank: (lambda e: e.tensor_copy(
                B3[:, :, tau * 128:(tau + 1) * 128],
                bank[:, 0:1024].rearrange("p (c n) -> p c n", c=NCH))))(),
                waits=[("pe", S.c("pe"))], inc="dv")
        return S.c("dv")

    XFM_DV = 0
    if part == "b":
        XFM_DV = emit_xload(x_in)
    if stop_after == "xfm":
        XFM_DV = emit_xload(x_in)
        return _probe_finish()

    castc = [("gp", 0), ("gp", 0)]
    def load_weight(hbm_ap, dest, free_pe, queue="sync", cast="gpsimd", pieces=None):
        src = hbm_ap.rearrange("(c p) d -> p c d", p=128)
        csem = "gp" if cast == "gpsimd" else "ac"
        done = None
        for idx, piece in enumerate(range(8) if pieces is None else pieces):
            slot = idx % 2
            wsem = "wd0" if slot == 0 else "wd1"
            w = [castc[slot]]
            if idx < 2 and free_pe:
                w.append(("pe", free_pe))
            S.do(queue, (lambda piece=piece, slot=slot, src=src: (lambda e: e.dma_start(
                out=stag_w[:, slot * 1024:(slot + 1) * 1024],
                in_=src[:, piece, :])))(),
                waits=w, inc=wsem, incn=16)
            wcast = [(wsem, S.c(wsem))]
            if idx < 2 and free_pe:
                wcast.append(("pe", free_pe))
            if cast == "gpsimd":
                cnt = S.do("gpsimd", (lambda piece=piece, slot=slot, dest=dest: (lambda e: e.tensor_copy(
                    dest[:, piece * 1024:(piece + 1) * 1024], stag_w[:, slot * 1024:(slot + 1) * 1024])))(),
                    waits=wcast, inc="gp")
            else:
                cnt = S.do("scalar", (lambda piece=piece, slot=slot, dest=dest: (lambda e: e.copy(
                    dest[:, piece * 1024:(piece + 1) * 1024], stag_w[:, slot * 1024:(slot + 1) * 1024])))(),
                    waits=wcast, inc="ac", drain=True)
            done = (csem, cnt)
            castc[slot] = done
        return done

    K_PROJ_PE = 0
    DIAGT_AC = 0
    GD_B16 = 0
    gp_wk = gp_wv = ("gp", 0)
    if part in ("a", "full"):
        kpasses = ["own"]
    elif part == "fused":
        kpasses = (["far"] if PAIRS == 2 else []) + ["own"]
    else:
        kpasses = []
    for kpass in kpasses:
      ctx_dv = {}
      cdma = {}
      if kpass == kpasses[0]:
          AC0 = S.c("ac")
          gp_wk = load_weight(w_k, attn_ring[0], 0, cast="scalar")
          XFM_DV = emit_xload(x_far if kpass == "far" else x_in, base_ac=AC0, split=True)
          gp_wv = load_weight(w_v, attn_ring[1], 0)
      else:
          XFM_DV = emit_xload(x_in, extra_pe=S.c("pe"))
      if kpass != kpasses[0]:
          S.do("gpsimd", lambda e: e.memset(maxk[:], -3e38),
               waits=[("dv", S.c("dv"))], inc="gp", drain=True)
      kev = {}
      kdiag_ac = {}
      drow_dma = {}
      for c in range(NCH):
          for tb in range(nB):
              u = c * nB + tb
              bank = u % 4
              wmm = [gp_wk, ("dv", XFM_DV)]
              if u < 4:
                  wmm.append(("ac", S.c("ac")))
                  wmm.append(("dv", S.c("dv")))
              if u >= 4:
                  wmm.append(("ac", kev[u - 4]))
              for kc in range(NCH):
                  S.do("tensor", (lambda c=c, tb=tb, kc=kc, bank=bank: (lambda e: e.matmul(
                      pb[bank][:, :], attn_ring[0][:, kc * 1024 + c * 128:kc * 1024 + (c + 1) * 128],
                      B_fm[:, kc * T + tb * 512:kc * T + (tb + 1) * 512],
                      start=(kc == 0), stop=(kc == NCH - 1))))(),
                      waits=wmm if kc == 0 else [], inc="pe" if kc == NCH - 1 else None)
              pemm = S.c("pe")
              S.do("vector", (lambda c=c, tb=tb, bank=bank: (lambda e: e.tensor_scalar_mul(
                  k_fm3[:, c, tb * 512:(tb + 1) * 512], pb[bank][:, :], DN)))(),
                  waits=[("pe", pemm)], inc="dv", drain=True)
              KCP_DV = S.c("dv")
              S.do("scalar", (lambda bank=bank, u=u: (lambda e: e.activation(
                  SQB[u % 2], pb[bank][:, :], AF.Square, bias=0.0, scale=DN)))(),
                  waits=[("dv", KCP_DV)], inc="ac")
              kev[u] = S.c("ac")
              for hh in range(2):
                  h = 2 * c + hh
                  dbank = 4 + (h * nB + tb) % 2
                  wdg = [("ac", kev[u])]
                  pk = h * nB + tb - 4
                  if pk in kdiag_ac:
                      wdg.append(("ac", kdiag_ac[pk]))
                  tp = (64, 64) if hh == 1 else None
                  S.do("tensor", (lambda hh=hh, u=u, dbank=dbank, tp=tp: (lambda e: e.matmul(
                      pb[dbank][64:65, :] if tp else pb[dbank][0:1, :],
                      ones_b[hh * 64:hh * 64 + 64, 0:1], SQB[u % 2][hh * 64:hh * 64 + 64, :],
                      start=True, stop=True, tile_position=tp)))(),
                      waits=wdg, inc="pe")
                  wdr = [("pe", S.c("pe"))]
                  if tb == 0 and c >= 1:
                      wdr.append(("gd", drow_dma[2 * (c - 1) + hh]))
                  kdiag_ac[h * nB + tb] = S.do("scalar",
                      (lambda hh=hh, tb=tb, dbank=dbank: (lambda e: e.activation(
                          DROW[hh][:, tb * 512:(tb + 1) * 512],
                          pb[dbank][64:65, :] if hh == 1 else pb[dbank][0:1, :],
                          AF.Copy, bias=0.0, scale=-0.5)))(),
                      waits=wdr, inc="ac")
          for hh in range(2):
              h = 2 * c + hh
              drow_dma[h] = S.do("gpsimd", (lambda h=h, hh=hh: (lambda e: e.dma_start(
                  out=dr_diag[h:h + 1, :], in_=DROW[hh])))(),
                  waits=[("ac", kdiag_ac[h * nB + nB - 1])], inc="gd", incn=16)
      K_PROJ_PE = S.c("pe")
      if stop_after == "kproj":
          return _probe_finish()

      vev = {}
      for tau in range(nT):
          for half in range(2):
              u = tau * 2 + half
              bank = u % 4
              wmm = [gp_wv, ("dv", XFM_DV)]
              if u < 4:
                  wmm.append(("ac", S.c("ac")))
                  wmm.append(("dv", S.c("dv")))
              if u >= 4:
                  wmm.append(("dv", vev[u - 4]))
              for kc in range(NCH):
                  S.do("tensor", (lambda tau=tau, half=half, kc=kc, bank=bank: (lambda e: e.matmul(
                      pb[bank][:, :], B_fm[:, kc * T + tau * 128:kc * T + (tau + 1) * 128],
                      attn_ring[1][:, kc * 1024 + half * 512:kc * 1024 + (half + 1) * 512],
                      start=(kc == 0), stop=(kc == NCH - 1))))(),
                      waits=wmm if kc == 0 else [], inc="pe" if kc == NCH - 1 else None)
              vev[u] = S.do("vector", (lambda tau=tau, half=half, bank=bank: (lambda e: e.tensor_copy(
                  C_big[:, VOFF + tau * 1040 + half * 520:VOFF + tau * 1040 + half * 520 + 520]
                  .rearrange("p (h d) -> p h d", h=8)[:, :, 0:64],
                  pb[bank][:, :].rearrange("p (h d) -> p h d", h=8))))(),
                  waits=[("pe", S.c("pe"))], inc="dv")
      V_DV = S.c("dv")
      if stop_after == "vproj":
          return _probe_finish()

      S.do("gpsimd", lambda e: e.dma_start(
          out=B16RT[0:nT, 0:H * 128].rearrange("q (h p) -> q h p", h=H),
          in_=dr_diag.rearrange("h (q p) -> q h p", p=128)),
          waits=[("gd", max(drow_dma.values()))], inc="gd", incn=16)
      GD_B16 = S.c("gd")
      dgt_ac = {}
      for h in range(H):
          tb6 = 6 + h % 2
          wtp = [("gd", GD_B16)]
          if h <= 1:
              wtp.append(("dv", S.c("dv")))
          if h >= 2:
              wtp.append(("ac", dgt_ac[h - 2]))
          S.do("tensor", (lambda h=h, tb6=tb6: (lambda e: e.transpose(
              pb[tb6][0:128, 0:nT], B16RT[0:nT, h * 128:(h + 1) * 128],
              ident_f[0:nT, 0:nT])))(),
              waits=wtp, inc="pe")
          dgt_ac[h] = S.do("scalar", (lambda h=h, tb6=tb6: (lambda e: e.copy(
              diagTk[:, h * 16:h * 16 + nT], pb[tb6][0:128, 0:nT])))(),
              waits=[("pe", S.c("pe"))], inc="ac")
      DIAGT_AC = S.c("ac")
      if stop_after == "diag":
          return _probe_finish()


      ctx_dv = {}
      cdma = {}
      kexp = {}
      ctx_pe = {}
      rmaxdv = {}
      LAG = 4
      NU = H * nT

      def kp_head(u):
          return u // nT, u % nT

      for u in range(NU + LAG):
          uc = u - LAG
          if uc >= 0:
              h, tau = kp_head(uc)
              cbank = 4 + h % 2
              wctx = [("ac", kexp[uc])]
              if tau == 0 and h >= 2:
                  wctx.append(("dv", ctx_dv[h - 2]))
              ctx_pe[uc] = S.do("tensor", (lambda h=h, tau=tau, cbank=cbank: (lambda e: e.matmul(
                  pb[cbank][0:65, 0:267],
                  C_big[:, VOFF + tau * 1040 + h * 65:VOFF + tau * 1040 + h * 65 + 65],
                  kp_sc[:, (tau % 4) * 267:(tau % 4) * 267 + 267],
                  start=(tau == 0), stop=(tau == nT - 1))))(),
                  waits=wctx, inc="pe")
              if tau == nT - 1:
                  S.do("scalar", (lambda h=h: (lambda e: e.activation(
                      lnout16[:, 0:nT], rmaxsb[:, (h % 2) * 16:(h % 2) * 16 + nT], AF.Ln, bias=0.0, scale=1.0)))(),
                      waits=[("dv", rmaxdv[uc])], inc="ac", drain=True)
                  S.do("vector", (lambda h=h: (lambda e: e.tensor_sub(
                      lnout16[:, 0:nT], lnout16[:, 0:nT], diagTk[:, h * 16:h * 16 + nT])))(),
                      waits=[("ac", S.c("ac"))], inc=None, drain=True)
                  S.do("vector", (lambda h=h: (lambda e: e.reduce_max(
                      lnin[:], lnout16[:, 0:nT], axis=AX)))(),
                      waits=[], inc=None, drain=True)
                  S.do("vector", (lambda h=h: (lambda e: e.tensor_max(
                      maxk[:, h:h + 1], maxk[:, h:h + 1], lnin[:])))(),
                      waits=[], inc="dv", drain=True)
                  wev = [("pe", S.c("pe"))]
                  if h >= 2:
                      wev.append(("gd" if part == "full" else "od", cdma[h - 2]))
                  ctx_dv[h] = S.do("vector", (lambda h=h, cbank=cbank: (lambda e: e.tensor_copy(
                      cstg[:, (h % 2) * 267:(h % 2) * 267 + 267], pb[cbank][0:65, 0:267])))(),
                      waits=wev, inc="dv")
                  if part == "fused":
                      ofs = 0 if kpass == "own" else 1
                      def slabfn(e, h=h, ofs=ofs, sl=(h % 2)):
                          sap = cstg[:, sl * 267:sl * 267 + 267]
                          with e.If_eq(rslot_sync, 0):
                              e.dma_start(out=d_cslab[0 * PAIRS + ofs][:, h * 267:(h + 1) * 267],
                                          in_=sap).then_inc(sems["od"], 16)
                          with e.Else():
                              e.dma_start(out=d_cslab[1 * PAIRS + ofs][:, h * 267:(h + 1) * 267],
                                          in_=sap).then_inc(sems["od"], 16)
                          e.end_ifs()
                          return None
                      S.do("sync", slabfn, waits=[("dv", ctx_dv[h]), ("od", S.c("od"))], inc=None)
                      S.bump("od", 16)
                      cdma[h] = S.c("od")
                  else:
                      cdst = dr_ci if part == "full" else c_out
                      csem = "gd" if part == "full" else "od"
                      cdma[h] = S.do("gpsimd" if part == "full" else "sync",
                          (lambda h=h, cdst=cdst: (lambda e: e.dma_start(
                          out=cdst[:, h * 267:(h + 1) * 267], in_=cstg[:, (h % 2) * 267:(h % 2) * 267 + 267])))(),
                          waits=[("dv", ctx_dv[h])], inc=csem, incn=16)
          if u < NU:
              h, tau = kp_head(u)
              hb = (h % 2) * 64
              hc = h // 2
              dbank = u % 4
              wmm = []
              if u < 4:
                  wmm = [("ac", DIAGT_AC), ("dv", V_DV), ("gd", GD_PROJ)]
              if u >= 4:
                  wmm.append(("ac", kexp[u - 4]))
              S.do("tensor", (lambda hb=hb, hc=hc, tau=tau, dbank=dbank: (lambda e: e.matmul(
                  pb[dbank][0:128, 0:266], k_fm3[hb:hb + 64, hc, tau * 128:(tau + 1) * 128],
                  projdup[hb:hb + 64, :], start=True, stop=True)))(),
                  waits=wmm, inc="pe")
              pedash = S.c("pe")
              wexp = [("pe", pedash)]
              if u >= 4:
                  wexp.append(("pe", ctx_pe.get(u - LAG, 0)))
                  wexp.append(("dv", rmaxdv[u - 4]))
              kexp[u] = S.do("scalar", (lambda h=h, tau=tau, dbank=dbank: (lambda e: e.activation(
                  kp_sc[:, (tau % 4) * 267:(tau % 4) * 267 + 266], pb[dbank][0:128, 0:266], AF.Exp,
                  bias=diagTk[:, h * 16 + tau:h * 16 + tau + 1], scale=1.0)))(),
                  waits=wexp, inc="ac")
              rmaxdv[u] = S.do("vector", (lambda h=h, tau=tau: (lambda e: e.reduce_max(
                  rmaxsb[:, (h % 2) * 16 + tau % 16:(h % 2) * 16 + tau % 16 + 1],
                  kp_sc[:, (tau % 4) * 267:(tau % 4) * 267 + 266], axis=AX)))(),
                  waits=[("ac", kexp[u])], inc="dv", drain=True)
      S.do("tensor", lambda e: e.transpose(pb[6][0:16, 0:128], maxk[:], ident_f[:]),
           waits=[("dv", S.c("dv")), ("ac", DIAGT_AC)], inc="pe")
      S.do("vector", lambda e: e.reduce_max(stabloc[:], pb[6][0:16, 0:128], axis=AX),
           waits=[("pe", S.c("pe"))], inc="dv", drain=True)
      if part == "fused":
          ofs = 0 if kpass == "own" else 1
          def stabfn(e, ofs=ofs):
              with e.If_eq(rslot_sync, 0):
                  e.dma_start(out=d_stab[0 * PAIRS + ofs][None, :].rearrange("o h -> (o h) o" if False else "o h -> h o"),
                              in_=stabloc[:]).then_inc(sems["od"], 16)
              with e.Else():
                  e.dma_start(out=d_stab[1 * PAIRS + ofs][None, :].rearrange("o h -> h o"),
                              in_=stabloc[:]).then_inc(sems["od"], 16)
              e.end_ifs()
              return None
          S.do("sync", stabfn, waits=[("dv", S.c("dv")), ("od", S.c("od"))], inc=None)
          S.bump("od", 16)

    if part != "a":
        gp_wq = load_weight(w_q, attn_ring[0], K_PROJ_PE)

    if part == "fused":
        S.do("gpsimd", lambda e: e.memset(magict[:], MAGIC_BITS),
             waits=[], inc="gp", drain=True)
        def flagfn(e):
            with e.If_eq(rslot_gp, 0):
                e.dma_start(out=d_flag[0], in_=magict[:]).then_inc(sems["gd"], 16)
            with e.Else():
                e.dma_start(out=d_flag[1], in_=magict[:]).then_inc(sems["gd"], 16)
            e.end_ifs()
            return None
        S.do("gpsimd", flagfn,
             waits=[("od", S.c("od")), ("gp", S.c("gp")), ("gd", S.c("gd"))], inc=None)
        S.bump("gd", 16)
        GD_FLAG = S.c("gd")
        def pollfn(e):
            with e.register("cnt") as rcnt, e.register("pdc") as rpd, e.register("fv") as rfv:
                e.reg_mov(rcnt, 0)
                e.reg_mov(rpd, 0)
                def cond():
                    return e.compute_val(rcnt) < 2
                with e.While(cond):
                    e.reg_add(rpd, rpd, 16)
                    e.dma_start(out=flgt[:], in_=d_flag.rearrange("a b -> (a b)")[None, :]).then_inc(sems["pd"], 16)
                    e.wait_ge(sems["pd"], rpd)
                    e.reg_mov(rcnt, 0)
                    for c in range(2):
                        e.reg_load(rfv, flgt[0:1, c * 16:c * 16 + 1])
                        with e.If_eq(rfv, MAGIC_BITS):
                            e.reg_add(rcnt, rcnt, 1)
                        e.end_ifs()
            return e.nop()
        S.do("gpsimd", pollfn, waits=[("gd", GD_FLAG)], inc="gp")
        BARRIER_GP = S.c("gp")

    if part == "a":
        S.do("sync", lambda e: e.dma_start(out=s_out, in_=stabloc[:]),
             waits=[("dv", S.c("dv"))], inc="od", incn=16)
        S.do("sync", lambda e: e.wait_ge(sems["od"], S.c("od")), waits=[("gd", S.c("gd"))])
        with nc.Block() as blk:
            for engname in ENGS:
                steps = S.q[engname]
                def body(eng, steps=steps):
                    for fn, waits, inc, incn, drain in steps:
                        for sname, v in waits:
                            eng.wait_ge(sems[sname], v)
                        if drain:
                            eng.drain()
                        ins = fn(eng)
                        if inc is not None:
                            assert ins is not None
                            ins.then_inc(sems[inc], incn)
                getattr(blk, engname)(body)
        st.close()
        return nc
    if part == "full":
        S.do("gpsimd", lambda e: e.dma_start(out=dr_si.rearrange("o h -> h o"), in_=stabloc[:]),
             waits=[("dv", S.c("dv"))], inc="gd", incn=16)
        S.do("gpsimd", lambda e: e.collective_compute("AllReduce", ALU.max, groups, [dr_si], [dr_so]),
             waits=[("gd", S.c("gd"))], inc="cc", incn=16)
        S.do("gpsimd", lambda e: e.collective_compute("AllReduce", ALU.add, groups, [dr_ci], [dr_co]),
             waits=[("gd", max(cdma.values())), ("cc", S.c("cc"))], inc="cc", incn=16)
        CC_DONE = S.c("cc")
        S.do("gpsimd", lambda e: e.dma_start(out=stabrow[:], in_=dr_so),
             waits=[("cc", CC_DONE)], inc="gd", incn=16)
    qev = {}
    qdiag_ac = {}
    for h in range(H):
        for tb in range(nB):
            u = h * nB + tb
            bank = u % 4
            wmm = [gp_wq, ("dv", XFM_DV)]
            if u < 4:
                wmm.append(("ac", S.c("ac")))
                wmm.append(("dv", S.c("dv")))
            if u >= 4:
                wmm.append(("ac", qev[u - 4]))
            for kc in range(NCH):
                S.do("tensor", (lambda h=h, tb=tb, kc=kc, bank=bank: (lambda e: e.matmul(
                    pb[bank][0:64, :], attn_ring[0][:, kc * 1024 + h * 64:kc * 1024 + h * 64 + 64],
                    B_fm[:, kc * T + tb * 512:kc * T + (tb + 1) * 512],
                    start=(kc == 0), stop=(kc == NCH - 1))))(),
                    waits=wmm if kc == 0 else [], inc="pe" if kc == NCH - 1 else None)
            peq = S.c("pe")
            S.do("vector", (lambda h=h, tb=tb, bank=bank: (lambda e: e.tensor_scalar_mul(
                q_ex[0:64, h * T + tb * 512:h * T + (tb + 1) * 512], pb[bank][0:64, :], DN)))(),
                waits=[("pe", peq)], inc="dv", drain=True)
            QCP_DV = S.c("dv")
            S.do("scalar", (lambda bank=bank, u=u: (lambda e: e.activation(
                SQB[u % 2][0:64, :], pb[bank][0:64, :], AF.Square, bias=0.0, scale=DN)))(),
                waits=[("dv", QCP_DV)], inc="ac")
            qev[u] = S.c("ac")
            dbank = 4 + u % 2
            wdg = [("ac", qev[u])]
            if u < 2:
                wdg.append(("ac", S.c("ac")))
                wdg.append(("dv", S.c("dv")))
            if u - 4 in qdiag_ac:
                wdg.append(("ac", qdiag_ac[u - 4]))
            S.do("tensor", (lambda u=u, dbank=dbank: (lambda e: e.matmul(
                pb[dbank][64:65, :], ones_b[0:64, 0:1], SQB[u % 2][0:64, :],
                start=True, stop=True, tile_position=(0, 64))))(),
                waits=wdg, inc="pe")
            qdiag_ac[u] = S.do("scalar", (lambda h=h, tb=tb, dbank=dbank: (lambda e: e.activation(
                q_ex[64:65, h * T + tb * 512:h * T + (tb + 1) * 512], pb[dbank][64:65, :],
                AF.Copy, bias=0.0, scale=-0.5)))(),
                waits=[("pe", S.c("pe"))], inc="ac")
    QPROJ_AC = S.c("ac")
    QPROJ_PE = S.c("pe")
    if part in ("b", "fused"):
        CC_DONE = 0
        GX = G if part == "b" else NSL
        if part == "b":
            S.do("sync", lambda e: e.dma_start(out=stab4[0:1, 0:GX * 16], in_=s_in), waits=[], inc="xd", incn=16)
        else:
            S.do("sync", lambda e: e.dma_start(out=stab4[0:1, 0:GX * 16],
                                               in_=d_stab.rearrange("a b -> (a b)")[None, :]),
                 waits=[("gp", BARRIER_GP)], inc="xd", incn=16)
        n = GX
        first = True
        while n > 1:
            half = n // 2
            S.do("vector", (lambda half=half: (lambda e: e.tensor_max(
                stab4[0:1, 0:half * 16].rearrange("o (a h) -> o a h", h=16),
                stab4[0:1, 0:half * 16].rearrange("o (a h) -> o a h", h=16),
                stab4[0:1, half * 16:2 * half * 16].rearrange("o (a h) -> o a h", h=16))))(),
                waits=[("xd", S.c("xd"))] if first else [], inc=None, drain=True)
            first = False
            n = half
        S.do("vector", lambda e: e.tensor_copy(stabrow[:], stab4[0:1, 0:16]),
             waits=[], inc="dv", drain=True)
    gp_woA = load_weight(w_o, attn_ring[0], QPROJ_PE, cast="scalar", pieces=(3, 4, 5, 6, 7))
    S.do("tensor", lambda e: e.matmul(pb[4][0:128, 0:16], onesf[:], stabrow[:], start=True, stop=True),
         waits=[("gd", S.c("gd")), ("ac", S.c("ac")), ("dv", S.c("dv")), ("gp", GP_INIT)], inc="pe")
    S.do("scalar", lambda e: e.activation(estab[:], pb[4][0:128, 0:16], AF.Exp, bias=0.0, scale=-1.0),
         waits=[("pe", S.c("pe"))], inc="ac")
    ESTAB_AC = S.c("ac")
    caug_dv = {}
    cadd_dv = {}
    for h in range(H):
        slot = h % 2
        c4 = cstg4s[h % 2]
        wdma = [("cc", CC_DONE)] if part == "full" else []
        if part == "full":
            if h >= 2:
                wdma.append(("dv", caug_dv[h - 2]))
            S.do("gpsimd", (lambda h=h, slot=slot: (lambda e: e.dma_start(
                out=cstg[:, slot * 267:slot * 267 + 267], in_=dr_co[:, h * 267:(h + 1) * 267])))(),
                waits=wdma, inc="gd", incn=16)
        else:
            GX = G if part == "b" else NSL
            CSRC = c_in if part == "b" else d_cslab
            if h >= 2:
                wdma.append(("dv", cadd_dv[h - 2]))
            else:
                wdma.append(("ac", ESTAB_AC))
            if part == "fused" and h == 0:
                wdma.append(("gp", BARRIER_GP))
            S.do("gpsimd", (lambda h=h, GX=GX, CSRC=CSRC, c4=c4: (lambda e: e.dma_start(
                out=c4[0:65, 0:GX * 267].rearrange("p (g n) -> p g n", g=GX),
                in_=CSRC[:, :, h * 267:(h + 1) * 267].rearrange("g p n -> p g n"))))(),
                waits=wdma, inc="gd", incn=16)
            n = GX
            first = True
            while n > 2:
                half = n // 2
                S.do("vector", (lambda half=half, c4=c4: (lambda e: e.tensor_add(
                    c4[0:65, 0:half * 267].rearrange("p (g n) -> p g n", n=267),
                    c4[0:65, 0:half * 267].rearrange("p (g n) -> p g n", n=267),
                    c4[0:65, half * 267:2 * half * 267].rearrange("p (g n) -> p g n", n=267))))(),
                    waits=[("gd", S.c("gd"))] if first else [], inc=None, drain=True)
                first = False
                n = half
            S.do("vector", (lambda slot=slot, first=first, c4=c4: (lambda e: e.tensor_add(
                cstg[:, slot * 267:slot * 267 + 267], c4[0:65, 0:267], c4[0:65, 267:534])))(),
                waits=[("gd", S.c("gd"))] if first else [], inc="dv", drain=True)
            cadd_dv[h] = S.c("dv")
        csrc_w = ("gd", S.c("gd")) if part == "full" else ("dv", S.c("dv"))
        tbank = 6 + h % 2
        wtp = [csrc_w, ("ac", ESTAB_AC)]
        if h < 2:
            wtp.append(("dv", S.c("dv")))
        if h >= 2:
            wtp.append(("dv", caug_dv[h - 2]))
        for i, (m0, m1) in enumerate(MCS):
            S.do("tensor", (lambda slot=slot, i=i, m0=m0, m1=m1, tbank=tbank: (lambda e: e.transpose(
                pb[tbank][0:m1 - m0, i * 65:(i + 1) * 65], cstg[:, slot * 267 + m0:slot * 267 + m1],
                ident_f[0:65, 0:65])))(),
                waits=wtp if i == 0 else [], inc=None)
        S.do("tensor", (lambda slot=slot, tbank=tbank: (lambda e: e.transpose(
            pb[tbank][0:1, 195:260], cstg[:, slot * 267 + 266:slot * 267 + 267], ident_f[0:65, 0:65])))(),
            waits=[], inc="pe")
        petp = S.c("pe")
        S.do("vector", (lambda tbank=tbank: (lambda e: e.tensor_copy(vsn[:], pb[tbank][0:1, 195:260])))(),
             waits=[("pe", petp)], inc="dv", drain=True)
        ebank = 2 + h % 2
        S.do("tensor", (lambda ebank=ebank: (lambda e: e.matmul(
            pb[ebank][0:128, 0:65], epsf[:], vsn[:], start=True, stop=True)))(),
            waits=[("dv", S.c("dv")), ("ac", QPROJ_AC)], inc="pe")
        S.do("vector", (lambda ebank=ebank: (lambda e: e.tensor_copy(epsv[:], pb[ebank][0:128, 0:65])))(),
             waits=[("pe", S.c("pe"))], inc="dv", drain=True)
        for i, (m0, m1) in enumerate(MCS):
            rows = m1 - m0
            caug_dv[h] = S.do("vector", (lambda h=h, i=i, rows=rows, tbank=tbank: (lambda e: e.scalar_tensor_tensor(
                CAUG[0:rows, h * 195 + i * 65:h * 195 + (i + 1) * 65],
                pb[tbank][0:rows, i * 65:(i + 1) * 65], estab[0:rows, h:h + 1],
                epsv[0:rows, :], ALU.mult, ALU.add)))(),
                waits=[], inc="dv", drain=True)
    CAUG_DV = S.c("dv")

    rmx_dma = {}
    p1dv = {}
    for h in range(H):
        for tau in range(nT):
            u = h * nT + tau
            dbank = u % 4
            wmm = [("ac", QPROJ_AC)]
            if u == 0:
                wmm.append(("gd", GD_B16))
                wmm.append(("dv", S.c("dv")))
            if u >= 4:
                wmm.append(("dv", p1dv[u - 4]))
            S.do("tensor", (lambda h=h, tau=tau, dbank=dbank: (lambda e: e.matmul(
                pb[dbank][0:128, 0:266], q_ex[:, h * T + tau * 128:h * T + (tau + 1) * 128],
                projex[:], start=True, stop=True)))(),
                waits=wmm, inc="pe")
            p1dv[u] = S.do("vector", (lambda tau=tau, dbank=dbank: (lambda e: e.reduce_max(
                rmaxsb[:, tau:tau + 1], pb[dbank][0:128, 0:266], axis=AX)))(),
                waits=[("pe", S.c("pe"))], inc="dv")
        wtp = [("dv", S.c("dv"))]
        if h >= 1:
            wtp.append(("gd", rmx_dma[h - 1]))
        S.do("tensor", (lambda: (lambda e: e.transpose(
            pb[6][0:nT, 128:256], rmaxsb[:, 0:nT], ident_f[:])))(),
            waits=wtp, inc="pe")
        S.do("vector", lambda e: e.tensor_copy(B16RT[0:nT, 0:128], pb[6][0:nT, 128:256]),
             waits=[("pe", S.c("pe"))], inc="dv", drain=True)
        rmx_dma[h] = S.do("gpsimd", (lambda h=h: (lambda e: e.dma_start(
            out=dr_rmax[h:h + 1, :].rearrange("o (q p) -> (o q) p", p=128), in_=B16RT[0:nT, 0:128])))(),
            waits=[("dv", S.c("dv"))], inc="gd", incn=16)
    P1_PE = S.c("pe")

    statq_dv = {}
    qexp = {}
    qexp_dv = {}
    st5ev = {}
    st5ac = {}
    numsc_dv = {}
    nsbdv = {}

    statq_dma = {}

    def emit_statq_dma(h):
        wrb = [("gd", rmx_dma[h])]
        if h >= 2:
            wrb.append(("dv", statq_dv[h - 2]))
        statq_dma[h] = S.do("gpsimd",
            (lambda h=h: (lambda e: e.dma_start(out=RM64S[h % 2], in_=dr_rmax[h:h + 1, :])))(),
            waits=wrb, inc="gd", incn=16)

    def emit_statq_stt(h):
        statq_dv[h] = S.do("vector", (lambda h=h: (lambda e: e.scalar_tensor_tensor(
            q_ex[64:65, h * T:(h + 1) * T], q_ex[64:65, h * T:(h + 1) * T], 2.0, RM64S[h % 2],
            ALU.mult, ALU.subtract)))(),
            waits=[("gd", statq_dma[h]), ("pe", P1_PE)], inc="dv", drain=True)

    for hh0 in (0, 1):
        emit_statq_dma(hh0)
        emit_statq_stt(hh0)
    U = H * nB

    def stage_A(u, mcs):
        h, tb = u // nB, u % nB
        if nB < 3 and 0 in mcs and tb == 0 and 1 <= h < H - 1:
            emit_statq_dma(h + 1)
            emit_statq_stt(h + 1)
        qslot = u % 2
        for i, (m0, m1) in ((j, MCS[j]) for j in mcs):
            rows = m1 - m0
            uu = u * 3 + i
            bank = uu % 2
            wmm = [("dv", statq_dv[h])]
            if uu >= 2:
                wmm.append(("ac", qexp[uu - 2]))
                wmm.append(("dv", qexp_dv[uu - 2]))
            S.do("tensor", (lambda h=h, tb=tb, m0=m0, m1=m1, bank=bank: (lambda e: e.matmul(
                pb[bank][0:m1 - m0, :], projex[:, m0:m1],
                q_ex[:, h * T + tb * 512:h * T + (tb + 1) * 512], start=True, stop=True)))(),
                waits=wmm, inc="pe")
            pem = S.c("pe")
            qexp[uu] = S.do("scalar", (lambda rows=rows, bank=bank: (lambda e: e.activation(
                SQF[bank][0:rows, :], pb[bank][0:rows, :], AF.Exp, bias=0.0, scale=1.0)))(),
                waits=[("pe", pem)], inc="ac", drain=True)
            qexp_dv[uu] = S.do("vector", (lambda i=i, rows=rows, bank=bank, qslot=qslot: (lambda e: e.tensor_scalar_add(
                QP.rearrange("p (s x) -> p s x", s=2)[0:rows, qslot, i * 512:(i + 1) * 512],
                SQF[bank][0:rows, :], EPS)))(),
                waits=[("ac", S.c("ac"))], inc="dv", drain=True)

    st5pe = {}

    def stage_B(u):
        h, tb = u // nB, u % nB
        qslot = u % 2
        sbank = 4 + u % 4
        for i, (m0, m1) in enumerate(MCS):
            rows = m1 - m0
            wmm = [("dv", qexp_dv[u * 3 + i]), ("dv", CAUG_DV)]
            if i == 0:
                if u - 4 in st5ev:
                    wmm.append(("dv", st5ev[u - 4]))
                    wmm.append(("ac", st5ac[u - 4]))
                if u < 4:
                    wmm.append(("dv", S.c("dv")))
            S.do("tensor", (lambda h=h, i=i, rows=rows, qslot=qslot, sbank=sbank: (lambda e: e.matmul(
                pb[sbank][0:65, :], CAUG[0:rows, h * 195 + i * 65:h * 195 + (i + 1) * 65],
                QP.rearrange("p (s x) -> p s x", s=2)[0:rows, qslot, i * 512:(i + 1) * 512],
                start=(i == 0), stop=(i == 2))))(),
                waits=wmm, inc="pe" if i == 2 else None)
        st5pe[u] = S.c("pe")
        st5ac[u] = S.do("scalar", (lambda u2=u % 2, sbank=sbank: (lambda e: e.activation(
            LNR[u2], pb[sbank][64:65, :], AF.Ln, bias=0.0, scale=1.0)))(),
            waits=[("pe", st5pe[u])], inc="ac", drain=True)
        S.do("scalar", (lambda u2=u % 2: (lambda e: e.activation(
            DINVB[u2], LNR[u2], AF.Exp, bias=0.0, scale=-1.0)))(),
            waits=[], inc="ac", drain=True)

    def stage_C(u):
        h, tb = u // nB, u % nB
        if nB >= 3:
            if tb == 0 and h + 2 < H:
                emit_statq_dma(h + 2)
            if tb == 2 and h + 2 < H:
                emit_statq_stt(h + 2)
        sbank = 4 + u % 4
        bbank = 2 + u % 2
        S.do("tensor", (lambda u2=u % 2, bbank=bbank: (lambda e: e.matmul(
            pb[bbank][0:64, :], ones_b[64:65, 0:64], DINVB[u2], start=True, stop=True)))(),
            waits=[("ac", st5ac[u] + 1), ("dv", nsbdv.get(u - 2, 0))], inc="pe")
        S.do("vector", (lambda u4=u % 4, bbank=bbank: (lambda e: e.tensor_copy(NSB[u4], pb[bbank][0:64, :])))(),
             waits=[("pe", S.c("pe"))], inc="dv", drain=True)
        ACnsb = S.c("dv")
        nsbdv[u] = ACnsb
        if h % 2 == 0:
            st5ev[u] = S.do("vector", (lambda h=h, tb=tb, sbank=sbank, u4=u % 4: (lambda e: e.tensor_mul(
                B3[0:64, h // 2, tb * 512:(tb + 1) * 512], pb[sbank][0:64, :], NSB[u4])))(),
                waits=[("dv", ACnsb)], inc="dv")
        else:
            st5ev[u] = S.do("vector", (lambda tb=tb, sbank=sbank, u4=u % 4: (lambda e: e.tensor_mul(
                NUMSC[tb % 2], pb[sbank][0:64, :], NSB[u4])))(),
                waits=[("dv", ACnsb)] + ([("gd", numsc_dv[u - 2])] if (u - 2) in numsc_dv else []),
                inc="dv")
            numsc_dv[u] = S.do("gpsimd", (lambda h=h, tb=tb: (lambda e: e.dma_start(
                out=B3[64:128, h // 2, tb * 512:(tb + 1) * 512], in_=NUMSC[tb % 2])))(),
                waits=[("dv", st5ev[u])], inc="gd", incn=16)

    RING_PE = {}
    for v in range(U + 2):
        if v < U:
            stage_A(v, (0, 1))
        if 1 <= v <= U:
            stage_B(v - 1)
            if v - 1 in (15, 31, 47):
                RING_PE[v - 1] = S.c("pe")
        if v < U:
            stage_A(v, (2,))
        if v >= 2:
            stage_C(v - 2)
    ATTN_DV = S.c("dv")
    ATTN_GD = S.c("gd")
    ST5_PE = S.c("pe")
    AC_ATTN = S.c("ac")

    wgp = {}
    S.do("sync", lambda e: e.dma_start(out=BOSB, in_=b_o[None, :]),
         waits=[("pe", P1_PE)], inc="xd", incn=16)
    XD_BO = S.c("xd")
    for half in range(2):
        S.do("tensor", (lambda half=half: (lambda e: e.matmul(
            pb[2 + half][0:128, :], onesf[:], BOSB[:, half * 512:(half + 1) * 512], start=True, stop=True)))(),
            waits=[("xd", XD_BO), ("dv", ATTN_DV), ("ac", AC_ATTN)], inc="pe")
        S.do("scalar", (lambda half=half: (lambda e: e.copy(
            BOBC[:, half * 512:(half + 1) * 512], pb[2 + half][0:128, :])))(),
            waits=[("pe", S.c("pe")), ("gd", ATTN_GD)], inc="ac")
    BOBC_AC = S.c("ac")

    gp_wo = load_weight(w_o, attn_ring[0], ST5_PE, cast="scalar", pieces=(0, 1, 2))
    wgp[("w1", 0)] = load_weight(w_1[0], ring_c(0), ST5_PE)
    wgp[("w2", 0)] = load_weight(w_2[0], ring_c(1), ST5_PE)

    woev = {}
    xfm2_dv = {}
    ln_dv = {}

    def emit_xfm2(tau):
        bank = pbb[6 + tau % 2]
        wtp = [("dv", ln_dv[tau])]
        if tau >= 2:
            wtp.append(("dv", xfm2_dv[tau - 2]))
        for c in range(NCH):
            S.do("tensor", (lambda tau=tau, c=c, bank=bank: (lambda e: e.transpose(
                bank[:, c * 128:(c + 1) * 128],
                x_tm[:, tau * 1024 + c * 128:tau * 1024 + (c + 1) * 128], ident_b[:])))(),
                waits=wtp if c == 0 else [], inc="pe" if c == NCH - 1 else None)
        xfm2_dv[tau] = S.do("vector", (lambda tau=tau, bank=bank: (lambda e: e.tensor_copy(
            B3[:, :, tau * 128:(tau + 1) * 128],
            bank[:, 0:1024].rearrange("p (c n) -> p c n", c=NCH))))(),
            waits=[("pe", S.c("pe"))], inc="dv")

    for tau in range(nT):
        zslot = tau % 2
        wz = [("ac", BOBC_AC)]
        if tau >= 2:
            wz.append(("dv", ln_dv[tau - 2]))
        S.do("vector", (lambda tau=tau, zslot=zslot: (lambda e: e.tensor_add(
            ZF[zslot], x_tm[:, tau * 1024:(tau + 1) * 1024], BOBC)))(),
            waits=wz, inc="dv", drain=True)
        ZXBO = S.c("dv")
        for half in range(2):
            u = tau * 2 + half
            bank = u % 2
            wmm = [gp_wo, ("dv", ATTN_DV), ("gd", ATTN_GD)]
            if u >= 2:
                wmm.append(("dv", woev[u - 2]))
            for kc in range(NCH):
                S.do("tensor", (lambda tau=tau, half=half, kc=kc, bank=bank: (lambda e: e.matmul(
                    pb[bank][:, :], B_fm[:, kc * T + tau * 128:kc * T + (tau + 1) * 128],
                    attn_ring[0][:, kc * 1024 + half * 512:kc * 1024 + (half + 1) * 512],
                    start=(kc == 0), stop=(kc == NCH - 1))))(),
                    waits=wmm if kc == 0 else [], inc="pe" if kc == NCH - 1 else None)
            woev[u] = S.do("vector", (lambda half=half, bank=bank, zslot=zslot: (lambda e: e.tensor_add(
                ZF[zslot][:, half * 512:(half + 1) * 512], ZF[zslot][:, half * 512:(half + 1) * 512],
                pb[bank][:, :])))(),
                waits=[("pe", S.c("pe")), ("dv", ZXBO)], inc="dv", drain=True)
        wln = [("dv", S.c("dv"))]
        if tau >= 1:
            wln.append(("ac", ln_dv[(tau - 1, "sq")]))
        S.do("vector", (lambda zslot=zslot: (lambda e: e.bn_stats(bn6[:, 0:6], ZF[zslot][:, 0:512])))(),
             waits=wln, inc=None, drain=True)
        S.do("vector", (lambda zslot=zslot: (lambda e: e.bn_stats(bn6[:, 6:12], ZF[zslot][:, 512:1024])))(),
             waits=[], inc=None, drain=True)
        S.do("vector", lambda e: e.bn_aggr(agg2[:], bn6[:]), waits=[], inc="dv", drain=True)
        ln_dv[(tau, "sq")] = S.do("scalar", lambda e: e.activation(
            rstd[:], agg2[:, 1:2], AF.Sqrt, bias=lneps[:], scale=1.0),
            waits=[("dv", S.c("dv"))], inc="ac")
        S.do("vector", lambda e: e.reciprocal(rstd[:], rstd[:]),
             waits=[("ac", ln_dv[(tau, "sq")])], inc=None, drain=True)
        S.do("vector", lambda e: e.tensor_scalar(nbv[:], agg2[:, 0:1], rstd[:, 0:1], -1.0, ALU.mult, ALU.mult),
             waits=[], inc=None, drain=True)
        ln_dv[tau] = S.do("vector", (lambda tau=tau, zslot=zslot: (lambda e: e.tensor_scalar(
            x_tm[:, tau * 1024:(tau + 1) * 1024], ZF[zslot], rstd[:, 0:1], nbv[:, 0:1], ALU.mult, ALU.add)))(),
            waits=[], inc="dv", drain=True)
        if tau >= 2:
            emit_xfm2(tau - 2)
    emit_xfm2(nT - 2)
    emit_xfm2(nT - 1)
    WO_PE = S.c("pe")
    LN1_DV = S.c("dv")

    S.do("sync", lambda e: e.dma_start(out=WGSTG.rearrange("p (c e) -> p c e", c=NCH),
                                       in_=w_g.rearrange("(c p) e -> p c e", p=128)),
         waits=[("dv", ATTN_DV)], inc="xd", incn=16)
    S.do("gpsimd", lambda e: e.tensor_copy(wgb[:], WGSTG), waits=[("xd", S.c("xd"))], inc="gp", drain=True)
    S.do("sync", lambda e: e.dma_start(out=BGSTG, in_=b_g[None, :]), waits=[], inc="xd", incn=16)
    S.do("gpsimd", lambda e: e.tensor_copy(bgb[:], BGSTG), waits=[("xd", S.c("xd"))], inc="gp", drain=True)
    GP_GATE = S.c("gp")
    S.do("sync", lambda e: e.dma_start(out=BSTG, in_=b_1), waits=[("gp", S.c("gp"))], inc="xd", incn=16)
    XD_B1 = S.c("xd")
    for c in range(NCH):
        S.do("tensor", (lambda c=c: (lambda e: e.transpose(
            pb[2][0:128, c * 8:(c + 1) * 8], BSTG[:, c * 128:(c + 1) * 128], ident_f[0:8, 0:8])))(),
            waits=[("xd", XD_B1), ("ac", BOBC_AC)] if c == 0 else [], inc="pe" if c == NCH - 1 else None)
    S.do("scalar", lambda e: e.copy(b1T[:], pb[2][0:128, 0:64]), waits=[("pe", S.c("pe"))], inc="ac")
    B1T_AC = S.c("ac")
    S.do("sync", lambda e: e.dma_start(out=BSTG, in_=b_2), waits=[("ac", B1T_AC)], inc="xd", incn=16)
    S.do("gpsimd", lambda e: e.tensor_copy(B2B, BSTG), waits=[("xd", S.c("xd")), ("pe", S.c("pe"))],
         inc="gp", drain=True)
    GP_MISC = S.c("gp")

    ohT_dv = {}
    for tau in range(nT):
        gbank = 2 + tau % 2
        wmm = [("dv", xfm2_dv[tau]), ("gp", GP_GATE), ("ac", B1T_AC)]
        if tau >= 2:
            wmm.append(("dv", ohT_dv[tau - 2]))
        for kc in range(NCH):
            S.do("tensor", (lambda tau=tau, kc=kc, gbank=gbank: (lambda e: e.matmul(
                pb[gbank][0:128, 0:8], B_fm[:, kc * T + tau * 128:kc * T + (tau + 1) * 128],
                wgb[:, kc * 8:(kc + 1) * 8], start=(kc == 0), stop=False)))(),
                waits=wmm if kc == 0 else [], inc=None)
        S.do("tensor", (lambda gbank=gbank: (lambda e: e.matmul(
            pb[gbank][0:128, 0:8], ones_b[0:1, 0:128], bgb[:], start=False, stop=True)))(),
            waits=[], inc="pe")
        peg = S.c("pe")
        S.do("vector", (lambda gbank=gbank: (lambda e: e.reduce_max(ohmax[:], pb[gbank][0:128, 0:8], axis=AX)))(),
             waits=[("pe", peg)], inc=None, drain=True)
        S.do("vector", (lambda gbank=gbank: (lambda e: e.tensor_scalar(
            oh_f[:], pb[gbank][0:128, 0:8], ohmax[:, 0:1], None, ALU.is_equal)))(),
            waits=[], inc="dv", drain=True)
        S.do("tensor", lambda e: e.transpose(pb[6][0:8, 256:384], oh_f[:], ident_f[:]),
             waits=[("dv", S.c("dv"))], inc="pe")
        ohT_dv[tau] = S.do("vector", (lambda tau=tau: (lambda e: e.tensor_copy(
            ohT[:, tau * 128:(tau + 1) * 128], pb[6][0:8, 256:384])))(),
            waits=[("pe", S.c("pe"))], inc="dv")
    OHT_DV = S.c("dv")

    ring_last = {0: RING_PE.get(15, ST5_PE), 1: RING_PE.get(31, ST5_PE), 2: RING_PE.get(47, ST5_PE)}
    hm_free = {0: 0, 1: 0}
    selfree = {0: OHT_DV, 1: OHT_DV}
    ybfree = {0: 0, 1: 0}
    pre_free = {0: 0, 1: 0}
    gelu_prev = {}
    lastmask = {}

    for ee in range(E):
        s1 = (2 * ee) % 3
        s2 = (2 * ee + 1) % 3

        def w1_block(tb, ee=ee, s1=s1):
            selb = (ee * nB + tb) % 2
            sb = 2 + selb
            selpe = S.do("tensor", (lambda ee=ee, tb=tb, sb=sb: (lambda e: e.matmul(
                pb[sb][0:128, :], sel8[:, ee * 128:(ee + 1) * 128],
                ohT[:, tb * 512:(tb + 1) * 512], start=True, stop=True)))(),
                waits=[("dv", selfree[selb]), wgp[("w1", ee)], ("gp", GP_MISC),
                       ("pe", hm_free[tb % 2])],
                inc="pe")
            for ic in range(NCH):
                u = ic % 2
                bank = u
                wmm = [("ac", pre_free[u])] if pre_free[u] else []
                for kc in range(NCH):
                    S.do("tensor", (lambda tb=tb, ic=ic, kc=kc, bank=bank, s1=s1: (lambda e: e.matmul(
                        pb[bank][:, :], ring_c(s1)[:, kc * 1024 + ic * 128:kc * 1024 + (ic + 1) * 128],
                        B_fm[:, kc * T + tb * 512:kc * T + (tb + 1) * 512],
                        start=(kc == 0), stop=(kc == NCH - 1))))(),
                        waits=wmm if kc == 0 else [], inc="pe" if kc == NCH - 1 else None)
                pew1 = S.c("pe")
                wg = [("pe", pew1), ("ac", B1T_AC)]
                if (ee, tb, ic - 1) in gelu_prev:
                    wg.append(("dv", gelu_prev[(ee, tb, ic - 1)]))
                pre_free[u] = S.do("scalar", (lambda ee=ee, ic=ic, bank=bank: (lambda e: e.activation(
                    hg_b[:], pb[bank][:, :], AF.Gelu, bias=b1T[:, ic * 8 + ee:ic * 8 + ee + 1], scale=1.0)))(),
                    waits=wg, inc="ac")
                gelu_prev[(ee, tb, ic)] = S.do("vector",
                    (lambda tb=tb, ic=ic, sb=sb: (lambda e: e.tensor_mul(
                        HM[tb % 2][:, ic * 512:(ic + 1) * 512], hg_b[:], pb[sb][0:128, :])))(),
                    waits=[("ac", pre_free[u]), ("pe", selpe)], inc="dv")
            lastmask[(ee, tb)] = gelu_prev[(ee, tb, NCH - 1)]
            selfree[selb] = lastmask[(ee, tb)]
            if tb == nB - 1:
                ring_last[s1] = S.c("pe")

        def w2_block(tb, ee=ee, s2=s2):
            for tsub in range(4):
                tau = tb * 4 + tsub
                for half in range(2):
                    u = (tsub * 2 + half) % 2
                    ybank = 4 + u
                    wmm = [wgp[("w2", ee)], ("dv", lastmask[(ee, tb)]), ("dv", OHT_DV)]
                    if ybfree[u]:
                        wmm.append(("dv", ybfree[u]))
                    first = True
                    if ee == 0:
                        S.do("tensor", (lambda tau=tau, half=half, ybank=ybank: (lambda e: e.matmul(
                            pb[ybank][:, :], ohT[:, tau * 128:(tau + 1) * 128],
                            B2B[:, half * 512:(half + 1) * 512], start=True, stop=False)))(),
                            waits=wmm, inc=None)
                        first = False
                    for ic in range(NCH):
                        S.do("tensor", (lambda tb=tb, tsub=tsub, half=half, ic=ic, ybank=ybank, s2=s2:
                             (lambda e: e.matmul(
                                 pb[ybank][:, :],
                                 HM[tb % 2][:, ic * 512 + tsub * 128:ic * 512 + (tsub + 1) * 128],
                                 ring_c(s2)[:, ic * 1024 + half * 512:ic * 1024 + (half + 1) * 512],
                                 start=(first and ic == 0), stop=(ic == NCH - 1))))(),
                            waits=wmm if (first and ic == 0) else [], inc="pe" if ic == NCH - 1 else None)
                    if ee == 0:
                        ybfree[u] = S.do("vector", (lambda tau=tau, half=half, ybank=ybank: (lambda e: e.tensor_copy(
                            y_acc[:, tau * 1024 + half * 512:tau * 1024 + (half + 1) * 512],
                            pb[ybank][:, :])))(),
                            waits=[("pe", S.c("pe")), ("pe", WO_PE)], inc="dv")
                    else:
                        ybfree[u] = S.do("vector", (lambda tau=tau, half=half, ybank=ybank: (lambda e: e.tensor_add(
                            y_acc[:, tau * 1024 + half * 512:tau * 1024 + (half + 1) * 512],
                            y_acc[:, tau * 1024 + half * 512:tau * 1024 + (half + 1) * 512],
                            pb[ybank][:, :])))(),
                            waits=[("pe", S.c("pe"))], inc="dv")
            hm_free[tb % 2] = S.c("pe")
            if tb == nB - 1:
                ring_last[s2] = S.c("pe")

        if nB == 1:
            w1_block(0)
            if ee < E - 1:
                wgp[("w1", ee + 1)] = load_weight(w_1[ee + 1], ring_c((2 * ee + 2) % 3), ring_last[(2 * ee + 2) % 3])
                wgp[("w2", ee + 1)] = load_weight(w_2[ee + 1], ring_c((2 * ee + 3) % 3), ring_last[(2 * ee + 3) % 3])
            w2_block(0)
        else:
            w1_block(0)
            if ee < E - 1:
                wgp[("w1", ee + 1)] = load_weight(w_1[ee + 1], ring_c((2 * ee + 2) % 3), ring_last[(2 * ee + 2) % 3])
            w1_block(1)
            w2_block(0)
            for tb in range(2, nB):
                w1_block(tb)
                if tb == nB - 1 and ee < E - 1:
                    wgp[("w2", ee + 1)] = load_weight(w_2[ee + 1], ring_c((2 * ee + 3) % 3), ring_last[(2 * ee + 3) % 3])
                w2_block(tb - 1)
            w2_block(nB - 1)
    FFN_DV = S.c("dv")

    odc = 0
    for tau in range(nT):
        zslot = tau % 2
        wz = [("dv", FFN_DV)]
        if tau >= 2:
            wz.append(("od", 16 * (tau - 1)))
        S.do("vector", (lambda tau=tau, zslot=zslot: (lambda e: e.tensor_add(
            OUTS[zslot], x_tm[:, tau * 1024:(tau + 1) * 1024],
            y_acc[:, tau * 1024:(tau + 1) * 1024])))(),
            waits=wz, inc="dv", drain=True)
        wln = []
        if tau >= 1:
            wln.append(("ac", ln_dv[("ln2sq", tau - 1)]))
        S.do("vector", (lambda zslot=zslot: (lambda e: e.bn_stats(bn6[:, 0:6], OUTS[zslot][:, 0:512])))(),
             waits=wln, inc=None, drain=True)
        S.do("vector", (lambda zslot=zslot: (lambda e: e.bn_stats(bn6[:, 6:12], OUTS[zslot][:, 512:1024])))(),
             waits=[], inc=None, drain=True)
        S.do("vector", lambda e: e.bn_aggr(agg2[:], bn6[:]), waits=[], inc="dv", drain=True)
        ln_dv[("ln2sq", tau)] = S.do("scalar", lambda e: e.activation(
            rstd[:], agg2[:, 1:2], AF.Sqrt, bias=lneps[:], scale=1.0),
            waits=[("dv", S.c("dv"))], inc="ac")
        S.do("vector", lambda e: e.reciprocal(rstd[:], rstd[:]),
             waits=[("ac", ln_dv[("ln2sq", tau)])], inc=None, drain=True)
        S.do("vector", lambda e: e.tensor_scalar(nbv[:], agg2[:, 0:1], rstd[:, 0:1], -1.0, ALU.mult, ALU.mult),
             waits=[], inc=None, drain=True)
        S.do("vector", (lambda zslot=zslot: (lambda e: e.tensor_scalar(
            OUTS[zslot], OUTS[zslot], rstd[:, 0:1], nbv[:, 0:1], ALU.mult, ALU.add)))(),
            waits=[], inc="dv", drain=True)
        S.do("sync", (lambda tau=tau, zslot=zslot: (lambda e: e.dma_start(
            out=out[tau * 128:(tau + 1) * 128, :], in_=OUTS[zslot])))(),
            waits=[("dv", S.c("dv"))], inc="od", incn=16)
        odc = S.c("od")
    S.do("sync", lambda e: e.wait_ge(sems["od"], odc), waits=[])

    with nc.Block() as blk:
        for engname in ENGS:
            steps = S.q[engname]
            def body(eng, steps=steps):
                for fn, waits, inc, incn, drain in steps:
                    for s, v in waits:
                        eng.wait_ge(sems[s], v)
                    if drain:
                        eng.drain()
                    ins = fn(eng)
                    if inc is not None:
                        assert ins is not None
                        ins.then_inc(sems[inc], incn)
            getattr(blk, engname)(body)

    st.close()
    return nc


MODE = "fused"


def kernel(L0_attn_x, L0_attn_Wq, L0_attn_Wk, L0_attn_Wv, L0_attn_Wo, L0_attn_bo,
           L0_attn_proj, L0_ffn_Wg, L0_ffn_bg, L0_ffn_W1, L0_ffn_b1, L0_ffn_W2, L0_ffn_b2):
    T = 2048
    groups = [[0, 1, 2, 3], [4, 5, 6, 7]]
    f = lambda a: np.ascontiguousarray(np.asarray(a, dtype=np.float32))
    x = f(L0_attn_x)
    shared = {
        "w_q": f(L0_attn_Wq), "w_k": f(L0_attn_Wk), "w_v": f(L0_attn_Wv),
        "w_o": f(L0_attn_Wo), "b_o": f(L0_attn_bo), "prj": f(L0_attn_proj),
        "w_g": f(L0_ffn_Wg), "b_g": f(L0_ffn_bg), "w_1": f(L0_ffn_W1),
        "b_1": f(L0_ffn_b1), "w_2": f(L0_ffn_W2), "b_2": f(L0_ffn_b2),
    }
    in_maps = []
    for core in range(8):
        b, t0 = core // 4, (core % 4) * T
        m = dict(shared)
        m["x_in"] = np.ascontiguousarray(x[b, t0:t0 + T, :])
        in_maps.append(m)
    if MODE == "fused":
        nc = build(T, groups, part="fused")
        for core in range(8):
            g0, i = (core // 4) * 4, core % 4
            far = g0 + (i + 2) % 4
            in_maps[core]["x_far"] = np.ascontiguousarray(x[far // 4, (far % 4) * T:(far % 4) * T + T, :])
            in_maps[core]["slot_in"] = np.full((1, 16), core % 2, np.int32)
        res = run_bass_kernel_spmd(nc, in_maps, core_ids=list(range(8)))
        outs = [res.results[c]["out"] for c in range(8)]
    elif MODE == "full":
        nc = build(T, groups, part="full")
        res = run_bass_kernel_spmd(nc, in_maps, core_ids=list(range(8)))
        outs = [res.results[c]["out"] for c in range(8)]
    else:
        nca = build(T, groups, part="a")
        ra = run_bass_kernel_spmd(nca, in_maps, core_ids=list(range(8)))
        c_parts = [ra.results[c]["c_out"] for c in range(8)]
        s_parts = [ra.results[c]["s_out"] for c in range(8)]
        ncb = build(T, groups, part="b")
        in_maps_b = []
        for core in range(8):
            g0 = (core // 4) * 4
            m = dict(in_maps[core])
            m["c_in"] = np.ascontiguousarray(np.stack(c_parts[g0:g0 + 4], axis=0))
            m["s_in"] = np.ascontiguousarray(
                np.concatenate([s_parts[g0 + i].reshape(-1) for i in range(4)]).reshape(1, 64))
            in_maps_b.append(m)
        rb = run_bass_kernel_spmd(ncb, in_maps_b, core_ids=list(range(8)))
        outs = [rb.results[c]["out"] for c in range(8)]
    return np.stack([np.concatenate(outs[0:4], axis=0),
                     np.concatenate(outs[4:8], axis=0)], axis=0)


# version 24
# speedup vs baseline: 1.0182x; 1.0181x over previous
import numpy as np
from contextlib import ExitStack

import concourse.bass as bass
import concourse.mybir as mybir
from concourse.bass_utils import run_bass_kernel_spmd

F32 = mybir.dt.float32
BF16 = mybir.dt.bfloat16
AX = mybir.AxisListType.X
AF = mybir.ActivationFunctionType
ALU = mybir.AluOpType

D = 1024
NCH = 8
H = 16
MF = 266
MCS = [(0, 128), (128, 256), (256, 266)]
EPS = 1e-4
LNEPS = 1e-5
DN = float(64) ** -0.25
E = 8

ENGS = ("sync", "tensor", "vector", "scalar", "gpsimd")


class Sched:
    def __init__(self):
        self.q = {e: [] for e in ENGS}
        self.n = {}
        self.lastwait = {}

    def c(self, sem):
        return self.n.get(sem, 0)

    def bump(self, sem, n):
        self.n[sem] = self.c(sem) + n

    DMASEMS = ("xd", "wd0", "wd1", "wg0", "wg1", "od", "gd", "cc")

    def do(self, eng, fn, waits=(), inc=None, incn=1, drain=False):
        waits = list(waits)
        if inc in self.DMASEMS:
            waits.append((inc, self.c(inc)))
        w = []
        for s, v in waits:
            if v <= 0:
                continue
            if s in self.DMASEMS:
                v = self.c(s)
            assert v <= self.c(s), f"forward wait {s} {v} > {self.c(s)}"
            key = (eng, s)
            if self.lastwait.get(key, -1) < v:
                w.append((s, v))
                self.lastwait[key] = v
        self.q[eng].append((fn, tuple(w), inc, incn, drain))
        if inc is not None:
            self.n[inc] = self.c(inc) + incn
        return self.c(inc) if inc is not None else None


def build(T, groups, part="full", stop_after=None):
    nT = T // 128
    nB = T // 512
    nc = bass.Bass("TRN2", target_bir_lowering=False,
                   detect_race_conditions=(part != "fused"))

    x_in = nc.dram_tensor("x_in", [T, D], F32, kind="ExternalInput").ap()
    w_q = nc.dram_tensor("w_q", [D, D], F32, kind="ExternalInput").ap()
    w_k = nc.dram_tensor("w_k", [D, D], F32, kind="ExternalInput").ap()
    w_v = nc.dram_tensor("w_v", [D, D], F32, kind="ExternalInput").ap()
    w_o = nc.dram_tensor("w_o", [D, D], F32, kind="ExternalInput").ap()
    b_o = nc.dram_tensor("b_o", [D], F32, kind="ExternalInput").ap()
    prj = nc.dram_tensor("prj", [MF, 64], F32, kind="ExternalInput").ap()
    w_g = nc.dram_tensor("w_g", [D, E], F32, kind="ExternalInput").ap()
    b_g = nc.dram_tensor("b_g", [E], F32, kind="ExternalInput").ap()
    w_1 = nc.dram_tensor("w_1", [E, D, D], F32, kind="ExternalInput").ap()
    b_1 = nc.dram_tensor("b_1", [E, D], F32, kind="ExternalInput").ap()
    w_2 = nc.dram_tensor("w_2", [E, D, D], F32, kind="ExternalInput").ap()
    b_2 = nc.dram_tensor("b_2", [E, D], F32, kind="ExternalInput").ap()
    out = None
    if part != "a":
        out = nc.dram_tensor("out", [T, D], F32, kind="ExternalOutput").ap()
    c_out = s_out = c_in = s_in = None
    if part == "a":
        c_out = nc.dram_tensor("c_out", [65, H * 267], F32, kind="ExternalOutput").ap()
        s_out = nc.dram_tensor("s_out", [16, 1], F32, kind="ExternalOutput").ap()
    G = len(groups[0])
    PAIRS = max(1, len(groups[0]) // 2)
    NSL = 2 * PAIRS
    if part == "b":
        c_in = nc.dram_tensor("c_in", [G, 65, H * 267], F32, kind="ExternalInput").ap()
        s_in = nc.dram_tensor("s_in", [1, G * 16], F32, kind="ExternalInput").ap()
    x_far = slot_in = d_cslab = d_stab = d_flag = None
    if part == "fused":
        if PAIRS == 2:
            x_far = nc.dram_tensor("x_far", [T, D], F32, kind="ExternalInput").ap()
        slot_in = nc.dram_tensor("slot_in", [1, 16], mybir.dt.int32, kind="ExternalInput").ap()
        d_cslab = nc.dram_tensor("d_cslab", [NSL, 65, H * 267], F32, kind="Internal", addr_space="Shared").ap()
        d_stab = nc.dram_tensor("d_stab", [NSL, 16], F32, kind="Internal", addr_space="Shared").ap()
        d_flag = nc.dram_tensor("d_flag", [2, 16], mybir.dt.int32, kind="Internal", addr_space="Shared").ap()

    dr_diag = nc.dram_tensor("dr_diag", [H, T], F32, kind="Internal").ap()
    dr_rmax = nc.dram_tensor("dr_rmax", [H, T], F32, kind="Internal").ap()
    dr_ci = nc.dram_tensor("dr_ci", [65, H * 267], F32, kind="Internal").ap()
    dr_co = nc.dram_tensor("dr_co", [65, H * 267], F32, kind="Internal").ap()
    dr_si = nc.dram_tensor("dr_si", [1, H], F32, kind="Internal").ap()
    dr_so = nc.dram_tensor("dr_so", [1, H], F32, kind="Internal").ap()

    st = ExitStack()
    A = st.enter_context

    x_tm = A(nc.sbuf_tensor("x_tm", [128, nT * 1024], BF16))
    B_fm = A(nc.sbuf_tensor("B_fm", [128, NCH * T], BF16))
    C_big = A(nc.sbuf_tensor("C_big", [128, max(NCH * T + nT * 16 * 65, 32768)], BF16))
    D_du = A(nc.sbuf_tensor("D_du", [128, max(nT * 1024, 16384)], BF16))
    scr = A(nc.sbuf_tensor("scr", [128, 7168], F32))
    stag_w = A(nc.sbuf_tensor("stag_w", [128, 2048], F32))

    ident_b = A(nc.sbuf_tensor("ident_b", [128, 128], BF16))
    ident_f = A(nc.sbuf_tensor("ident_f", [128, 128], F32))
    ones_b = A(nc.sbuf_tensor("ones_b", [128, 128], BF16))
    onesf = A(nc.sbuf_tensor("onesf", [1, 128], F32))
    epsf = A(nc.sbuf_tensor("epsf", [1, 128], F32))
    projdup = A(nc.sbuf_tensor("projdup", [128, MF], BF16))
    projex = A(nc.sbuf_tensor("projex", [65, MF], BF16))
    projcb = A(nc.sbuf_tensor("projcb", [128, 64], BF16))
    diagTk = A(nc.sbuf_tensor("diagTk", [128, H * 16], F32))
    estab = A(nc.sbuf_tensor("estab", [128, H], F32))
    kp_sc = A(nc.sbuf_tensor("kp_sc", [128, 4 * 267], BF16))
    epsv = A(nc.sbuf_tensor("epsv", [128, 65], F32))
    vsn = A(nc.sbuf_tensor("vsn", [1, 65], F32))
    stabloc = A(nc.sbuf_tensor("stabloc", [16, 1], F32))
    stabrow = A(nc.sbuf_tensor("stabrow", [1, H], F32))
    stab4 = A(nc.sbuf_tensor("stab4", [1, 64], F32))
    slotv = A(nc.sbuf_tensor("slotv", [1, 16], mybir.dt.int32))
    magict = A(nc.sbuf_tensor("magict", [1, 16], mybir.dt.int32))
    flgt = A(nc.sbuf_tensor("flgt", [1, 32], mybir.dt.int32))
    maxk = A(nc.sbuf_tensor("maxk", [128, H], F32))
    rmaxsb = A(nc.sbuf_tensor("rmaxsb", [128, 32], F32))
    lnin = A(nc.sbuf_tensor("lnin", [128, 1], F32))
    lnout16 = A(nc.sbuf_tensor("lnout16", [128, 16], F32))
    bn6 = A(nc.sbuf_tensor("bn6", [128, 12], F32))
    agg2 = A(nc.sbuf_tensor("agg2", [128, 2], F32))
    rstd = A(nc.sbuf_tensor("rstd", [128, 1], F32))
    nbv = A(nc.sbuf_tensor("nbv", [128, 1], F32))
    lneps = A(nc.sbuf_tensor("lneps", [128, 1], F32))
    epsc = A(nc.sbuf_tensor("epsc", [128, 1], F32))
    b1T = A(nc.sbuf_tensor("b1T", [128, 64], F32))
    wgb = A(nc.sbuf_tensor("wgb", [128, 64], BF16))
    bgb = A(nc.sbuf_tensor("bgb", [1, 8], BF16))
    sel8 = A(nc.sbuf_tensor("sel8", [8, 8 * 128], BF16))
    oh_f = A(nc.sbuf_tensor("oh_f", [128, 8], F32))
    ohmax = A(nc.sbuf_tensor("ohmax", [128, 1], F32))

    pb = [A(nc.psum_tensor(f"pb{i}", [128, 512], F32)) for i in range(8)]
    pbb = [p[:, :].bitcast(BF16) for p in pb]

    sems = {}
    for s in ("xd", "wd0", "wd1", "wg0", "wg1", "od", "gd", "cc", "pe", "ac", "dv", "gp", "pd"):
        sems[s] = A(nc.semaphore(s))

    nc.const_aps.register(F32, EPS, epsc[:, 0:1])

    S = Sched()
    MAGIC_BITS = 0x5CA1AB1E
    rslot_sync = rslot_gp = None
    if part == "fused":
        rslot_sync = nc.sync.alloc_register("rslot_s")
        rslot_gp = nc.gpsimd.alloc_register("rslot_g")
        S.do("sync", lambda e: e.dma_start(out=slotv[:], in_=slot_in), waits=[], inc="xd", incn=16)
        S.do("sync", (lambda: (lambda e: e.reg_load(rslot_sync, slotv[0:1, 0:1])))(),
             waits=[("xd", S.c("xd"))], inc=None)
        S.do("gpsimd", (lambda: (lambda e: e.reg_load(rslot_gp, slotv[0:1, 0:1])))(),
             waits=[("xd", S.c("xd"))], inc="gp")

    SQF = [scr[:, 0:512], scr[:, 512:1024]]
    SQB = [scr[:, 0:256].bitcast(BF16),
           scr[:, 512:768].bitcast(BF16)]
    DROW = [scr[0:1, 1024:1024 + T], scr[64:65, 1024:1024 + T]]
    B16RT = scr[0:16, 1024:3072]
    RM64S = [scr[64:65, 1024:1024 + T], scr[64:65, 3072:3072 + T]]
    NSB = [scr[0:64, 1024:1536], scr[0:64, 1536:2048], scr[0:64, 2048:2560], scr[0:64, 2560:3072]]
    PROJSTG = scr[0:128, 1024:1088]
    cstg = scr[0:65, 6144:6678]
    cstg4 = scr[0:65, 1024:1024 + 4 * 267]
    cstg4s = [scr[0:65, 1024:1024 + 4 * 267], scr[0:65, 3606:3606 + 4 * 267]]
    ZF = [scr[:, 0:1024], scr[:, 1024:2048]]
    OUTS = [scr[:, 2048:3072], scr[:, 3072:4096]]
    BOBC = scr[:, 4096:5120]
    DINVB = [scr[64:65, 6144 + i * 256:6144 + (i + 1) * 256].bitcast(BF16) for i in (0, 1)]
    NUMSC = [scr[0:64, 4608 + i * 256:4608 + (i + 1) * 256].bitcast(BF16) for i in (0, 1)]
    BOSB = scr[0:1, 5120:6144]
    LNR = [scr[64:65, 5120 + i * 512:5120 + (i + 1) * 512] for i in (0, 1)]
    B2B = scr[0:8, 5120:5632].bitcast(BF16)
    BSTG = scr[0:8, 6144:7168]
    hg_b = scr[:, 6656:6912].bitcast(BF16)
    ohT = scr[0:8, 2048:2048 + T // 2].bitcast(BF16)
    WGSTG = scr[0:128, 6144:6208]
    BGSTG = scr[0:1, 6208:6216]


    k_fm = C_big[:, 0:NCH * T]
    k_fm3 = C_big[:, 0:NCH * T].rearrange("p (c t) -> p c t", c=NCH)
    VOFF = NCH * T
    q_ex = C_big[0:65, 0:16 * T]
    HM = [C_big[:, 24576:28672], C_big[:, 28672:32768]]
    def ring_c(s):
        return C_big[:, s * 8192:(s + 1) * 8192]

    attn_ring = [D_du[:, 0:8192], D_du[:, 8192:16384]]
    QP = D_du[:, 0:3072]
    CAUG = D_du[:, 8192:11392]
    y_acc = D_du
    B3 = B_fm.rearrange("p (c t) -> p c t", c=NCH)

    def init_g(g):
        g.memset(ident_b[:], 0.0)
        g.memset(ident_f[:], 0.0)
        g.memset(ones_b[:], 1.0)
        g.memset(onesf[:], 1.0)
        g.memset(epsf[:], EPS)
        g.memset(lneps[:], LNEPS)
        g.memset(epsc[:], EPS)
        g.memset(maxk[:], -3e38)
        g.memset(sel8[:], 0.0)
        g.memset(projex[64:65, :], 1.0)
        g.memset(kp_sc[:].rearrange("p (s c) -> p s c", c=267)[:, :, 266:267], 1.0)
        g.memset(C_big[:, VOFF:VOFF + nT * 16 * 65].rearrange("p (a c) -> p a c", c=65)[:, :, 64:65], 1.0)
        g.drain()
        g.affine_select(out=ident_b[:], in_=ident_b[:], compare_op=ALU.not_equal,
                        fill=1.0, base=0, pattern=[[-1, 128]], channel_multiplier=1)
        g.affine_select(out=ident_f[:], in_=ident_f[:], compare_op=ALU.not_equal,
                        fill=1.0, base=0, pattern=[[-1, 128]], channel_multiplier=1)
        ins = g.affine_select(out=sel8[:].rearrange("p (e n) -> p e n", e=8),
                              in_=sel8[:].rearrange("p (e n) -> p e n", e=8),
                              compare_op=ALU.not_equal, fill=1.0, base=0,
                              pattern=[[-1, 8], [0, 128]], channel_multiplier=1)
        g.drain()
        return ins
    S.do("gpsimd", init_g, inc="gp")
    GP_INIT = S.c("gp")


    def _probe_finish():
        S.do("sync", lambda e: e.dma_start(out=s_out, in_=stabloc[:]),
             waits=[("gp", GP_INIT)], inc="od", incn=16)
        S.do("sync", lambda e: e.wait_ge(sems["od"], S.c("od")), waits=[])
        with nc.Block() as blk:
            for engname in ENGS:
                steps = S.q[engname]
                def body(eng, steps=steps):
                    for fn, waits, inc, incn, drain in steps:
                        for sname, v in waits:
                            eng.wait_ge(sems[sname], v)
                        if drain:
                            eng.drain()
                        ins = fn(eng)
                        if inc is not None:
                            ins.then_inc(sems[inc], incn)
                getattr(blk, engname)(body)
        st.close()
        return nc

    if stop_after == "init":
        return _probe_finish()

    for pi, (r0, r1) in enumerate([(0, 128), (128, 256), (256, 266)]):
        rows = r1 - r0
        S.do("sync", (lambda r0=r0, rows=rows: (lambda e: e.dma_start(
            out=PROJSTG[0:rows, :], in_=prj[r0:r0 + rows, :])))(),
            waits=[("gp", S.c("gp"))], inc="xd", incn=16)
        S.do("gpsimd", (lambda rows=rows: (lambda e: e.tensor_copy(projcb[0:rows, :], PROJSTG[0:rows, :])))(),
             waits=[("xd", S.c("xd")), ("pe", S.c("pe"))], inc="gp", drain=True)
        S.do("tensor", (lambda rows=rows: (lambda e: e.transpose(
            pbb[6][0:64, 0:rows], projcb[0:rows, :], ident_b[0:rows, 0:rows])))(),
            waits=[("gp", S.c("gp")), ("dv", S.c("dv"))], inc="pe")
        S.do("vector", (lambda r0=r0, rows=rows: (lambda e: e.tensor_copy(
            projdup[0:64, r0:r0 + rows], pbb[6][0:64, 0:rows])))(),
            waits=[("pe", S.c("pe"))], inc="dv")
    S.do("vector", lambda e: e.tensor_copy(projex[0:64, :], projdup[0:64, :]),
         waits=[("dv", S.c("dv"))], inc="dv", drain=True)
    S.do("gpsimd", lambda e: e.dma_start(out=projdup[64:128, :], in_=projdup[0:64, :]),
         waits=[("dv", S.c("dv"))], inc="gd", incn=16)
    GD_PROJ = S.c("gd")
    DV_PROJ = S.c("dv")

    def emit_xload(x_src, extra_pe=0, base_ac=None, split=False, extra_w0=None):
        xcast = {}
        xtp_dv = {}
        xdsem = {}
        for tau in range(nT):
            if tau % 2 == 0:
                w = [("ac", xcast[tau - 3])] if tau >= 4 else [("ac", S.c("ac") if base_ac is None else base_ac)]
                if tau == 0 and extra_w0:
                    w = w + extra_w0
                q, sem = (("gpsimd", "wg0") if (split and (tau // 2) % 2 == 1) else ("sync", "xd"))
                S.do(q, (lambda tau=tau, x_src=x_src: (lambda e: e.dma_start(
                    out=scr[:, 2048 + (tau % 4) * 1024:2048 + (tau % 4) * 1024 + 2048]
                    .rearrange("p (a d) -> p a d", a=2),
                    in_=x_src.rearrange("(q p) d -> p q d", p=128)[:, tau:tau + 2, :])))(),
                    waits=w, inc=sem, incn=16)
                xdsem[tau] = xdsem[tau + 1] = (sem, S.c(sem))
            wc = [xdsem[tau]]
            if tau == 0 and extra_pe:
                wc.append(("pe", extra_pe))
            xcast[tau] = S.do("scalar", (lambda tau=tau: (lambda e: e.copy(
                x_tm[:, tau * 1024:(tau + 1) * 1024],
                scr[:, 2048 + (tau % 4) * 1024:2048 + (tau % 4) * 1024 + 1024])))(),
                waits=wc, inc="ac")
            bank = pbb[6 + tau % 2]
            wtp = [("ac", xcast[tau]), ("gp", GP_INIT), ("dv", DV_PROJ)]
            if tau < 2 and extra_pe:
                wtp.append(("pe", extra_pe))
                wtp.append(("dv", S.c("dv")))
            if tau >= 2:
                wtp.append(("dv", xtp_dv[tau - 2]))
            for c in range(NCH):
                S.do("tensor", (lambda tau=tau, c=c, bank=bank: (lambda e: e.transpose(
                    bank[:, c * 128:(c + 1) * 128],
                    x_tm[:, tau * 1024 + c * 128:tau * 1024 + (c + 1) * 128], ident_b[:])))(),
                    waits=wtp if c == 0 else [], inc="pe" if c == NCH - 1 else None)
            xtp_dv[tau] = S.do("vector", (lambda tau=tau, bank=bank: (lambda e: e.tensor_copy(
                B3[:, :, tau * 128:(tau + 1) * 128],
                bank[:, 0:1024].rearrange("p (c n) -> p c n", c=NCH))))(),
                waits=[("pe", S.c("pe"))], inc="dv")
        return S.c("dv")

    XFM_DV = 0
    if part == "b":
        XFM_DV = emit_xload(x_in)
    if stop_after == "xfm":
        XFM_DV = emit_xload(x_in)
        return _probe_finish()

    castc = [("gp", 0), ("gp", 0)]
    def load_weight(hbm_ap, dest, free_pe, queue="sync", cast="gpsimd", pieces=None):
        src = hbm_ap.rearrange("(c p) d -> p c d", p=128)
        csem = "gp" if cast == "gpsimd" else "ac"
        done = None
        for idx, piece in enumerate(range(8) if pieces is None else pieces):
            slot = idx % 2
            wsem = "wd0" if slot == 0 else "wd1"
            w = [castc[slot]]
            if idx < 2 and free_pe:
                w.append(("pe", free_pe))
            S.do(queue, (lambda piece=piece, slot=slot, src=src: (lambda e: e.dma_start(
                out=stag_w[:, slot * 1024:(slot + 1) * 1024],
                in_=src[:, piece, :])))(),
                waits=w, inc=wsem, incn=16)
            wcast = [(wsem, S.c(wsem))]
            if idx < 2 and free_pe:
                wcast.append(("pe", free_pe))
            if cast == "gpsimd":
                cnt = S.do("gpsimd", (lambda piece=piece, slot=slot, dest=dest: (lambda e: e.tensor_copy(
                    dest[:, piece * 1024:(piece + 1) * 1024], stag_w[:, slot * 1024:(slot + 1) * 1024])))(),
                    waits=wcast, inc="gp")
            else:
                cnt = S.do("scalar", (lambda piece=piece, slot=slot, dest=dest: (lambda e: e.copy(
                    dest[:, piece * 1024:(piece + 1) * 1024], stag_w[:, slot * 1024:(slot + 1) * 1024])))(),
                    waits=wcast, inc="ac", drain=True)
            done = (csem, cnt)
            castc[slot] = done
        return done

    K_PROJ_PE = 0
    DIAGT_AC = 0
    GD_B16 = 0
    gp_wk = gp_wv = ("gp", 0)
    if part in ("a", "full"):
        kpasses = ["own"]
    elif part == "fused":
        kpasses = (["far"] if PAIRS == 2 else []) + ["own"]
    else:
        kpasses = []
    for kpass in kpasses:
      ctx_dv = {}
      cdma = {}
      if kpass == kpasses[0]:
          AC0 = S.c("ac")
          gp_wk = load_weight(w_k, attn_ring[0], 0, cast="scalar")
          XFM_DV = emit_xload(x_far if kpass == "far" else x_in, base_ac=AC0, split=True)
          XFM_AC = S.c("ac")
          gp_wv = load_weight(w_v, attn_ring[1], 0)
      else:
          XFM_DV = emit_xload(x_in, extra_pe=S.c("pe"), base_ac=XFM_AC,
                              extra_w0=[("ac", DIAGT_AC), ("gd", S.c("gd"))])
      if kpass != kpasses[0]:
          S.do("gpsimd", lambda e: e.memset(maxk[:], -3e38),
               waits=[("dv", S.c("dv"))], inc="gp", drain=True)
      kev = {}
      kdiag_ac = {}
      drow_dma = {}
      for c in range(NCH):
          for tb in range(nB):
              u = c * nB + tb
              bank = u % 4
              wmm = [gp_wk, ("dv", XFM_DV)]
              if u < 4:
                  wmm.append(("ac", S.c("ac")))
                  wmm.append(("dv", S.c("dv")))
              if u >= 4:
                  wmm.append(("ac", kev[u - 4]))
              for kc in range(NCH):
                  S.do("tensor", (lambda c=c, tb=tb, kc=kc, bank=bank: (lambda e: e.matmul(
                      pb[bank][:, :], attn_ring[0][:, kc * 1024 + c * 128:kc * 1024 + (c + 1) * 128],
                      B_fm[:, kc * T + tb * 512:kc * T + (tb + 1) * 512],
                      start=(kc == 0), stop=(kc == NCH - 1))))(),
                      waits=wmm if kc == 0 else [], inc="pe" if kc == NCH - 1 else None)
              pemm = S.c("pe")
              S.do("vector", (lambda c=c, tb=tb, bank=bank: (lambda e: e.tensor_scalar_mul(
                  k_fm3[:, c, tb * 512:(tb + 1) * 512], pb[bank][:, :], DN)))(),
                  waits=[("pe", pemm)], inc="dv", drain=True)
              KCP_DV = S.c("dv")
              S.do("scalar", (lambda bank=bank, u=u: (lambda e: e.activation(
                  SQB[u % 2], pb[bank][:, :], AF.Square, bias=0.0, scale=DN)))(),
                  waits=[("dv", KCP_DV)], inc="ac")
              kev[u] = S.c("ac")
              for hh in range(2):
                  h = 2 * c + hh
                  dbank = 4 + (h * nB + tb) % 2
                  wdg = [("ac", kev[u])]
                  pk = h * nB + tb - 4
                  if pk in kdiag_ac:
                      wdg.append(("ac", kdiag_ac[pk]))
                  tp = (64, 64) if hh == 1 else None
                  S.do("tensor", (lambda hh=hh, u=u, dbank=dbank, tp=tp: (lambda e: e.matmul(
                      pb[dbank][64:65, :] if tp else pb[dbank][0:1, :],
                      ones_b[hh * 64:hh * 64 + 64, 0:1], SQB[u % 2][hh * 64:hh * 64 + 64, :],
                      start=True, stop=True, tile_position=tp)))(),
                      waits=wdg, inc="pe")
                  wdr = [("pe", S.c("pe"))]
                  if tb == 0 and c >= 1:
                      wdr.append(("gd", drow_dma[2 * (c - 1) + hh]))
                  kdiag_ac[h * nB + tb] = S.do("scalar",
                      (lambda hh=hh, tb=tb, dbank=dbank: (lambda e: e.activation(
                          DROW[hh][:, tb * 512:(tb + 1) * 512],
                          pb[dbank][64:65, :] if hh == 1 else pb[dbank][0:1, :],
                          AF.Copy, bias=0.0, scale=-0.5)))(),
                      waits=wdr, inc="ac")
          for hh in range(2):
              h = 2 * c + hh
              drow_dma[h] = S.do("gpsimd", (lambda h=h, hh=hh: (lambda e: e.dma_start(
                  out=dr_diag[h:h + 1, :], in_=DROW[hh])))(),
                  waits=[("ac", kdiag_ac[h * nB + nB - 1])], inc="gd", incn=16)
      K_PROJ_PE = S.c("pe")
      if part != "a" and kpass == kpasses[-1]:
          gp_wq = load_weight(w_q, attn_ring[0], K_PROJ_PE)
      if stop_after == "kproj":
          return _probe_finish()

      vev = {}
      for tau in range(nT):
          for half in range(2):
              u = tau * 2 + half
              bank = u % 4
              wmm = [gp_wv, ("dv", XFM_DV)]
              if u < 4:
                  wmm.append(("ac", S.c("ac")))
                  wmm.append(("dv", S.c("dv")))
              if u >= 4:
                  wmm.append(("dv", vev[u - 4]))
              for kc in range(NCH):
                  S.do("tensor", (lambda tau=tau, half=half, kc=kc, bank=bank: (lambda e: e.matmul(
                      pb[bank][:, :], B_fm[:, kc * T + tau * 128:kc * T + (tau + 1) * 128],
                      attn_ring[1][:, kc * 1024 + half * 512:kc * 1024 + (half + 1) * 512],
                      start=(kc == 0), stop=(kc == NCH - 1))))(),
                      waits=wmm if kc == 0 else [], inc="pe" if kc == NCH - 1 else None)
              vev[u] = S.do("vector", (lambda tau=tau, half=half, bank=bank: (lambda e: e.tensor_copy(
                  C_big[:, VOFF + tau * 1040 + half * 520:VOFF + tau * 1040 + half * 520 + 520]
                  .rearrange("p (h d) -> p h d", h=8)[:, :, 0:64],
                  pb[bank][:, :].rearrange("p (h d) -> p h d", h=8))))(),
                  waits=[("pe", S.c("pe"))], inc="dv")
      V_DV = S.c("dv")
      if stop_after == "vproj":
          return _probe_finish()

      S.do("gpsimd", lambda e: e.dma_start(
          out=B16RT[0:nT, 0:H * 128].rearrange("q (h p) -> q h p", h=H),
          in_=dr_diag.rearrange("h (q p) -> q h p", p=128)),
          waits=[("gd", max(drow_dma.values()))], inc="gd", incn=16)
      GD_B16 = S.c("gd")
      dgt_ac = {}
      for h in range(H):
          tb6 = 6 + h % 2
          wtp = [("gd", GD_B16)]
          if h <= 1:
              wtp.append(("dv", S.c("dv")))
          if h >= 2:
              wtp.append(("ac", dgt_ac[h - 2]))
          S.do("tensor", (lambda h=h, tb6=tb6: (lambda e: e.transpose(
              pb[tb6][0:128, 0:nT], B16RT[0:nT, h * 128:(h + 1) * 128],
              ident_f[0:nT, 0:nT])))(),
              waits=wtp, inc="pe")
          dgt_ac[h] = S.do("scalar", (lambda h=h, tb6=tb6: (lambda e: e.copy(
              diagTk[:, h * 16:h * 16 + nT], pb[tb6][0:128, 0:nT])))(),
              waits=[("pe", S.c("pe"))], inc="ac")
      DIAGT_AC = S.c("ac")
      if stop_after == "diag":
          return _probe_finish()


      ctx_dv = {}
      cdma = {}
      kexp = {}
      ctx_pe = {}
      rmaxdv = {}
      LAG = 4
      NU = H * nT

      def kp_head(u):
          return u // nT, u % nT

      for u in range(NU + LAG):
          uc = u - LAG
          if uc >= 0:
              h, tau = kp_head(uc)
              cbank = 4 + h % 2
              wctx = [("ac", kexp[uc])]
              if tau == 0 and h >= 2:
                  wctx.append(("dv", ctx_dv[h - 2]))
              ctx_pe[uc] = S.do("tensor", (lambda h=h, tau=tau, cbank=cbank: (lambda e: e.matmul(
                  pb[cbank][0:65, 0:267],
                  C_big[:, VOFF + tau * 1040 + h * 65:VOFF + tau * 1040 + h * 65 + 65],
                  kp_sc[:, (tau % 4) * 267:(tau % 4) * 267 + 267],
                  start=(tau == 0), stop=(tau == nT - 1))))(),
                  waits=wctx, inc="pe")
              if tau == nT - 1:
                  S.do("scalar", (lambda h=h: (lambda e: e.activation(
                      lnout16[:, 0:nT], rmaxsb[:, (h % 2) * 16:(h % 2) * 16 + nT], AF.Ln, bias=0.0, scale=1.0)))(),
                      waits=[("dv", rmaxdv[uc])], inc="ac", drain=True)
                  S.do("vector", (lambda h=h: (lambda e: e.tensor_sub(
                      lnout16[:, 0:nT], lnout16[:, 0:nT], diagTk[:, h * 16:h * 16 + nT])))(),
                      waits=[("ac", S.c("ac"))], inc=None, drain=True)
                  S.do("vector", (lambda h=h: (lambda e: e.reduce_max(
                      lnin[:], lnout16[:, 0:nT], axis=AX)))(),
                      waits=[], inc=None, drain=True)
                  S.do("vector", (lambda h=h: (lambda e: e.tensor_max(
                      maxk[:, h:h + 1], maxk[:, h:h + 1], lnin[:])))(),
                      waits=[], inc="dv", drain=True)
                  wev = [("pe", S.c("pe"))]
                  if h >= 2:
                      wev.append(("gd" if part == "full" else ("wg1" if part == "fused" else "od"), cdma[h - 2]))
                  ctx_dv[h] = S.do("vector", (lambda h=h, cbank=cbank: (lambda e: e.tensor_copy(
                      cstg[:, (h % 2) * 267:(h % 2) * 267 + 267], pb[cbank][0:65, 0:267])))(),
                      waits=wev, inc="dv")
                  if part == "fused":
                      ofs = 0 if kpass == "own" else 1
                      def slabfn(e, h=h, ofs=ofs, sl=(h % 2)):
                          sap = cstg[:, sl * 267:sl * 267 + 267]
                          with e.If_eq(rslot_gp, 0):
                              e.dma_start(out=d_cslab[0 * PAIRS + ofs][:, h * 267:(h + 1) * 267],
                                          in_=sap).then_inc(sems["wg1"], 16)
                          with e.Else():
                              e.dma_start(out=d_cslab[1 * PAIRS + ofs][:, h * 267:(h + 1) * 267],
                                          in_=sap).then_inc(sems["wg1"], 16)
                          e.end_ifs()
                          return None
                      S.do("gpsimd", slabfn, waits=[("dv", ctx_dv[h]), ("wg1", S.c("wg1"))], inc=None)
                      S.bump("wg1", 16)
                      cdma[h] = S.c("wg1")
                  else:
                      cdst = dr_ci if part == "full" else c_out
                      csem = "gd" if part == "full" else "od"
                      cdma[h] = S.do("gpsimd" if part == "full" else "sync",
                          (lambda h=h, cdst=cdst: (lambda e: e.dma_start(
                          out=cdst[:, h * 267:(h + 1) * 267], in_=cstg[:, (h % 2) * 267:(h % 2) * 267 + 267])))(),
                          waits=[("dv", ctx_dv[h])], inc=csem, incn=16)
          if u < NU:
              h, tau = kp_head(u)
              hb = (h % 2) * 64
              hc = h // 2
              dbank = u % 4
              wmm = []
              if u < 4:
                  wmm = [("ac", DIAGT_AC), ("dv", V_DV), ("gd", GD_PROJ)]
              if u >= 4:
                  wmm.append(("ac", kexp[u - 4]))
              S.do("tensor", (lambda hb=hb, hc=hc, tau=tau, dbank=dbank: (lambda e: e.matmul(
                  pb[dbank][0:128, 0:266], k_fm3[hb:hb + 64, hc, tau * 128:(tau + 1) * 128],
                  projdup[hb:hb + 64, :], start=True, stop=True)))(),
                  waits=wmm, inc="pe")
              pedash = S.c("pe")
              wexp = [("pe", pedash)]
              if u >= 4:
                  wexp.append(("pe", ctx_pe.get(u - LAG, 0)))
                  wexp.append(("dv", rmaxdv[u - 4]))
              kexp[u] = S.do("scalar", (lambda h=h, tau=tau, dbank=dbank: (lambda e: e.activation(
                  kp_sc[:, (tau % 4) * 267:(tau % 4) * 267 + 266], pb[dbank][0:128, 0:266], AF.Exp,
                  bias=diagTk[:, h * 16 + tau:h * 16 + tau + 1], scale=1.0)))(),
                  waits=wexp, inc="ac")
              rmaxdv[u] = S.do("vector", (lambda h=h, tau=tau: (lambda e: e.reduce_max(
                  rmaxsb[:, (h % 2) * 16 + tau % 16:(h % 2) * 16 + tau % 16 + 1],
                  kp_sc[:, (tau % 4) * 267:(tau % 4) * 267 + 266], axis=AX)))(),
                  waits=[("ac", kexp[u])], inc="dv", drain=True)
      S.do("tensor", lambda e: e.transpose(pb[6][0:16, 0:128], maxk[:], ident_f[:]),
           waits=[("dv", S.c("dv")), ("ac", DIAGT_AC)], inc="pe")
      S.do("vector", lambda e: e.reduce_max(stabloc[:], pb[6][0:16, 0:128], axis=AX),
           waits=[("pe", S.c("pe"))], inc="dv", drain=True)
      if part == "fused":
          ofs = 0 if kpass == "own" else 1
          def stabfn(e, ofs=ofs):
              with e.If_eq(rslot_gp, 0):
                  e.dma_start(out=d_stab[0 * PAIRS + ofs][None, :].rearrange("o h -> h o"),
                              in_=stabloc[:]).then_inc(sems["wg1"], 16)
              with e.Else():
                  e.dma_start(out=d_stab[1 * PAIRS + ofs][None, :].rearrange("o h -> h o"),
                              in_=stabloc[:]).then_inc(sems["wg1"], 16)
              e.end_ifs()
              return None
          S.do("gpsimd", stabfn, waits=[("dv", S.c("dv")), ("wg1", S.c("wg1"))], inc=None)
          S.bump("wg1", 16)

    if part != "a" and not kpasses:
        gp_wq = load_weight(w_q, attn_ring[0], K_PROJ_PE)

    if part == "fused":
        S.do("gpsimd", lambda e: e.memset(magict[:], MAGIC_BITS),
             waits=[], inc="gp", drain=True)
        def flagfn(e):
            with e.If_eq(rslot_gp, 0):
                e.dma_start(out=d_flag[0], in_=magict[:]).then_inc(sems["gd"], 16)
            with e.Else():
                e.dma_start(out=d_flag[1], in_=magict[:]).then_inc(sems["gd"], 16)
            e.end_ifs()
            return None
        S.do("gpsimd", flagfn,
             waits=[("wg1", S.c("wg1")), ("gp", S.c("gp")), ("gd", S.c("gd"))], inc=None)
        S.bump("gd", 16)
        GD_FLAG = S.c("gd")
        def pollfn(e):
            with e.register("cnt") as rcnt, e.register("pdc") as rpd, e.register("fv") as rfv:
                e.reg_mov(rcnt, 0)
                e.reg_mov(rpd, 0)
                def cond():
                    return e.compute_val(rcnt) < 2
                with e.While(cond):
                    e.reg_add(rpd, rpd, 16)
                    e.dma_start(out=flgt[:], in_=d_flag.rearrange("a b -> (a b)")[None, :]).then_inc(sems["pd"], 16)
                    e.wait_ge(sems["pd"], rpd)
                    e.reg_mov(rcnt, 0)
                    for c in range(2):
                        e.reg_load(rfv, flgt[0:1, c * 16:c * 16 + 1])
                        with e.If_eq(rfv, MAGIC_BITS):
                            e.reg_add(rcnt, rcnt, 1)
                        e.end_ifs()
            return e.nop()
        S.do("gpsimd", pollfn, waits=[("gd", GD_FLAG)], inc="gp")
        BARRIER_GP = S.c("gp")

    if part == "a":
        S.do("sync", lambda e: e.dma_start(out=s_out, in_=stabloc[:]),
             waits=[("dv", S.c("dv"))], inc="od", incn=16)
        S.do("sync", lambda e: e.wait_ge(sems["od"], S.c("od")), waits=[("gd", S.c("gd"))])
        with nc.Block() as blk:
            for engname in ENGS:
                steps = S.q[engname]
                def body(eng, steps=steps):
                    for fn, waits, inc, incn, drain in steps:
                        for sname, v in waits:
                            eng.wait_ge(sems[sname], v)
                        if drain:
                            eng.drain()
                        ins = fn(eng)
                        if inc is not None:
                            assert ins is not None
                            ins.then_inc(sems[inc], incn)
                getattr(blk, engname)(body)
        st.close()
        return nc
    if part == "full":
        S.do("gpsimd", lambda e: e.dma_start(out=dr_si.rearrange("o h -> h o"), in_=stabloc[:]),
             waits=[("dv", S.c("dv"))], inc="gd", incn=16)
        S.do("gpsimd", lambda e: e.collective_compute("AllReduce", ALU.max, groups, [dr_si], [dr_so]),
             waits=[("gd", S.c("gd"))], inc="cc", incn=16)
        S.do("gpsimd", lambda e: e.collective_compute("AllReduce", ALU.add, groups, [dr_ci], [dr_co]),
             waits=[("gd", max(cdma.values())), ("cc", S.c("cc"))], inc="cc", incn=16)
        CC_DONE = S.c("cc")
        S.do("gpsimd", lambda e: e.dma_start(out=stabrow[:], in_=dr_so),
             waits=[("cc", CC_DONE)], inc="gd", incn=16)
    qev = {}
    qdiag_ac = {}
    for h in range(H):
        for tb in range(nB):
            u = h * nB + tb
            bank = u % 4
            wmm = [gp_wq, ("dv", XFM_DV)]
            if u < 4:
                wmm.append(("ac", S.c("ac")))
                wmm.append(("dv", S.c("dv")))
            if u >= 4:
                wmm.append(("ac", qev[u - 4]))
            for kc in range(NCH):
                S.do("tensor", (lambda h=h, tb=tb, kc=kc, bank=bank: (lambda e: e.matmul(
                    pb[bank][0:64, :], attn_ring[0][:, kc * 1024 + h * 64:kc * 1024 + h * 64 + 64],
                    B_fm[:, kc * T + tb * 512:kc * T + (tb + 1) * 512],
                    start=(kc == 0), stop=(kc == NCH - 1))))(),
                    waits=wmm if kc == 0 else [], inc="pe" if kc == NCH - 1 else None)
            peq = S.c("pe")
            S.do("vector", (lambda h=h, tb=tb, bank=bank: (lambda e: e.tensor_scalar_mul(
                q_ex[0:64, h * T + tb * 512:h * T + (tb + 1) * 512], pb[bank][0:64, :], DN)))(),
                waits=[("pe", peq)], inc="dv", drain=True)
            QCP_DV = S.c("dv")
            S.do("scalar", (lambda bank=bank, u=u: (lambda e: e.activation(
                SQB[u % 2][0:64, :], pb[bank][0:64, :], AF.Square, bias=0.0, scale=DN)))(),
                waits=[("dv", QCP_DV)], inc="ac")
            qev[u] = S.c("ac")
            dbank = 4 + u % 2
            wdg = [("ac", qev[u])]
            if u < 2:
                wdg.append(("ac", S.c("ac")))
                wdg.append(("dv", S.c("dv")))
            if u - 4 in qdiag_ac:
                wdg.append(("ac", qdiag_ac[u - 4]))
            S.do("tensor", (lambda u=u, dbank=dbank: (lambda e: e.matmul(
                pb[dbank][64:65, :], ones_b[0:64, 0:1], SQB[u % 2][0:64, :],
                start=True, stop=True, tile_position=(0, 64))))(),
                waits=wdg, inc="pe")
            qdiag_ac[u] = S.do("scalar", (lambda h=h, tb=tb, dbank=dbank: (lambda e: e.activation(
                q_ex[64:65, h * T + tb * 512:h * T + (tb + 1) * 512], pb[dbank][64:65, :],
                AF.Copy, bias=0.0, scale=-0.5)))(),
                waits=[("pe", S.c("pe"))], inc="ac")
    QPROJ_AC = S.c("ac")
    QPROJ_PE = S.c("pe")
    if part in ("b", "fused"):
        CC_DONE = 0
        GX = G if part == "b" else NSL
        if part == "b":
            S.do("sync", lambda e: e.dma_start(out=stab4[0:1, 0:GX * 16], in_=s_in), waits=[], inc="xd", incn=16)
        else:
            S.do("sync", lambda e: e.dma_start(out=stab4[0:1, 0:GX * 16],
                                               in_=d_stab.rearrange("a b -> (a b)")[None, :]),
                 waits=[("gp", BARRIER_GP)], inc="xd", incn=16)
        n = GX
        first = True
        while n > 1:
            half = n // 2
            S.do("vector", (lambda half=half: (lambda e: e.tensor_max(
                stab4[0:1, 0:half * 16].rearrange("o (a h) -> o a h", h=16),
                stab4[0:1, 0:half * 16].rearrange("o (a h) -> o a h", h=16),
                stab4[0:1, half * 16:2 * half * 16].rearrange("o (a h) -> o a h", h=16))))(),
                waits=[("xd", S.c("xd"))] if first else [], inc=None, drain=True)
            first = False
            n = half
        S.do("vector", lambda e: e.tensor_copy(stabrow[:], stab4[0:1, 0:16]),
             waits=[], inc="dv", drain=True)
    gp_woA = load_weight(w_o, attn_ring[0], QPROJ_PE, cast="scalar", pieces=(3, 4, 5, 6, 7))
    S.do("tensor", lambda e: e.matmul(pb[4][0:128, 0:16], onesf[:], stabrow[:], start=True, stop=True),
         waits=[("gd", S.c("gd")), ("ac", S.c("ac")), ("dv", S.c("dv")), ("gp", GP_INIT)], inc="pe")
    S.do("scalar", lambda e: e.activation(estab[:], pb[4][0:128, 0:16], AF.Exp, bias=0.0, scale=-1.0),
         waits=[("pe", S.c("pe"))], inc="ac")
    ESTAB_AC = S.c("ac")
    caug_dv = {}
    cadd_dv = {}
    for h in range(H):
        slot = h % 2
        c4 = cstg4s[h % 2]
        wdma = [("cc", CC_DONE)] if part == "full" else []
        if part == "full":
            if h >= 2:
                wdma.append(("dv", caug_dv[h - 2]))
            S.do("gpsimd", (lambda h=h, slot=slot: (lambda e: e.dma_start(
                out=cstg[:, slot * 267:slot * 267 + 267], in_=dr_co[:, h * 267:(h + 1) * 267])))(),
                waits=wdma, inc="gd", incn=16)
        else:
            GX = G if part == "b" else NSL
            CSRC = c_in if part == "b" else d_cslab
            if h >= 2:
                wdma.append(("dv", cadd_dv[h - 2]))
            else:
                wdma.append(("ac", ESTAB_AC))
            if part == "fused" and h == 0:
                wdma.append(("gp", BARRIER_GP))
            S.do("gpsimd", (lambda h=h, GX=GX, CSRC=CSRC, c4=c4: (lambda e: e.dma_start(
                out=c4[0:65, 0:GX * 267].rearrange("p (g n) -> p g n", g=GX),
                in_=CSRC[:, :, h * 267:(h + 1) * 267].rearrange("g p n -> p g n"))))(),
                waits=wdma, inc="gd", incn=16)
            n = GX
            first = True
            while n > 2:
                half = n // 2
                S.do("vector", (lambda half=half, c4=c4: (lambda e: e.tensor_add(
                    c4[0:65, 0:half * 267].rearrange("p (g n) -> p g n", n=267),
                    c4[0:65, 0:half * 267].rearrange("p (g n) -> p g n", n=267),
                    c4[0:65, half * 267:2 * half * 267].rearrange("p (g n) -> p g n", n=267))))(),
                    waits=[("gd", S.c("gd"))] if first else [], inc=None, drain=True)
                first = False
                n = half
            S.do("vector", (lambda slot=slot, first=first, c4=c4: (lambda e: e.tensor_add(
                cstg[:, slot * 267:slot * 267 + 267], c4[0:65, 0:267], c4[0:65, 267:534])))(),
                waits=[("gd", S.c("gd"))] if first else [], inc="dv", drain=True)
            cadd_dv[h] = S.c("dv")
        csrc_w = ("gd", S.c("gd")) if part == "full" else ("dv", S.c("dv"))
        tbank = 6 + h % 2
        wtp = [csrc_w, ("ac", ESTAB_AC)]
        if h < 2:
            wtp.append(("dv", S.c("dv")))
        if h >= 2:
            wtp.append(("dv", caug_dv[h - 2]))
        for i, (m0, m1) in enumerate(MCS):
            S.do("tensor", (lambda slot=slot, i=i, m0=m0, m1=m1, tbank=tbank: (lambda e: e.transpose(
                pb[tbank][0:m1 - m0, i * 65:(i + 1) * 65], cstg[:, slot * 267 + m0:slot * 267 + m1],
                ident_f[0:65, 0:65])))(),
                waits=wtp if i == 0 else [], inc=None)
        S.do("tensor", (lambda slot=slot, tbank=tbank: (lambda e: e.transpose(
            pb[tbank][0:1, 195:260], cstg[:, slot * 267 + 266:slot * 267 + 267], ident_f[0:65, 0:65])))(),
            waits=[], inc="pe")
        petp = S.c("pe")
        S.do("vector", (lambda tbank=tbank: (lambda e: e.tensor_copy(vsn[:], pb[tbank][0:1, 195:260])))(),
             waits=[("pe", petp)], inc="dv", drain=True)
        ebank = 2 + h % 2
        S.do("tensor", (lambda ebank=ebank: (lambda e: e.matmul(
            pb[ebank][0:128, 0:65], epsf[:], vsn[:], start=True, stop=True)))(),
            waits=[("dv", S.c("dv")), ("ac", QPROJ_AC)], inc="pe")
        S.do("vector", (lambda ebank=ebank: (lambda e: e.tensor_copy(epsv[:], pb[ebank][0:128, 0:65])))(),
             waits=[("pe", S.c("pe"))], inc="dv", drain=True)
        for i, (m0, m1) in enumerate(MCS):
            rows = m1 - m0
            caug_dv[h] = S.do("vector", (lambda h=h, i=i, rows=rows, tbank=tbank: (lambda e: e.scalar_tensor_tensor(
                CAUG[0:rows, h * 195 + i * 65:h * 195 + (i + 1) * 65],
                pb[tbank][0:rows, i * 65:(i + 1) * 65], estab[0:rows, h:h + 1],
                epsv[0:rows, :], ALU.mult, ALU.add)))(),
                waits=[], inc="dv", drain=True)
    CAUG_DV = S.c("dv")

    rmx_dma = {}
    p1dv = {}
    for h in range(H):
        for tau in range(nT):
            u = h * nT + tau
            dbank = u % 4
            wmm = [("ac", QPROJ_AC)]
            if u == 0:
                wmm.append(("gd", GD_B16))
                wmm.append(("dv", S.c("dv")))
            if u >= 4:
                wmm.append(("dv", p1dv[u - 4]))
            S.do("tensor", (lambda h=h, tau=tau, dbank=dbank: (lambda e: e.matmul(
                pb[dbank][0:128, 0:266], q_ex[:, h * T + tau * 128:h * T + (tau + 1) * 128],
                projex[:], start=True, stop=True)))(),
                waits=wmm, inc="pe")
            p1dv[u] = S.do("vector", (lambda tau=tau, dbank=dbank: (lambda e: e.reduce_max(
                rmaxsb[:, tau:tau + 1], pb[dbank][0:128, 0:266], axis=AX)))(),
                waits=[("pe", S.c("pe"))], inc="dv")
        wtp = [("dv", S.c("dv"))]
        if h >= 1:
            wtp.append(("gd", rmx_dma[h - 1]))
        S.do("tensor", (lambda: (lambda e: e.transpose(
            pb[6][0:nT, 128:256], rmaxsb[:, 0:nT], ident_f[:])))(),
            waits=wtp, inc="pe")
        S.do("vector", lambda e: e.tensor_copy(B16RT[0:nT, 0:128], pb[6][0:nT, 128:256]),
             waits=[("pe", S.c("pe"))], inc="dv", drain=True)
        rmx_dma[h] = S.do("gpsimd", (lambda h=h: (lambda e: e.dma_start(
            out=dr_rmax[h:h + 1, :].rearrange("o (q p) -> (o q) p", p=128), in_=B16RT[0:nT, 0:128])))(),
            waits=[("dv", S.c("dv"))], inc="gd", incn=16)
    P1_PE = S.c("pe")

    statq_dv = {}
    qexp = {}
    qexp_dv = {}
    st5ev = {}
    st5ac = {}
    numsc_dv = {}
    nsbdv = {}

    statq_dma = {}

    def emit_statq_dma(h):
        wrb = [("gd", rmx_dma[h])]
        if h >= 2:
            wrb.append(("dv", statq_dv[h - 2]))
        statq_dma[h] = S.do("gpsimd",
            (lambda h=h: (lambda e: e.dma_start(out=RM64S[h % 2], in_=dr_rmax[h:h + 1, :])))(),
            waits=wrb, inc="gd", incn=16)

    def emit_statq_stt(h):
        statq_dv[h] = S.do("vector", (lambda h=h: (lambda e: e.scalar_tensor_tensor(
            q_ex[64:65, h * T:(h + 1) * T], q_ex[64:65, h * T:(h + 1) * T], 2.0, RM64S[h % 2],
            ALU.mult, ALU.subtract)))(),
            waits=[("gd", statq_dma[h]), ("pe", P1_PE)], inc="dv", drain=True)

    for hh0 in (0, 1):
        emit_statq_dma(hh0)
        emit_statq_stt(hh0)
    U = H * nB

    def stage_A(u, mcs):
        h, tb = u // nB, u % nB
        if nB < 3 and 0 in mcs and tb == 0 and 1 <= h < H - 1:
            emit_statq_dma(h + 1)
            emit_statq_stt(h + 1)
        qslot = u % 2
        for i, (m0, m1) in ((j, MCS[j]) for j in mcs):
            rows = m1 - m0
            uu = u * 3 + i
            bank = uu % 2
            wmm = [("dv", statq_dv[h])]
            if uu >= 2:
                wmm.append(("ac", qexp[uu - 2]))
                wmm.append(("dv", qexp_dv[uu - 2]))
            S.do("tensor", (lambda h=h, tb=tb, m0=m0, m1=m1, bank=bank: (lambda e: e.matmul(
                pb[bank][0:m1 - m0, :], projex[:, m0:m1],
                q_ex[:, h * T + tb * 512:h * T + (tb + 1) * 512], start=True, stop=True)))(),
                waits=wmm, inc="pe")
            pem = S.c("pe")
            qexp[uu] = S.do("scalar", (lambda rows=rows, bank=bank: (lambda e: e.activation(
                SQF[bank][0:rows, :], pb[bank][0:rows, :], AF.Exp, bias=0.0, scale=1.0)))(),
                waits=[("pe", pem)], inc="ac", drain=True)
            qexp_dv[uu] = S.do("vector", (lambda i=i, rows=rows, bank=bank, qslot=qslot: (lambda e: e.tensor_scalar_add(
                QP.rearrange("p (s x) -> p s x", s=2)[0:rows, qslot, i * 512:(i + 1) * 512],
                SQF[bank][0:rows, :], EPS)))(),
                waits=[("ac", S.c("ac"))], inc="dv", drain=True)

    st5pe = {}

    def stage_B(u):
        h, tb = u // nB, u % nB
        qslot = u % 2
        sbank = 4 + u % 4
        for i, (m0, m1) in enumerate(MCS):
            rows = m1 - m0
            wmm = [("dv", qexp_dv[u * 3 + i]), ("dv", CAUG_DV)]
            if i == 0:
                if u - 4 in st5ev:
                    wmm.append(("dv", st5ev[u - 4]))
                    wmm.append(("ac", st5ac[u - 4]))
                if u < 4:
                    wmm.append(("dv", S.c("dv")))
            S.do("tensor", (lambda h=h, i=i, rows=rows, qslot=qslot, sbank=sbank: (lambda e: e.matmul(
                pb[sbank][0:65, :], CAUG[0:rows, h * 195 + i * 65:h * 195 + (i + 1) * 65],
                QP.rearrange("p (s x) -> p s x", s=2)[0:rows, qslot, i * 512:(i + 1) * 512],
                start=(i == 0), stop=(i == 2))))(),
                waits=wmm, inc="pe" if i == 2 else None)
        st5pe[u] = S.c("pe")
        st5ac[u] = S.do("scalar", (lambda u2=u % 2, sbank=sbank: (lambda e: e.activation(
            LNR[u2], pb[sbank][64:65, :], AF.Ln, bias=0.0, scale=1.0)))(),
            waits=[("pe", st5pe[u])], inc="ac", drain=True)
        S.do("scalar", (lambda u2=u % 2: (lambda e: e.activation(
            DINVB[u2], LNR[u2], AF.Exp, bias=0.0, scale=-1.0)))(),
            waits=[], inc="ac", drain=True)

    def stage_C(u):
        h, tb = u // nB, u % nB
        if nB >= 3:
            if tb == 0 and h + 2 < H:
                emit_statq_dma(h + 2)
            if tb == 2 and h + 2 < H:
                emit_statq_stt(h + 2)
        sbank = 4 + u % 4
        bbank = 2 + u % 2
        S.do("tensor", (lambda u2=u % 2, bbank=bbank: (lambda e: e.matmul(
            pb[bbank][0:64, :], ones_b[64:65, 0:64], DINVB[u2], start=True, stop=True)))(),
            waits=[("ac", st5ac[u] + 1), ("dv", nsbdv.get(u - 2, 0))], inc="pe")
        S.do("vector", (lambda u4=u % 4, bbank=bbank: (lambda e: e.tensor_copy(NSB[u4], pb[bbank][0:64, :])))(),
             waits=[("pe", S.c("pe"))], inc="dv", drain=True)
        ACnsb = S.c("dv")
        nsbdv[u] = ACnsb
        if h % 2 == 0:
            st5ev[u] = S.do("vector", (lambda h=h, tb=tb, sbank=sbank, u4=u % 4: (lambda e: e.tensor_mul(
                B3[0:64, h // 2, tb * 512:(tb + 1) * 512], pb[sbank][0:64, :], NSB[u4])))(),
                waits=[("dv", ACnsb)], inc="dv")
        else:
            st5ev[u] = S.do("vector", (lambda tb=tb, sbank=sbank, u4=u % 4: (lambda e: e.tensor_mul(
                NUMSC[tb % 2], pb[sbank][0:64, :], NSB[u4])))(),
                waits=[("dv", ACnsb)] + ([("gd", numsc_dv[u - 2])] if (u - 2) in numsc_dv else []),
                inc="dv")
            numsc_dv[u] = S.do("gpsimd", (lambda h=h, tb=tb: (lambda e: e.dma_start(
                out=B3[64:128, h // 2, tb * 512:(tb + 1) * 512], in_=NUMSC[tb % 2])))(),
                waits=[("dv", st5ev[u])], inc="gd", incn=16)

    RING_PE = {}
    for v in range(U + 2):
        if v < U:
            stage_A(v, (0, 1))
        if 1 <= v <= U:
            stage_B(v - 1)
            if v - 1 in (15, 31, 47):
                RING_PE[v - 1] = S.c("pe")
        if v < U:
            stage_A(v, (2,))
        if v >= 2:
            stage_C(v - 2)
    ATTN_DV = S.c("dv")
    ATTN_GD = S.c("gd")
    ST5_PE = S.c("pe")
    AC_ATTN = S.c("ac")

    wgp = {}
    S.do("sync", lambda e: e.dma_start(out=BOSB, in_=b_o[None, :]),
         waits=[("pe", P1_PE)], inc="xd", incn=16)
    XD_BO = S.c("xd")
    for half in range(2):
        S.do("tensor", (lambda half=half: (lambda e: e.matmul(
            pb[2 + half][0:128, :], onesf[:], BOSB[:, half * 512:(half + 1) * 512], start=True, stop=True)))(),
            waits=[("xd", XD_BO), ("dv", ATTN_DV), ("ac", AC_ATTN)], inc="pe")
        S.do("scalar", (lambda half=half: (lambda e: e.copy(
            BOBC[:, half * 512:(half + 1) * 512], pb[2 + half][0:128, :])))(),
            waits=[("pe", S.c("pe")), ("gd", ATTN_GD)], inc="ac")
    BOBC_AC = S.c("ac")

    gp_wo = load_weight(w_o, attn_ring[0], ST5_PE, cast="scalar", pieces=(0, 1, 2))
    wgp[("w1", 0)] = load_weight(w_1[0], ring_c(0), ST5_PE)
    wgp[("w2", 0)] = load_weight(w_2[0], ring_c(1), ST5_PE)

    woev = {}
    xfm2_dv = {}
    ln_dv = {}

    def emit_xfm2(tau):
        bank = pbb[6 + tau % 2]
        wtp = [("dv", ln_dv[tau])]
        if tau >= 2:
            wtp.append(("dv", xfm2_dv[tau - 2]))
        for c in range(NCH):
            S.do("tensor", (lambda tau=tau, c=c, bank=bank: (lambda e: e.transpose(
                bank[:, c * 128:(c + 1) * 128],
                x_tm[:, tau * 1024 + c * 128:tau * 1024 + (c + 1) * 128], ident_b[:])))(),
                waits=wtp if c == 0 else [], inc="pe" if c == NCH - 1 else None)
        xfm2_dv[tau] = S.do("vector", (lambda tau=tau, bank=bank: (lambda e: e.tensor_copy(
            B3[:, :, tau * 128:(tau + 1) * 128],
            bank[:, 0:1024].rearrange("p (c n) -> p c n", c=NCH))))(),
            waits=[("pe", S.c("pe"))], inc="dv")

    for tau in range(nT):
        zslot = tau % 2
        wz = [("ac", BOBC_AC)]
        if tau >= 2:
            wz.append(("dv", ln_dv[tau - 2]))
        S.do("vector", (lambda tau=tau, zslot=zslot: (lambda e: e.tensor_add(
            ZF[zslot], x_tm[:, tau * 1024:(tau + 1) * 1024], BOBC)))(),
            waits=wz, inc="dv", drain=True)
        ZXBO = S.c("dv")
        for half in range(2):
            u = tau * 2 + half
            bank = u % 2
            wmm = [gp_wo, ("dv", ATTN_DV), ("gd", ATTN_GD)]
            if u >= 2:
                wmm.append(("dv", woev[u - 2]))
            for kc in range(NCH):
                S.do("tensor", (lambda tau=tau, half=half, kc=kc, bank=bank: (lambda e: e.matmul(
                    pb[bank][:, :], B_fm[:, kc * T + tau * 128:kc * T + (tau + 1) * 128],
                    attn_ring[0][:, kc * 1024 + half * 512:kc * 1024 + (half + 1) * 512],
                    start=(kc == 0), stop=(kc == NCH - 1))))(),
                    waits=wmm if kc == 0 else [], inc="pe" if kc == NCH - 1 else None)
            woev[u] = S.do("vector", (lambda half=half, bank=bank, zslot=zslot: (lambda e: e.tensor_add(
                ZF[zslot][:, half * 512:(half + 1) * 512], ZF[zslot][:, half * 512:(half + 1) * 512],
                pb[bank][:, :])))(),
                waits=[("pe", S.c("pe")), ("dv", ZXBO)], inc="dv", drain=True)
        wln = [("dv", S.c("dv"))]
        if tau >= 1:
            wln.append(("ac", ln_dv[(tau - 1, "sq")]))
        S.do("vector", (lambda zslot=zslot: (lambda e: e.bn_stats(bn6[:, 0:6], ZF[zslot][:, 0:512])))(),
             waits=wln, inc=None, drain=True)
        S.do("vector", (lambda zslot=zslot: (lambda e: e.bn_stats(bn6[:, 6:12], ZF[zslot][:, 512:1024])))(),
             waits=[], inc=None, drain=True)
        S.do("vector", lambda e: e.bn_aggr(agg2[:], bn6[:]), waits=[], inc="dv", drain=True)
        ln_dv[(tau, "sq")] = S.do("scalar", lambda e: e.activation(
            rstd[:], agg2[:, 1:2], AF.Sqrt, bias=lneps[:], scale=1.0),
            waits=[("dv", S.c("dv"))], inc="ac")
        S.do("vector", lambda e: e.reciprocal(rstd[:], rstd[:]),
             waits=[("ac", ln_dv[(tau, "sq")])], inc=None, drain=True)
        S.do("vector", lambda e: e.tensor_scalar(nbv[:], agg2[:, 0:1], rstd[:, 0:1], -1.0, ALU.mult, ALU.mult),
             waits=[], inc=None, drain=True)
        ln_dv[tau] = S.do("vector", (lambda tau=tau, zslot=zslot: (lambda e: e.tensor_scalar(
            x_tm[:, tau * 1024:(tau + 1) * 1024], ZF[zslot], rstd[:, 0:1], nbv[:, 0:1], ALU.mult, ALU.add)))(),
            waits=[], inc="dv", drain=True)
        if tau >= 2:
            emit_xfm2(tau - 2)
    emit_xfm2(nT - 2)
    emit_xfm2(nT - 1)
    WO_PE = S.c("pe")
    LN1_DV = S.c("dv")

    S.do("sync", lambda e: e.dma_start(out=WGSTG.rearrange("p (c e) -> p c e", c=NCH),
                                       in_=w_g.rearrange("(c p) e -> p c e", p=128)),
         waits=[("dv", ATTN_DV)], inc="xd", incn=16)
    S.do("gpsimd", lambda e: e.tensor_copy(wgb[:], WGSTG), waits=[("xd", S.c("xd"))], inc="gp", drain=True)
    S.do("sync", lambda e: e.dma_start(out=BGSTG, in_=b_g[None, :]), waits=[], inc="xd", incn=16)
    S.do("gpsimd", lambda e: e.tensor_copy(bgb[:], BGSTG), waits=[("xd", S.c("xd"))], inc="gp", drain=True)
    GP_GATE = S.c("gp")
    S.do("sync", lambda e: e.dma_start(out=BSTG, in_=b_1), waits=[("gp", S.c("gp"))], inc="xd", incn=16)
    XD_B1 = S.c("xd")
    for c in range(NCH):
        S.do("tensor", (lambda c=c: (lambda e: e.transpose(
            pb[2][0:128, c * 8:(c + 1) * 8], BSTG[:, c * 128:(c + 1) * 128], ident_f[0:8, 0:8])))(),
            waits=[("xd", XD_B1), ("ac", BOBC_AC)] if c == 0 else [], inc="pe" if c == NCH - 1 else None)
    S.do("scalar", lambda e: e.copy(b1T[:], pb[2][0:128, 0:64]), waits=[("pe", S.c("pe"))], inc="ac")
    B1T_AC = S.c("ac")
    S.do("sync", lambda e: e.dma_start(out=BSTG, in_=b_2), waits=[("ac", B1T_AC)], inc="xd", incn=16)
    S.do("gpsimd", lambda e: e.tensor_copy(B2B, BSTG), waits=[("xd", S.c("xd")), ("pe", S.c("pe"))],
         inc="gp", drain=True)
    GP_MISC = S.c("gp")

    ohT_dv = {}
    for tau in range(nT):
        gbank = 2 + tau % 2
        wmm = [("dv", xfm2_dv[tau]), ("gp", GP_GATE), ("ac", B1T_AC)]
        if tau >= 2:
            wmm.append(("dv", ohT_dv[tau - 2]))
        for kc in range(NCH):
            S.do("tensor", (lambda tau=tau, kc=kc, gbank=gbank: (lambda e: e.matmul(
                pb[gbank][0:128, 0:8], B_fm[:, kc * T + tau * 128:kc * T + (tau + 1) * 128],
                wgb[:, kc * 8:(kc + 1) * 8], start=(kc == 0), stop=False)))(),
                waits=wmm if kc == 0 else [], inc=None)
        S.do("tensor", (lambda gbank=gbank: (lambda e: e.matmul(
            pb[gbank][0:128, 0:8], ones_b[0:1, 0:128], bgb[:], start=False, stop=True)))(),
            waits=[], inc="pe")
        peg = S.c("pe")
        S.do("vector", (lambda gbank=gbank: (lambda e: e.reduce_max(ohmax[:], pb[gbank][0:128, 0:8], axis=AX)))(),
             waits=[("pe", peg)], inc=None, drain=True)
        S.do("vector", (lambda gbank=gbank: (lambda e: e.tensor_scalar(
            oh_f[:], pb[gbank][0:128, 0:8], ohmax[:, 0:1], None, ALU.is_equal)))(),
            waits=[], inc="dv", drain=True)
        S.do("tensor", lambda e: e.transpose(pb[6][0:8, 256:384], oh_f[:], ident_f[:]),
             waits=[("dv", S.c("dv"))], inc="pe")
        ohT_dv[tau] = S.do("vector", (lambda tau=tau: (lambda e: e.tensor_copy(
            ohT[:, tau * 128:(tau + 1) * 128], pb[6][0:8, 256:384])))(),
            waits=[("pe", S.c("pe"))], inc="dv")
    OHT_DV = S.c("dv")

    ring_last = {0: RING_PE.get(15, ST5_PE), 1: RING_PE.get(31, ST5_PE), 2: RING_PE.get(47, ST5_PE)}
    hm_free = {0: 0, 1: 0}
    selfree = {0: OHT_DV, 1: OHT_DV}
    ybfree = {0: 0, 1: 0}
    pre_free = {0: 0, 1: 0}
    gelu_prev = {}
    lastmask = {}

    for ee in range(E):
        s1 = (2 * ee) % 3
        s2 = (2 * ee + 1) % 3

        def w1_block(tb, ee=ee, s1=s1):
            selb = (ee * nB + tb) % 2
            sb = 2 + selb
            selpe = S.do("tensor", (lambda ee=ee, tb=tb, sb=sb: (lambda e: e.matmul(
                pb[sb][0:128, :], sel8[:, ee * 128:(ee + 1) * 128],
                ohT[:, tb * 512:(tb + 1) * 512], start=True, stop=True)))(),
                waits=[("dv", selfree[selb]), wgp[("w1", ee)], ("gp", GP_MISC),
                       ("pe", hm_free[tb % 2])],
                inc="pe")
            for ic in range(NCH):
                u = ic % 2
                bank = u
                wmm = [("ac", pre_free[u])] if pre_free[u] else []
                for kc in range(NCH):
                    S.do("tensor", (lambda tb=tb, ic=ic, kc=kc, bank=bank, s1=s1: (lambda e: e.matmul(
                        pb[bank][:, :], ring_c(s1)[:, kc * 1024 + ic * 128:kc * 1024 + (ic + 1) * 128],
                        B_fm[:, kc * T + tb * 512:kc * T + (tb + 1) * 512],
                        start=(kc == 0), stop=(kc == NCH - 1))))(),
                        waits=wmm if kc == 0 else [], inc="pe" if kc == NCH - 1 else None)
                pew1 = S.c("pe")
                wg = [("pe", pew1), ("ac", B1T_AC)]
                if (ee, tb, ic - 1) in gelu_prev:
                    wg.append(("dv", gelu_prev[(ee, tb, ic - 1)]))
                pre_free[u] = S.do("scalar", (lambda ee=ee, ic=ic, bank=bank: (lambda e: e.activation(
                    hg_b[:], pb[bank][:, :], AF.Gelu, bias=b1T[:, ic * 8 + ee:ic * 8 + ee + 1], scale=1.0)))(),
                    waits=wg, inc="ac")
                gelu_prev[(ee, tb, ic)] = S.do("vector",
                    (lambda tb=tb, ic=ic, sb=sb: (lambda e: e.tensor_mul(
                        HM[tb % 2][:, ic * 512:(ic + 1) * 512], hg_b[:], pb[sb][0:128, :])))(),
                    waits=[("ac", pre_free[u]), ("pe", selpe)], inc="dv")
            lastmask[(ee, tb)] = gelu_prev[(ee, tb, NCH - 1)]
            selfree[selb] = lastmask[(ee, tb)]
            if tb == nB - 1:
                ring_last[s1] = S.c("pe")

        def w2_block(tb, ee=ee, s2=s2):
            for tsub in range(4):
                tau = tb * 4 + tsub
                for half in range(2):
                    u = (tsub * 2 + half) % 2
                    ybank = 4 + u
                    wmm = [wgp[("w2", ee)], ("dv", lastmask[(ee, tb)]), ("dv", OHT_DV)]
                    if ybfree[u]:
                        wmm.append(("dv", ybfree[u]))
                    first = True
                    if ee == 0:
                        S.do("tensor", (lambda tau=tau, half=half, ybank=ybank: (lambda e: e.matmul(
                            pb[ybank][:, :], ohT[:, tau * 128:(tau + 1) * 128],
                            B2B[:, half * 512:(half + 1) * 512], start=True, stop=False)))(),
                            waits=wmm, inc=None)
                        first = False
                    for ic in range(NCH):
                        S.do("tensor", (lambda tb=tb, tsub=tsub, half=half, ic=ic, ybank=ybank, s2=s2:
                             (lambda e: e.matmul(
                                 pb[ybank][:, :],
                                 HM[tb % 2][:, ic * 512 + tsub * 128:ic * 512 + (tsub + 1) * 128],
                                 ring_c(s2)[:, ic * 1024 + half * 512:ic * 1024 + (half + 1) * 512],
                                 start=(first and ic == 0), stop=(ic == NCH - 1))))(),
                            waits=wmm if (first and ic == 0) else [], inc="pe" if ic == NCH - 1 else None)
                    if ee == 0:
                        ybfree[u] = S.do("vector", (lambda tau=tau, half=half, ybank=ybank: (lambda e: e.tensor_copy(
                            y_acc[:, tau * 1024 + half * 512:tau * 1024 + (half + 1) * 512],
                            pb[ybank][:, :])))(),
                            waits=[("pe", S.c("pe")), ("pe", WO_PE)], inc="dv")
                    else:
                        ybfree[u] = S.do("vector", (lambda tau=tau, half=half, ybank=ybank: (lambda e: e.tensor_add(
                            y_acc[:, tau * 1024 + half * 512:tau * 1024 + (half + 1) * 512],
                            y_acc[:, tau * 1024 + half * 512:tau * 1024 + (half + 1) * 512],
                            pb[ybank][:, :])))(),
                            waits=[("pe", S.c("pe"))], inc="dv")
            hm_free[tb % 2] = S.c("pe")
            if tb == nB - 1:
                ring_last[s2] = S.c("pe")

        if nB == 1:
            w1_block(0)
            if ee < E - 1:
                wgp[("w1", ee + 1)] = load_weight(w_1[ee + 1], ring_c((2 * ee + 2) % 3), ring_last[(2 * ee + 2) % 3])
                wgp[("w2", ee + 1)] = load_weight(w_2[ee + 1], ring_c((2 * ee + 3) % 3), ring_last[(2 * ee + 3) % 3])
            w2_block(0)
        else:
            w1_block(0)
            if ee < E - 1:
                wgp[("w1", ee + 1)] = load_weight(w_1[ee + 1], ring_c((2 * ee + 2) % 3), ring_last[(2 * ee + 2) % 3])
            w1_block(1)
            w2_block(0)
            for tb in range(2, nB):
                w1_block(tb)
                if tb == nB - 1 and ee < E - 1:
                    wgp[("w2", ee + 1)] = load_weight(w_2[ee + 1], ring_c((2 * ee + 3) % 3), ring_last[(2 * ee + 3) % 3])
                w2_block(tb - 1)
            w2_block(nB - 1)
    FFN_DV = S.c("dv")

    odc = 0
    for tau in range(nT):
        zslot = tau % 2
        wz = [("dv", FFN_DV)]
        if tau >= 2:
            wz.append(("od", 16 * (tau - 1)))
        S.do("vector", (lambda tau=tau, zslot=zslot: (lambda e: e.tensor_add(
            OUTS[zslot], x_tm[:, tau * 1024:(tau + 1) * 1024],
            y_acc[:, tau * 1024:(tau + 1) * 1024])))(),
            waits=wz, inc="dv", drain=True)
        wln = []
        if tau >= 1:
            wln.append(("ac", ln_dv[("ln2sq", tau - 1)]))
        S.do("vector", (lambda zslot=zslot: (lambda e: e.bn_stats(bn6[:, 0:6], OUTS[zslot][:, 0:512])))(),
             waits=wln, inc=None, drain=True)
        S.do("vector", (lambda zslot=zslot: (lambda e: e.bn_stats(bn6[:, 6:12], OUTS[zslot][:, 512:1024])))(),
             waits=[], inc=None, drain=True)
        S.do("vector", lambda e: e.bn_aggr(agg2[:], bn6[:]), waits=[], inc="dv", drain=True)
        ln_dv[("ln2sq", tau)] = S.do("scalar", lambda e: e.activation(
            rstd[:], agg2[:, 1:2], AF.Sqrt, bias=lneps[:], scale=1.0),
            waits=[("dv", S.c("dv"))], inc="ac")
        S.do("vector", lambda e: e.reciprocal(rstd[:], rstd[:]),
             waits=[("ac", ln_dv[("ln2sq", tau)])], inc=None, drain=True)
        S.do("vector", lambda e: e.tensor_scalar(nbv[:], agg2[:, 0:1], rstd[:, 0:1], -1.0, ALU.mult, ALU.mult),
             waits=[], inc=None, drain=True)
        S.do("vector", (lambda zslot=zslot: (lambda e: e.tensor_scalar(
            OUTS[zslot], OUTS[zslot], rstd[:, 0:1], nbv[:, 0:1], ALU.mult, ALU.add)))(),
            waits=[], inc="dv", drain=True)
        S.do("sync", (lambda tau=tau, zslot=zslot: (lambda e: e.dma_start(
            out=out[tau * 128:(tau + 1) * 128, :], in_=OUTS[zslot])))(),
            waits=[("dv", S.c("dv"))], inc="od", incn=16)
        odc = S.c("od")
    S.do("sync", lambda e: e.wait_ge(sems["od"], odc), waits=[])

    with nc.Block() as blk:
        for engname in ENGS:
            steps = S.q[engname]
            def body(eng, steps=steps):
                for fn, waits, inc, incn, drain in steps:
                    for s, v in waits:
                        eng.wait_ge(sems[s], v)
                    if drain:
                        eng.drain()
                    ins = fn(eng)
                    if inc is not None:
                        assert ins is not None
                        ins.then_inc(sems[inc], incn)
            getattr(blk, engname)(body)

    st.close()
    return nc


MODE = "fused"


def kernel(L0_attn_x, L0_attn_Wq, L0_attn_Wk, L0_attn_Wv, L0_attn_Wo, L0_attn_bo,
           L0_attn_proj, L0_ffn_Wg, L0_ffn_bg, L0_ffn_W1, L0_ffn_b1, L0_ffn_W2, L0_ffn_b2):
    T = 2048
    groups = [[0, 1, 2, 3], [4, 5, 6, 7]]
    f = lambda a: np.ascontiguousarray(np.asarray(a, dtype=np.float32))
    x = f(L0_attn_x)
    shared = {
        "w_q": f(L0_attn_Wq), "w_k": f(L0_attn_Wk), "w_v": f(L0_attn_Wv),
        "w_o": f(L0_attn_Wo), "b_o": f(L0_attn_bo), "prj": f(L0_attn_proj),
        "w_g": f(L0_ffn_Wg), "b_g": f(L0_ffn_bg), "w_1": f(L0_ffn_W1),
        "b_1": f(L0_ffn_b1), "w_2": f(L0_ffn_W2), "b_2": f(L0_ffn_b2),
    }
    in_maps = []
    for core in range(8):
        b, t0 = core // 4, (core % 4) * T
        m = dict(shared)
        m["x_in"] = np.ascontiguousarray(x[b, t0:t0 + T, :])
        in_maps.append(m)
    if MODE == "fused":
        nc = build(T, groups, part="fused")
        for core in range(8):
            g0, i = (core // 4) * 4, core % 4
            far = g0 + (i + 2) % 4
            in_maps[core]["x_far"] = np.ascontiguousarray(x[far // 4, (far % 4) * T:(far % 4) * T + T, :])
            in_maps[core]["slot_in"] = np.full((1, 16), core % 2, np.int32)
        res = run_bass_kernel_spmd(nc, in_maps, core_ids=list(range(8)))
        outs = [res.results[c]["out"] for c in range(8)]
    elif MODE == "full":
        nc = build(T, groups, part="full")
        res = run_bass_kernel_spmd(nc, in_maps, core_ids=list(range(8)))
        outs = [res.results[c]["out"] for c in range(8)]
    else:
        nca = build(T, groups, part="a")
        ra = run_bass_kernel_spmd(nca, in_maps, core_ids=list(range(8)))
        c_parts = [ra.results[c]["c_out"] for c in range(8)]
        s_parts = [ra.results[c]["s_out"] for c in range(8)]
        ncb = build(T, groups, part="b")
        in_maps_b = []
        for core in range(8):
            g0 = (core // 4) * 4
            m = dict(in_maps[core])
            m["c_in"] = np.ascontiguousarray(np.stack(c_parts[g0:g0 + 4], axis=0))
            m["s_in"] = np.ascontiguousarray(
                np.concatenate([s_parts[g0 + i].reshape(-1) for i in range(4)]).reshape(1, 64))
            in_maps_b.append(m)
        rb = run_bass_kernel_spmd(ncb, in_maps_b, core_ids=list(range(8)))
        outs = [rb.results[c]["out"] for c in range(8)]
    return np.stack([np.concatenate(outs[0:4], axis=0),
                     np.concatenate(outs[4:8], axis=0)], axis=0)
